# version 11
# baseline (speedup 1.0000x reference)
import numpy as np
import ml_dtypes
from collections import defaultdict
from contextlib import ExitStack
import concourse.bass as bass
import concourse.mybir as mybir
from concourse.bass_utils import run_bass_kernel_spmd

F32 = mybir.dt.float32
BF16 = mybir.dt.bfloat16
I32 = mybir.dt.int32
AF = mybir.ActivationFunctionType
ALU = mybir.AluOpType
AX = mybir.AxisListType.X

NCORE = 8
B = 2
D = 1024
H = 16
HP = 8
MLOC = 1024
TLOC = 2048
NT = 16
NKT = 64
NP = 2048
EPC = 8
SHIFT = 600.0
CFFN = 4.0
EPS = 1e-5
MAGIC = 0x5F3759DF

ENGS = ("sync", "gpsimd", "vector", "scalar", "tensor")


class Sched:

    def __init__(self):
        self.q = {e: [] for e in ENGS}
        self.n = defaultdict(int)

    def do(self, eng, fn):
        self.q[eng].append(fn)

    def inc(self, sem, k=1):
        self.n[sem] += k
        return self.n[sem]

    def flush(self, nc):
        with nc.Block() as blk:
            for name in ENGS:
                fns = self.q[name]

                def runner(e, fns=fns):
                    for f in fns:
                        f(e)
                getattr(blk, name)(runner)
        self.q = {e: [] for e in ENGS}


def newton_rsqrt(v, vec_ap, y_ap, t1_ap, ti1, ti2, iters=4):
    def f(e):
        e.drain()
        e.tensor_scalar(out=ti1, in0=vec_ap.bitcast(I32), scalar1=1, scalar2=None,
                        op0=ALU.arith_shift_right)
        e.drain()
        e.tensor_scalar(out=ti2, in0=ti1, scalar1=-1, scalar2=int(MAGIC),
                        op0=ALU.mult, op1=ALU.add)
        e.drain()
        e.tensor_copy(out=y_ap.bitcast(I32), in_=ti2)
        e.drain()
        for _ in range(iters):
            e.tensor_tensor(out=t1_ap, in0=y_ap, in1=y_ap, op=ALU.mult)
            e.drain()
            e.tensor_tensor(out=t1_ap, in0=t1_ap, in1=vec_ap, op=ALU.mult)
            e.drain()
            e.tensor_scalar(out=t1_ap, in0=t1_ap, scalar1=-0.5, scalar2=1.5,
                            op0=ALU.mult, op1=ALU.add)
            e.drain()
            e.tensor_tensor(out=y_ap, in0=y_ap, in1=t1_ap, op=ALU.mult)
            e.drain()
    v.do("vector", f)


def build_nc(debug=False, phases=99, stage=99, lite=0):
    nc = bass.Bass("TRN2", target_bir_lowering=False)

    xs = nc.dram_tensor("xs", [TLOC, D], F32, kind="ExternalInput").ap()
    qkv_wT = nc.dram_tensor("qkv_wT", [D, 3 * D], F32, kind="ExternalInput").ap()
    proj_wT_d = nc.dram_tensor("proj_wT", [D, D], F32, kind="ExternalInput").ap()
    phi_d = nc.dram_tensor("phi2", [D, NP], F32, kind="ExternalInput").ap()
    wsl = nc.dram_tensor("wsl", [EPC, D, D], F32, kind="ExternalInput").ap()
    bsl = nc.dram_tensor("bsl", [1, EPC * D], F32, kind="ExternalInput").ap()
    qkvb = nc.dram_tensor("qkvb", [1, 3 * D], F32, kind="ExternalInput").ap()
    projb_d = nc.dram_tensor("projb", [1, D], F32, kind="ExternalInput").ap()
    ident_in = nc.dram_tensor("ident", [128, 128], BF16, kind="ExternalInput").ap()
    yout = nc.dram_tensor("yout", [TLOC, D], F32, kind="ExternalOutput").ap()

    kvkin = nc.dram_tensor("kvkin", [H * 65, TLOC], BF16).ap()
    kvkout = nc.dram_tensor("kvkout", [NCORE * H * 65, TLOC], BF16).ap()
    kvvin = nc.dram_tensor("kvvin", [TLOC, D], BF16).ap()
    kvvout = nc.dram_tensor("kvvout", [NCORE * TLOC, D], BF16).ap()
    q_d = nc.dram_tensor("q_d", [D, TLOC], BF16).ap()
    m8_d = nc.dram_tensor("m8_d", [H, TLOC], BF16).ap()
    kmax_in = nc.dram_tensor("kmax_in", [2, H], F32).ap()
    kmax_out = nc.dram_tensor("kmax_out", [2, H], F32).ap()
    attn_d = nc.dram_tensor("attn_d", [B, D, MLOC], BF16).ap()
    wbf = nc.dram_tensor("wbf", [EPC, D, D], BF16).ap()
    slots_in = nc.dram_tensor("slots_in", [NP * B, D], F32).ap()
    slots_out = nc.dram_tensor("slots_out", [NP * B // NCORE, D], F32).ap()
    y_in = nc.dram_tensor("y_in", [EPC * B * 32, D], BF16).ap()
    y_ag = nc.dram_tensor("y_ag", [NP * B, D], BF16).ap()
    bc_d = nc.dram_tensor("bc_d", [1, 1024], F32).ap()

    dbg = {}
    if debug:
        for nm, shp in [("dbg_xln1T", [128, 2048]), ("dbg_k", [128, 2048]),
                        ("dbg_q", [128, 2048]), ("dbg_m8", [2, 2048]),
                        ("dbg_attn", [128, 1024]), ("dbg_xln3", [128, 1024]),
                        ("dbg_disp", [128, 1024]), ("dbg_ct", [128, 1024]),
                        ("dbg_y", [128, 1024])]:
            dbg[nm] = nc.dram_tensor(nm, shp, F32, kind="ExternalOutput").ap()

    rg = [list(range(NCORE))]
    es = ExitStack()
    E = es.enter_context

    sems = {}
    for s in ["ph", "d_x", "d_w", "d_m", "d_i", "d_kmx", "d_kmg", "d_kst", "d_vst", "d_qst", "d_qs1", "d_kld", "d_kld1",
              "d_vld", "d_vld1", "d_qld", "d_qld1", "d_at", "d_sl", "d_y", "d_yb", "d_out", "d_dbg",
              "d_bc", "d_g", "d_g2", "d_pw", "d_ph2", "d_p0", "d_p1", "d_w0", "d_w1", "cc", "c_pe", "c_dv", "c_ac", "c_sc", "c_ex", "c_av", "c_nr",
              "c_rc", "c_bc", "c_mul"]:
        sems[s] = E(nc.semaphore(s))

    v = Sched()

    def W(eng, sem, val):
        v.do(eng, lambda e, s=sems[sem], x=val: e.wait_ge(s, x))

    def dma(eng, sem, out, in_):
        def f(e, o=out, i=in_, s=sems[sem]):
            e.dma_start(out=o, in_=i).then_inc(s, 16)
        v.do(eng, f)
        return v.inc(sem, 16)


    def emit_rstd_sqrt(n, c0=0):
        def alg(e):
            sl = slice(c0, c0 + n)
            e.drain()
            e.tensor_scalar(out=mean_t[:, sl], in0=stat[:, sl], scalar1=1.0 / D,
                            scalar2=None, op0=ALU.mult)
            e.drain()
            e.tensor_tensor(out=var_t[:, sl], in0=mean_t[:, sl], in1=mean_t[:, sl], op=ALU.mult)
            e.drain()
            e.tensor_scalar(out=nsc1[:, sl], in0=stat2[:, sl], scalar1=1.0 / D,
                            scalar2=None, op0=ALU.mult)
            e.drain()
            e.tensor_tensor(out=var_t[:, sl], in0=nsc1[:, sl], in1=var_t[:, sl], op=ALU.subtract)
            e.drain()
            e.tensor_scalar(out=var_t[:, sl], in0=var_t[:, sl], scalar1=EPS,
                            scalar2=None, op0=ALU.add).then_inc(sems["c_dv"], 1)
        v.do("vector", alg)
        cd_v = v.inc("c_dv")
        W("scalar", "c_dv", cd_v)
        W("scalar", "c_ac", v.n["c_ac"])

        def sq_(e):
            e.activation(out=rstd_t[:, c0:c0 + n], in_=var_t[:, c0:c0 + n], func=AF.Sqrt,
                         bias=zero_b[:, 0:1], scale=1.0).then_inc(sems["c_ac"], 1)
        v.do("scalar", sq_)
        ac_s = v.inc("c_ac")
        W("vector", "c_ac", ac_s)

        def rc(e):
            e.reciprocal(out=rstd_t[:, c0:c0 + n], in_=rstd_t[:, c0:c0 + n])
            e.drain()
        v.do("vector", rc)

    def barrier_all(phase_idx):
        for e in ENGS:
            v.do(e, lambda x, s=sems["ph"]: x.sem_inc(s, 1))
        v.inc("ph", 5)
        tgt = v.n["ph"]
        for e in ENGS:
            v.do(e, lambda x, s=sems["ph"], t=tgt: x.wait_ge(s, t))

    Rt = E(nc.sbuf_tensor("Rt", [128, NT, D], F32))
    ident = E(nc.sbuf_tensor("identb", [128, 128], BF16))
    ones_row = E(nc.sbuf_tensor("ones_row", [1, 512], BF16))
    negc = E(nc.sbuf_tensor("negc", [128, 1], F32))
    zero_b = E(nc.sbuf_tensor("zero_b", [128, 1], F32))
    stat = E(nc.sbuf_tensor("stat", [128, 16], F32))
    stat2 = E(nc.sbuf_tensor("stat2", [128, 16], F32))
    mean_t = E(nc.sbuf_tensor("mean_t", [128, 16], F32))
    var_t = E(nc.sbuf_tensor("var_t", [128, 16], F32))
    rstd_t = E(nc.sbuf_tensor("rstd_t", [128, 16], F32))
    nsc1 = E(nc.sbuf_tensor("nsc1", [128, 16], F32))
    nsi1 = E(nc.sbuf_tensor("nsi1", [128, 16], I32))
    nsi2 = E(nc.sbuf_tensor("nsi2", [128, 16], I32))
    sq_scr = E(nc.sbuf_tensor("sq_scr", [128, 2048], BF16))

    p1 = ExitStack()
    qwT = p1.enter_context(nc.sbuf_tensor("qwT", [128, 8, 3 * D], BF16))
    xln1 = p1.enter_context(nc.sbuf_tensor("xln1", [128, 2, D], BF16))
    xln1T = p1.enter_context(nc.sbuf_tensor("xln1T", [128, 8, TLOC], BF16))
    ktmp = p1.enter_context(nc.sbuf_tensor("ktmp", [128, 2, TLOC], BF16))
    qtmp = p1.enter_context(nc.sbuf_tensor("qtmp", [128, 2, TLOC], BF16))
    vtmp = qtmp[:, :, 0:D]
    hind = p1.enter_context(nc.sbuf_tensor("hind", [128, 2], BF16))
    qkb_row = p1.enter_context(nc.sbuf_tensor("qkb_row", [1, 3 * D], BF16))
    kml = p1.enter_context(nc.sbuf_tensor("kml", [2, H], F32))
    kmt = p1.enter_context(nc.sbuf_tensor("kmt", [2, 2], F32))
    kmg = p1.enter_context(nc.sbuf_tensor("kmg", [2, H], F32))
    kng = p1.enter_context(nc.sbuf_tensor("kng", [2, H], F32))
    qn2 = p1.enter_context(nc.sbuf_tensor("qn2", [2, TLOC], F32))
    qnw = p1.enter_context(nc.sbuf_tensor("qnw", [2, MLOC], F32))
    qns = p1.enter_context(nc.sbuf_tensor("qns", [2, MLOC], F32))
    qni1 = p1.enter_context(nc.sbuf_tensor("qni1", [2, MLOC], I32))
    m8s = p1.enter_context(nc.sbuf_tensor("m8s", [2, 2, TLOC], BF16))

    psq = [p1.enter_context(nc.psum_tensor(f"psq{i}", [128, TLOC], F32)) for i in range(2)]
    ptrv = psq[1][:].bitcast(BF16)
    ptr0 = ptrv[:, 0:512]
    ptr1 = ptrv[:, 1024:1536]

    def _cut():
        barrier_all(1)
        v.flush(nc)
        p1.close()
        es.close()
        return nc, dbg

    xsr = xs.rearrange("(t p) d -> p t d", p=128)
    dma("sync", "d_x", Rt[:, 0:8, :], xsr[:, 0:8, :])
    dma("sync", "d_x", Rt[:, 8:16, :], xsr[:, 8:16, :])
    dma("sync", "d_i", ident[:], ident_in[:, :])
    qwr = qkv_wT.rearrange("(dt p) e -> p dt e", p=128)
    for i in range(4):
        dma("gpsimd", "d_w", qwT[:, 2 * i:2 * i + 2, :], qwr[:, 2 * i:2 * i + 2, :])

    def consts(e):
        e.memset(ones_row[:], 1.0).then_inc(sems["c_dv"], 1)
        e.memset(negc[:], -CFFN).then_inc(sems["c_dv"], 1)
        e.memset(zero_b[:], 0.0).then_inc(sems["c_dv"], 1)
        e.memset(hind[:], 0.0)
        e.drain()
        e.memset(hind[0:64, 0:1], 1.0).then_inc(sems["c_dv"], 1)
        e.memset(hind[64:128, 1:2], 1.0).then_inc(sems["c_dv"], 1)
        e.drain()
    v.do("vector", consts)
    CD_CONST = v.inc("c_dv", 5)

    dma("gpsimd", "d_g", qkb_row[:], qkvb[:, :])
    QKB_TGT = v.n["d_g"]

    W("sync", "c_dv", CD_CONST)
    for h in range(H):
        for ch4 in range(4):
            dma("sync", "d_kst", kvkin[h * 65 + 64:h * 65 + 65, 512 * ch4:512 * ch4 + 512],
                ones_row[0:1, :])

    W("vector", "d_x", 32)
    W("scalar", "d_x", 32)
    W("scalar", "c_dv", CD_CONST)
    for t in range(NT):
        v.do("vector", lambda e, t=t: e.tensor_reduce(
            out=stat[:, t:t + 1], in_=Rt[:, t, :], axis=AX, op=ALU.add))
        if t > 0:
            W("scalar", "c_ac", v.n["c_ac"])
        def sq(e, t=t):
            e.activation(out=sq_scr[:, 0:D], in_=Rt[:, t, :], func=AF.Square,
                         bias=zero_b[:, 0:1], scale=1.0,
                         accum_out=stat2[:, t:t + 1]).then_inc(sems["c_ac"], 1)
        v.do("scalar", sq)
        v.inc("c_ac", 1)
    AC_LN1 = v.n["c_ac"]

    def ln_finish(e, n=NT):
        e.drain()
        e.tensor_scalar(out=mean_t[:, 0:n], in0=stat[:, 0:n], scalar1=1.0 / D,
                        scalar2=None, op0=ALU.mult)
        e.drain()
        e.tensor_tensor(out=var_t[:, 0:n], in0=mean_t[:, 0:n], in1=mean_t[:, 0:n], op=ALU.mult)
        e.drain()
        e.tensor_scalar(out=nsc1[:, 0:n], in0=stat2[:, 0:n], scalar1=1.0 / D,
                        scalar2=None, op0=ALU.mult)
        e.drain()
        e.tensor_tensor(out=var_t[:, 0:n], in0=nsc1[:, 0:n], in1=var_t[:, 0:n], op=ALU.subtract)
        e.drain()
        e.tensor_scalar(out=var_t[:, 0:n], in0=var_t[:, 0:n], scalar1=EPS, scalar2=None, op0=ALU.add)
        e.drain()
        e.tensor_scalar(out=nsi1[:, 0:n], in0=var_t[:, 0:n].bitcast(I32), scalar1=1,
                        scalar2=None, op0=ALU.arith_shift_right)
        e.drain()
        e.tensor_scalar(out=nsi2[:, 0:n], in0=nsi1[:, 0:n], scalar1=-1, scalar2=int(MAGIC),
                        op0=ALU.mult, op1=ALU.add)
        e.drain()
        e.tensor_copy(out=rstd_t[:, 0:n].bitcast(I32), in_=nsi2[:, 0:n])
        e.drain()
        for _ in range(4):
            e.tensor_tensor(out=nsc1[:, 0:n], in0=rstd_t[:, 0:n], in1=rstd_t[:, 0:n], op=ALU.mult)
            e.drain()
            e.tensor_tensor(out=nsc1[:, 0:n], in0=nsc1[:, 0:n], in1=var_t[:, 0:n], op=ALU.mult)
            e.drain()
            e.tensor_scalar(out=nsc1[:, 0:n], in0=nsc1[:, 0:n], scalar1=-0.5, scalar2=1.5,
                            op0=ALU.mult, op1=ALU.add)
            e.drain()
            e.tensor_tensor(out=rstd_t[:, 0:n], in0=rstd_t[:, 0:n], in1=nsc1[:, 0:n], op=ALU.mult)
            e.drain()

    if stage < 2:
        return _cut()
    W("vector", "c_ac", AC_LN1)
    emit_rstd_sqrt(NT)
    W("tensor", "d_i", 16)
    ptrs = [ptr0, ptr1]
    tr1_hist = []
    ev1_hist = []
    for t in range(NT):
        if t >= 2:
            W("vector", "c_pe", tr1_hist[2 * (t - 2) + 1])
        v.do("vector", lambda e, t=t: e.tensor_scalar(
            out=xln1[:, t % 2, :], in0=Rt[:, t, :], scalar1=mean_t[:, t:t + 1],
            scalar2=rstd_t[:, t:t + 1], op0=ALU.subtract,
            op1=ALU.mult).then_inc(sems["c_dv"], 1))
        cd_ap = v.inc("c_dv")
        W("tensor", "c_dv", cd_ap)
        for dh in range(2):
            g1 = 2 * t + dh
            if g1 >= 2:
                W("tensor", "c_dv", ev1_hist[g1 - 2])

            def tr(e, t=t, dh=dh, p=ptrs[dh]):
                for j in range(4):
                    dt = 4 * dh + j
                    mm = e.transpose(out=p[:, 128 * j:128 * j + 128],
                                     in_=xln1[:, t % 2, 128 * dt:128 * dt + 128],
                                     identity=ident[:])
                    if j == 3:
                        mm.then_inc(sems["c_pe"], 1)
            v.do("tensor", tr)
            pe_g = v.inc("c_pe")
            tr1_hist.append(pe_g)
            W("vector", "c_pe", pe_g)

            def ev(e, t=t, dh=dh, p=ptrs[dh]):
                e.tensor_copy(
                    out=xln1T[:, 4 * dh:4 * dh + 4, 128 * t:128 * t + 128],
                    in_=p.rearrange("p (dt c) -> p dt c", c=128),
                ).then_inc(sems["c_dv"], 1)
            v.do("vector", ev)
            ev1_hist.append(v.inc("c_dv"))
    CD_TRE = v.n["c_dv"]

    if stage < 3:
        return _cut()
    W("tensor", "c_dv", CD_TRE)
    W("tensor", "d_w", 64)
    W("tensor", "d_g", QKB_TGT)

    def qk_eti(eti_list, tmp, kind):
        kn_hist = []
        ev_hist = []
        for idx, eti in enumerate(eti_list):
            if idx >= 2:
                W("tensor", "c_dv", ev_hist[idx - 2])
            else:
                W("tensor", "c_dv", v.n["c_dv"])

            def mms(e, eti=eti, i=idx):
                for dt in range(8):
                    for c in range(4):
                        e.matmul(out=psq[i % 2][:, 512 * c:512 * c + 512],
                                 lhsT=qwT[:, dt, 128 * eti:128 * eti + 128],
                                 rhs=xln1T[:, dt, 512 * c:512 * c + 512],
                                 start=(dt == 0), stop=False)
                for c in range(4):
                    mm = e.matmul(out=psq[i % 2][:, 512 * c:512 * c + 512],
                                  lhsT=qkb_row[0:1, 128 * eti:128 * eti + 128],
                                  rhs=ones_row[0:1, 0:512],
                                  start=False, stop=True)
                    if c == 3:
                        mm.then_inc(sems["c_pe"], 1)
            v.do("tensor", mms)
            pe_m = v.inc("c_pe")
            W("vector", "c_pe", pe_m)
            if kind == "Q" and idx < 2:
                W("vector", "d_vst", 16 * NT)
            if idx >= 2:
                if kind == "K":
                    W("vector", "d_kst", v.n["d_kst"])
                else:
                    W("vector", qst_hist[idx - 2][0], qst_hist[idx - 2][1])
                W("vector", "c_ac", v.n["c_ac"])
            v.do("vector", lambda e, t=tmp, i=idx: e.tensor_copy(
                out=t[:, i % 2, :], in_=psq[i % 2][:, :]).then_inc(sems["c_dv"], 1))
            cd_e = v.inc("c_dv")
            if lite < 2:
                W("scalar", "c_dv", cd_e)
                W("scalar", "c_ac", v.n["c_ac"])

                def sq(e, t=tmp, i=idx):
                    e.activation(out=sq_scr[:, :], in_=t[:, i % 2, :], func=AF.Square,
                                 bias=zero_b[:, 0:1], scale=1.0).then_inc(sems["c_ac"], 1)
                v.do("scalar", sq)
                ac_m = v.inc("c_ac")
            else:
                ac_m = v.n["c_ac"]
            if lite >= 1:
                ev_hist.append(cd_e)
                yield eti, cd_e
                continue
            W("tensor", "c_ac", ac_m)
            W("tensor", "c_dv", cd_e)
            for c in range(4):
                if c >= 2:
                    W("tensor", "c_dv", kn_hist[-2])

                def knm(e, c=c, i=idx):
                    e.matmul(out=psq[i % 2][0:2, 1024 * (c % 2):1024 * (c % 2) + 512],
                             lhsT=hind[:, :],
                             rhs=sq_scr[:, 512 * c:512 * c + 512],
                             start=True, stop=True).then_inc(sems["c_pe"], 1)
                v.do("tensor", knm)
                pe_k = v.inc("c_pe")
                W("vector", "c_pe", pe_k)
                pkn_src = lambda c_, i=idx: psq[i % 2][0:2, 1024 * (c_ % 2):1024 * (c_ % 2) + 512]
                if kind == "K":
                    def red(e, c=c, eti=eti, pkn_src=pkn_src):
                        e.drain()
                        b = c // 2
                        e.tensor_reduce(out=kmt[:, 0:1],
                                        in_=pkn_src(c), axis=AX, op=ALU.max)
                        e.drain()
                        hp = eti - 8
                        col = 2 * hp + b
                        if c % 2 == 0:
                            e.tensor_copy(out=kml[:, col:col + 1],
                                          in_=kmt[:, 0:1]).then_inc(sems["c_dv"], 1)
                        else:
                            e.tensor_tensor(out=kml[:, col:col + 1], in0=kml[:, col:col + 1],
                                            in1=kmt[:, 0:1], op=ALU.max).then_inc(sems["c_dv"], 1)
                        e.drain()
                    v.do("vector", red)
                    kn_hist.append(v.inc("c_dv"))
                else:
                    def cp(e, c=c, pkn_src=pkn_src):
                        e.tensor_copy(out=qn2[:, 512 * c:512 * c + 512],
                                      in_=pkn_src(c)).then_inc(sems["c_dv"], 1)
                    v.do("vector", cp)
                    kn_hist.append(v.inc("c_dv"))
            ev_hist.append(v.n["c_dv"])
            yield eti, cd_e

    k_ev = []
    for eti, cd_e in qk_eti(range(8, 16), ktmp, "K"):
        k_ev.append((eti, cd_e))
        hp = eti - 8
        W("sync", "c_dv", cd_e)
        dma("sync", "d_kst", kvkin[(2 * hp) * 65:(2 * hp) * 65 + 64, :], ktmp[0:64, (eti - 8) % 2, :])
        dma("sync", "d_kst", kvkin[(2 * hp + 1) * 65:(2 * hp + 1) * 65 + 64, :], ktmp[64:128, (eti - 8) % 2, :])
        if hp >= 1:
            pass
    KST_TOTAL = v.n["d_kst"]
    W("sync", "c_dv", v.n["c_dv"])
    dma("sync", "d_kmx", kmax_in[:, :], kml[:])
    if stage < 4:
        return _cut()
    W("gpsimd", "d_kmx", 16)

    def arm(e):
        e.collective_compute("AllReduce", ALU.max, replica_groups=rg,
                             ins=[kmax_in.opt()], outs=[kmax_out.opt()]).then_inc(sems["cc"])
    v.do("gpsimd", arm)
    v.inc("cc")
    W("gpsimd", "d_kst", KST_TOTAL)

    def agk(e):
        e.collective_compute("AllGather", ALU.bypass, replica_groups=rg,
                             ins=[kvkin.opt()], outs=[kvkout.opt()]).then_inc(sems["cc"])
    v.do("gpsimd", agk)
    v.inc("cc")

    if stage < 5:
        return _cut()
    W("tensor", "c_dv", v.n["c_dv"])
    W("tensor", "c_ac", v.n["c_ac"])
    vev_hist = []
    for t in range(NT):
        if t >= 2:
            W("tensor", "c_dv", vev_hist[t - 2])

        def vmm(e, t=t):
            for dt in range(8):
                for c in range(2):
                    e.matmul(out=psq[t % 2][:, 512 * c:512 * c + 512],
                             lhsT=xln1T[:, dt, 128 * t:128 * t + 128],
                             rhs=qwT[:, dt, 2 * D + 512 * c:2 * D + 512 * c + 512],
                             start=(dt == 0), stop=False)
            for c in range(2):
                mm = e.matmul(out=psq[t % 2][:, 512 * c:512 * c + 512],
                              lhsT=ones_row[0:1, 0:128],
                              rhs=qkb_row[0:1, 2 * D + 512 * c:2 * D + 512 * c + 512],
                              start=False, stop=True)
                if c == 1:
                    mm.then_inc(sems["c_pe"], 1)
        v.do("tensor", vmm)
        pe_v = v.inc("c_pe")
        W("vector", "c_pe", pe_v)
        if t >= 2:
            W("vector", "d_vst", v.n["d_vst"])

        def vev(e, t=t):
            e.tensor_copy(out=vtmp[:, t % 2, :], in_=psq[t % 2][:, 0:1024]).then_inc(sems["c_dv"], 1)
        v.do("vector", vev)
        cd_v = v.inc("c_dv")
        vev_hist.append(cd_v)
        W("sync", "c_dv", cd_v)
        dma("sync", "d_vst", kvvin[128 * t:128 * t + 128, :], vtmp[:, t % 2, :])
    W("gpsimd", "d_vst", 16 * NT)

    def agv(e):
        e.collective_compute("AllGather", ALU.bypass, replica_groups=rg,
                             ins=[kvvin.opt()], outs=[kvvout.opt()]).then_inc(sems["cc"])
    v.do("gpsimd", agv)
    v.inc("cc")
    for eI in range(EPC):
        dma("gpsimd", "d_w", wbf[eI, :, :], wsl[eI, :, :])
    W("gpsimd", "cc", 1)
    dma("gpsimd", "d_kmg", kmg[:], kmax_out[:, :])
    if stage < 6:
        return _cut()
    W("scalar", "d_kmg", 16)
    W("scalar", "c_ac", v.n["c_ac"])

    def kngs(e):
        e.activation(out=kng[:], in_=kmg[:], func=AF.Sqrt,
                     bias=zero_b[0:2, 0:1], scale=1.0).then_inc(sems["c_ac"], 1)
    v.do("scalar", kngs)
    AC_KNG = v.inc("c_ac")
    W("vector", "c_ac", AC_KNG)

    def kngn(e):
        e.drain()
        e.tensor_scalar(out=kng[:], in0=kng[:], scalar1=-1.0, scalar2=None, op0=ALU.mult)
        e.drain()
    v.do("vector", kngn)

    if stage < 7:
        return _cut()
    qst_hist = []
    for eti, cd_e in qk_eti(range(0, 8), qtmp, "Q"):
        hp = eti
        W("scalar", "c_dv", v.n["c_dv"])
        W("scalar", "c_ac", v.n["c_ac"])

        def m8sq(e):
            e.activation(out=qnw[0:2, :], in_=qn2[:, 0:MLOC], func=AF.Sqrt,
                         bias=zero_b[0:2, 0:1], scale=1.0)
            e.activation(out=qns[0:2, :], in_=qn2[:, MLOC:TLOC], func=AF.Sqrt,
                         bias=zero_b[0:2, 0:1], scale=1.0).then_inc(sems["c_ac"], 1)
        v.do("scalar", m8sq)
        ac_m8 = v.inc("c_ac")
        W("vector", "c_ac", ac_m8)

        def m8chain(e, hp=hp):
            e.drain()
            e.tensor_scalar(out=m8s[:, hp % 2, 0:MLOC], in0=qnw[0:2, :],
                            scalar1=kng[:, 2 * hp:2 * hp + 1], scalar2=SHIFT,
                            op0=ALU.mult, op1=ALU.add)
            e.tensor_scalar(out=m8s[:, hp % 2, MLOC:TLOC], in0=qns[0:2, :],
                            scalar1=kng[:, 2 * hp + 1:2 * hp + 2], scalar2=SHIFT,
                            op0=ALU.mult, op1=ALU.add).then_inc(sems["c_dv"], 1)
            e.drain()
        v.do("vector", m8chain)
        cd_m8 = v.inc("c_dv")
        W("sync", "c_dv", cd_m8)
        qsem2 = "d_qst" if hp % 2 == 0 else "d_qs1"
        dma("sync", qsem2, q_d[128 * hp:128 * hp + 128, :], qtmp[:, hp % 2, :])
        qst = dma("sync", qsem2, m8_d[2 * hp:2 * hp + 2, :], m8s[:, hp % 2, :])
        qst_hist.append((qsem2, qst))
        if hp >= 1:
            pass

    if debug and stage >= 7 and phases == 1:
        W("sync", "c_dv", v.n["c_dv"])
        W("sync", "d_qst", v.n["d_qst"])
        W("sync", "cc", 3)
        v.do("gpsimd", lambda e, x=v.n["c_dv"]: e.wait_ge(sems["c_dv"], x))
        dma("gpsimd", "d_dbg", dbg["dbg_xln1T"][:, :], xln1T[:, 0, :])
        dma("gpsimd", "d_dbg", dbg["dbg_q"][:, :], q_d[0:128, :])
        dma("gpsimd", "d_dbg", dbg["dbg_m8"][:, :], m8_d[0:2, :])
        dma("gpsimd", "d_dbg", dbg["dbg_k"][:, :], kvkout[0:128, :])
        W("gpsimd", "d_dbg", v.n["d_dbg"])

    barrier_all(1)
    v.flush(nc)
    p1.close()

    if phases < 2:
        es.close()
        return nc, dbg

    p2 = ExitStack()
    kbufA = [p2.enter_context(nc.sbuf_tensor(f"kbufA{i}", [65, 8192], BF16)) for i in range(2)]
    kbufB = [p2.enter_context(nc.sbuf_tensor(f"kbufB{i}", [65, 8192], BF16)) for i in range(2)]
    vbufA = [p2.enter_context(nc.sbuf_tensor(f"vbufA{i}", [128, 65 * 64], BF16)) for i in range(2)]
    vbufB = [p2.enter_context(nc.sbuf_tensor(f"vbufB{i}", [128, 65 * 64], BF16)) for i in range(2)]
    qbufA = [p2.enter_context(nc.sbuf_tensor(f"qbufA{i}", [65, 512], BF16)) for i in range(2)]
    qbufB = [p2.enter_context(nc.sbuf_tensor(f"qbufB{i}", [65, 512], BF16)) for i in range(2)]
    pt = [p2.enter_context(nc.sbuf_tensor(f"pt{i}", [128, 1024], BF16)) for i in range(4)]
    uA = p2.enter_context(nc.sbuf_tensor("uA", [64, 512], F32))
    uB = p2.enter_context(nc.sbuf_tensor("uB", [64, 512], F32))
    s65 = p2.enter_context(nc.sbuf_tensor("s65", [65, 1024], F32))
    sbcA = p2.enter_context(nc.sbuf_tensor("sbcA", [64, 512], F32))
    sbcB = p2.enter_context(nc.sbuf_tensor("sbcB", [64, 512], F32))
    astA = p2.enter_context(nc.sbuf_tensor("astA", [64, 512], BF16))
    astB = p2.enter_context(nc.sbuf_tensor("astB", [64, 512], BF16))
    psum_s = [p2.enter_context(nc.psum_tensor(f"psum_s{i}", [128, 1024], F32)) for i in range(2)]
    pavA = [p2.enter_context(nc.psum_tensor(f"pavA{i}", [128, 512], F32)) for i in range(2)]
    pavB = [p2.enter_context(nc.psum_tensor(f"pavB{i}", [128, 512], F32)) for i in range(2)]

    kvk_r = kvkout.rearrange("(r h j2) c -> h j2 r c", h=H, j2=65)
    kvv_r = kvvout.rearrange("(r bb i) d -> bb r i d", bb=2, i=MLOC)

    for s_ in ("d_kst", "d_vst", "d_qst", "d_qs1"):
        W("vector", s_, v.n[s_])

    def vones(e):
        for buf in (vbufA[0], vbufA[1], vbufB[0], vbufB[1]):
            e.memset(buf[:].rearrange("p (k c) -> p k c", c=65)[:, :, 64:65], 1.0).then_inc(sems["c_dv"], 1)
    v.do("vector", vones)
    CD_VONES = v.inc("c_dv", 4)

    NUNIT = 32
    W("sync", "cc", 3)
    for s_ in ("d_kst", "d_vst", "d_qst", "d_qs1", "d_kmx"):
        W("sync", s_, v.n[s_])
    for j in range(NUNIT):
        u, hp = divmod(j, 8)
        b, qbi = divmod(u, 2)
        colbase = b * MLOC + qbi * 512
        par = j % 2
        kldsem = "d_kld" if par == 0 else "d_kld"
        ksem = "d_kld" if par == 0 else "d_kld1"
        vsem = "d_vld" if par == 0 else "d_vld1"
        qsem = "d_qld" if par == 0 else "d_qld1"
        if j >= 2:
            W("sync", "c_av", 64 * (j - 1))
        dma("sync", ksem, kbufA[par][:].rearrange("p (r c) -> p r c", r=8),
            kvk_r[2 * hp][:, :, b * MLOC:(b + 1) * MLOC])
        dma("sync", ksem, kbufB[par][:].rearrange("p (r c) -> p r c", r=8),
            kvk_r[2 * hp + 1][:, :, b * MLOC:(b + 1) * MLOC])
        for c3 in range(8):
            src = kvv_r[b][c3].rearrange("(kt p) d -> p kt d", p=128)
            srcA = src[:, :, 128 * hp:128 * hp + 64]
            srcB = src[:, :, 128 * hp + 64:128 * hp + 128]
            dstA = vbufA[par][:].rearrange("p (k c) -> p k c", c=65)[:, 8 * c3:8 * c3 + 8, 0:64]
            dstB = vbufB[par][:].rearrange("p (k c) -> p k c", c=65)[:, 8 * c3:8 * c3 + 8, 0:64]
            dma("sync", vsem, dstA, srcA)
            dma("sync", vsem, dstB, srcB)
        dma("sync", qsem, qbufA[par][0:64, :], q_d[128 * hp:128 * hp + 64, colbase:colbase + 512])
        dma("sync", qsem, qbufA[par][64:65, :], m8_d[2 * hp:2 * hp + 1, colbase:colbase + 512])
        dma("sync", qsem, qbufB[par][0:64, :], q_d[128 * hp + 64:128 * hp + 128, colbase:colbase + 512])
        dma("sync", qsem, qbufB[par][64:65, :], m8_d[2 * hp + 1:2 * hp + 2, colbase:colbase + 512])
        kld_tgt = (ksem, v.n[ksem])
        vld_tgt = (vsem, v.n[vsem])
        qld_tgt = (qsem, v.n[qsem])

        W("tensor", kld_tgt[0], kld_tgt[1])
        W("tensor", qld_tgt[0], qld_tgt[1])

        def emit_scores(kt):
            I = 64 * j + kt
            if I >= 2:
                W("tensor", "c_ex", I - 1)

            def sc(e, par=par, kt=kt, I=I):
                e.matmul(out=psum_s[I % 2][:, 0:512],
                         lhsT=kbufA[par][:, 128 * kt:128 * kt + 128],
                         rhs=qbufA[par][:, :], start=True, stop=True)
                e.matmul(out=psum_s[I % 2][:, 512:1024],
                         lhsT=kbufB[par][:, 128 * kt:128 * kt + 128],
                         rhs=qbufB[par][:, :], start=True,
                         stop=True).then_inc(sems["c_sc"], 1)
            v.do("tensor", sc)
            v.inc("c_sc")
            W("scalar", "c_sc", I + 1)
            if I >= 4:
                W("scalar", "c_av", I - 3)

            def ex(e, kt=kt, I=I):
                e.activation(out=pt[I % 4][:, :], in_=psum_s[I % 2][:, :], func=AF.Exp,
                             bias=zero_b[:, 0:1], scale=0.125).then_inc(sems["c_ex"], 1)
            v.do("scalar", ex)
            v.inc("c_ex")

        def emit_attnv(kt):
            I = 64 * j + kt
            W("tensor", "c_ex", I + 1)
            if kt == 0:
                W("tensor", vld_tgt[0], vld_tgt[1])
                if j == 0:
                    W("tensor", "c_dv", CD_VONES)
                if j >= 2:
                    W("tensor", "c_nr", j - 1)

            def av(e, par=par, kt=kt, I=I, j=j):
                e.matmul(out=pavA[j % 2][0:65, :], lhsT=vbufA[par][:, 65 * kt:65 * kt + 65],
                         rhs=pt[I % 4][:, 0:512], start=(kt == 0), stop=(kt == 63))
                e.matmul(out=pavB[j % 2][0:65, :], lhsT=vbufB[par][:, 65 * kt:65 * kt + 65],
                         rhs=pt[I % 4][:, 512:1024], start=(kt == 0),
                         stop=(kt == 63)).then_inc(sems["c_av"], 1)
            v.do("tensor", av)
            v.inc("c_av")

        emit_scores(0)
        for kt in range(1, NKT):
            emit_scores(kt)
            emit_attnv(kt - 1)
        emit_attnv(NKT - 1)

        W("vector", "c_av", 64 * (j + 1))
        if j >= 1:
            W("vector", "d_at", 32 * j)

        def nrm1(e, j=j):
            e.tensor_scalar(out=s65[64:65, 0:512], in0=pavA[j % 2][64:65, :], scalar1=1e-30,
                            scalar2=None, op0=ALU.add)
            e.tensor_scalar(out=s65[64:65, 512:1024], in0=pavB[j % 2][64:65, :], scalar1=1e-30,
                            scalar2=None, op0=ALU.add).then_inc(sems["c_rc"], 1)
            e.tensor_copy(out=uA[:], in_=pavA[j % 2][0:64, :])
            e.tensor_copy(out=uB[:], in_=pavB[j % 2][0:64, :]).then_inc(sems["c_nr"], 1)
        v.do("vector", nrm1)
        v.inc("c_rc")
        v.inc("c_nr")
        W("gpsimd", "c_rc", v.n["c_rc"])

        def bc(e, n0=v.n["d_bc"]):
            e.dma_start(out=bc_d[:, :], in_=s65[64:65, :]).then_inc(sems["d_bc"], 16)
            e.wait_ge(sems["d_bc"], n0 + 16)
            e.dma_start(out=sbcA[:], in_=bc_d[0:1, 0:512].broadcast_to([64, 512])).then_inc(sems["d_bc"], 16)
            e.dma_start(out=sbcB[:], in_=bc_d[0:1, 512:1024].broadcast_to([64, 512])).then_inc(sems["d_bc"], 16)
        v.do("gpsimd", bc)
        v.inc("d_bc", 48)
        W("vector", "d_bc", v.n["d_bc"])

        def nrm2(e):
            e.drain()
            e.reciprocal(out=sbcA[:], in_=sbcA[:])
            e.reciprocal(out=sbcB[:], in_=sbcB[:])
            e.drain()
            e.tensor_tensor(out=astA[:], in0=uA[:], in1=sbcA[:], op=ALU.mult).then_inc(sems["c_mul"], 1)
            e.tensor_tensor(out=astB[:], in0=uB[:], in1=sbcB[:], op=ALU.mult).then_inc(sems["c_mul"], 1)
        v.do("vector", nrm2)
        v.inc("c_mul", 2)
        W("gpsimd", "c_mul", v.n["c_mul"])
        dma("gpsimd", "d_at", attn_d[b, 128 * hp:128 * hp + 64, qbi * 512:qbi * 512 + 512], astA[:])
        dma("gpsimd", "d_at", attn_d[b, 128 * hp + 64:128 * hp + 128, qbi * 512:qbi * 512 + 512], astB[:])

    if debug and phases == 2:
        v.do("gpsimd", lambda e, x=v.n["d_at"]: e.wait_ge(sems["d_at"], x))
        dma("gpsimd", "d_dbg", dbg["dbg_attn"][:, :], attn_d[0, 0:128, :])
        W("gpsimd", "d_dbg", v.n["d_dbg"])

    barrier_all(2)
    v.flush(nc)
    p2.close()

    if phases < 3:
        es.close()
        return nc, dbg

    p36 = ExitStack()
    xln3T = p36.enter_context(nc.sbuf_tensor("xln3T", [128, 8, TLOC], BF16))
    phi_sb = p36.enter_context(nc.sbuf_tensor("phi_sb", [128, 8, NP], BF16))
    p34 = ExitStack()
    xln3 = p34.enter_context(nc.sbuf_tensor("xln3", [128, NT, D], BF16))
    p3 = ExitStack()
    attn_t = p3.enter_context(nc.sbuf_tensor("attn_t", [128, 2, 8, 128], BF16))
    pwT = p3.enter_context(nc.sbuf_tensor("pwT", [128, 8, D], BF16))
    prow = p3.enter_context(nc.sbuf_tensor("prow", [1, D], BF16))
    o_bf = p3.enter_context(nc.sbuf_tensor("o_bf", [128, 8, D], BF16))
    scrA = p3.enter_context(nc.sbuf_tensor("scrA", [128, D], F32))
    scrB_ap = sq_scr[:].bitcast(F32)
    ppo = [p3.enter_context(nc.psum_tensor(f"ppo{i}", [128, D], F32)) for i in range(2)]
    ptr2 = [p3.enter_context(nc.psum_tensor(f"ptr2{i}", [128, 512], BF16)) for i in range(2)]

    def lnstats_sbuf(src_ap_fn, ntiles, base_col=0):
        for t in range(ntiles):
            v.do("vector", lambda e, t=t: e.tensor_reduce(
                out=stat[:, base_col + t:base_col + t + 1], in_=src_ap_fn(t), axis=AX, op=ALU.add))
            W("scalar", "c_ac", v.n["c_ac"])

            def sqf(e, t=t):
                e.activation(out=sq_scr[:, 0:D], in_=src_ap_fn(t), func=AF.Square,
                             bias=zero_b[:, 0:1], scale=1.0,
                             accum_out=stat2[:, base_col + t:base_col + t + 1]).then_inc(sems["c_ac"], 1)
            v.do("scalar", sqf)
            v.inc("c_ac")

    def ln_rstd(n, c0=0):
        def f(e):
            sl = slice(c0, c0 + n)
            e.drain()
            e.tensor_scalar(out=mean_t[:, sl], in0=stat[:, sl], scalar1=1.0 / D,
                            scalar2=None, op0=ALU.mult)
            e.drain()
            e.tensor_tensor(out=var_t[:, sl], in0=mean_t[:, sl], in1=mean_t[:, sl], op=ALU.mult)
            e.drain()
            e.tensor_scalar(out=nsc1[:, sl], in0=stat2[:, sl], scalar1=1.0 / D,
                            scalar2=None, op0=ALU.mult)
            e.drain()
            e.tensor_tensor(out=var_t[:, sl], in0=nsc1[:, sl], in1=var_t[:, sl], op=ALU.subtract)
            e.drain()
            e.tensor_scalar(out=var_t[:, sl], in0=var_t[:, sl], scalar1=EPS, scalar2=None, op0=ALU.add)
            e.drain()
            e.tensor_scalar(out=nsi1[:, sl], in0=var_t[:, sl].bitcast(I32), scalar1=1,
                            scalar2=None, op0=ALU.arith_shift_right)
            e.drain()
            e.tensor_scalar(out=nsi2[:, sl], in0=nsi1[:, sl], scalar1=-1, scalar2=int(MAGIC),
                            op0=ALU.mult, op1=ALU.add)
            e.drain()
            e.tensor_copy(out=rstd_t[:, sl].bitcast(I32), in_=nsi2[:, sl])
            e.drain()
            for _ in range(4):
                e.tensor_tensor(out=nsc1[:, sl], in0=rstd_t[:, sl], in1=rstd_t[:, sl], op=ALU.mult)
                e.drain()
                e.tensor_tensor(out=nsc1[:, sl], in0=nsc1[:, sl], in1=var_t[:, sl], op=ALU.mult)
                e.drain()
                e.tensor_scalar(out=nsc1[:, sl], in0=nsc1[:, sl], scalar1=-0.5, scalar2=1.5,
                                op0=ALU.mult, op1=ALU.add)
                e.drain()
                e.tensor_tensor(out=rstd_t[:, sl], in0=rstd_t[:, sl], in1=nsc1[:, sl], op=ALU.mult)
                e.drain()
        v.do("vector", f)

    dma("gpsimd", "d_pw", pwT[:], proj_wT_d.rearrange("(dt p) e -> p dt e", p=128))
    PWT_TGT = v.n["d_pw"]
    for i in range(2):
        dma("gpsimd", "d_ph2", phi_sb[:, 4 * i:4 * i + 4, :],
            phi_d.rearrange("(dt p) n -> p dt n", p=128)[:, 4 * i:4 * i + 4, :])
    PHI_TGT = v.n["d_ph2"]
    dma("gpsimd", "d_g", prow[:], projb_d[:, :])
    PROW_TGT = v.n["d_g"]
    W("sync", "d_at", v.n["d_at"])

    pmm_hist = []
    evac3_hist = []
    for b in range(B):
        if b == 0:
            W("tensor", "d_pw", PWT_TGT)
            W("tensor", "d_g", PROW_TGT)
        for ti8 in range(8):
            t_glob = 8 * b + ti8
            n3 = 8 * b + ti8
            psem = "d_p0" if n3 % 2 == 0 else "d_p1"
            if n3 >= 2:
                W("sync", "c_pe", pmm_hist[n3 - 2])
            dma("sync", psem, attn_t[:, ti8 % 2],
                attn_d[b].rearrange("(dt p) q -> p dt q", p=128)[:, :, 128 * ti8:128 * ti8 + 128])
            W("tensor", psem, v.n[psem])
            if n3 >= 2:
                W("tensor", "c_dv", evac3_hist[n3 - 2])

            def pmm(e, ti8=ti8):
                for c in range(2):
                    for dt in range(8):
                        e.matmul(out=ppo[ti8 % 2][:, 512 * c:512 * c + 512],
                                 lhsT=attn_t[:, ti8 % 2, dt, :],
                                 rhs=pwT[:, dt, 512 * c:512 * c + 512],
                                 start=(dt == 0), stop=False)
                    mm = e.matmul(out=ppo[ti8 % 2][:, 512 * c:512 * c + 512],
                                  lhsT=ones_row[0:1, 0:128],
                                  rhs=prow[0:1, 512 * c:512 * c + 512],
                                  start=False, stop=True)
                    if c == 1:
                        mm.then_inc(sems["c_pe"], 1)
            v.do("tensor", pmm)
            pe_p = v.inc("c_pe")
            pmm_hist.append(pe_p)
            W("vector", "c_pe", pe_p)
            v.do("vector", lambda e, ti8=ti8: e.tensor_copy(
                out=o_bf[:, ti8, :], in_=ppo[ti8 % 2][:, :]).then_inc(sems["c_dv"], 1))
            evac3_hist.append(v.inc("c_dv"))
        CD_EV = v.n["c_dv"]
        W("vector", "c_dv", CD_EV)
        W("scalar", "c_dv", CD_EV)
        lnstats_sbuf(lambda t: o_bf[:, t, :], 8)
        AC_LN2 = v.n["c_ac"]
        W("vector", "c_ac", AC_LN2)
        emit_rstd_sqrt(8)
        for pair in range(4):
            def appl(e, pair=pair, b=b):
                for k2 in range(2):
                    t = 2 * pair + k2
                    e.tensor_scalar(out=(scrA[:] if k2 == 0 else scrB_ap),
                                    in0=o_bf[:, t, :], scalar1=mean_t[:, t:t + 1],
                                    scalar2=rstd_t[:, t:t + 1], op0=ALU.subtract, op1=ALU.mult)
                e.drain()
                for k2 in range(2):
                    t = 2 * pair + k2
                    mm = e.tensor_tensor(out=Rt[:, 8 * b + t, :], in0=Rt[:, 8 * b + t, :],
                                         in1=(scrA[:] if k2 == 0 else scrB_ap), op=ALU.add)
                    if k2 == 1:
                        mm.then_inc(sems["c_dv"], 1)
                e.drain()
            v.do("vector", appl)
            v.inc("c_dv")
        W("scalar", "c_dv", v.n["c_dv"])
        lnstats_sbuf(lambda t, b=b: Rt[:, 8 * b + t, :], 8)
        W("vector", "c_ac", v.n["c_ac"])
        emit_rstd_sqrt(8)
        for ti8 in range(8):
            t_glob = 8 * b + ti8
            if ti8 >= 1:
                W("vector", "c_pe", v.n["c_pe"])
            v.do("vector", lambda e, ti8=ti8, t_glob=t_glob: e.tensor_scalar(
                out=xln3[:, t_glob, :], in0=Rt[:, t_glob, :], scalar1=mean_t[:, ti8:ti8 + 1],
                scalar2=rstd_t[:, ti8:ti8 + 1], op0=ALU.subtract,
                op1=ALU.mult).then_inc(sems["c_dv"], 1))
            cd_x3 = v.inc("c_dv")
            W("tensor", "c_dv", cd_x3)
            for dh in range(2):
                W("tensor", "c_dv", v.n["c_dv"])

                def tr3b(e, t_glob=t_glob, dh=dh, p=ptr2[dh]):
                    for jj in range(4):
                        dt = 4 * dh + jj
                        mm = e.transpose(out=p[:, 128 * jj:128 * jj + 128],
                                         in_=xln3[:, t_glob, 128 * dt:128 * dt + 128],
                                         identity=ident[:])
                        if jj == 3:
                            mm.then_inc(sems["c_pe"], 1)
                v.do("tensor", tr3b)
                pe_t3 = v.inc("c_pe")
                W("vector", "c_pe", pe_t3)

                def ev3(e, t_glob=t_glob, dh=dh, p=ptr2[dh]):
                    e.tensor_copy(
                        out=xln3T[:, 4 * dh:4 * dh + 4, 128 * t_glob:128 * t_glob + 128],
                        in_=p[:].rearrange("p (dt c) -> p dt c", c=128),
                    ).then_inc(sems["c_dv"], 1)
                v.do("vector", ev3)
                v.inc("c_dv")

    if debug and phases == 3:
        v.do("gpsimd", lambda e, x=v.n["c_dv"]: e.wait_ge(sems["c_dv"], x))
        dma("gpsimd", "d_dbg", dbg["dbg_xln3"][:, :], xln3[:, 0, :])
        W("gpsimd", "d_dbg", v.n["d_dbg"])

    barrier_all(3)
    v.flush(nc)
    p3.close()

    if phases < 4:
        p34.close()
        p36.close()
        es.close()
        return nc, dbg

    p4 = ExitStack()
    logits_q = p4.enter_context(nc.sbuf_tensor("logits_q", [128, 16, 512], BF16))
    disp_q = p4.enter_context(nc.sbuf_tensor("disp_q", [128, 16, 512], BF16))
    dsc = p4.enter_context(nc.sbuf_tensor("dsc", [128, 512], BF16))
    th = p4.enter_context(nc.sbuf_tensor("th", [128, 512], BF16))
    slscr = p4.enter_context(nc.sbuf_tensor("slscr", [128, D], F32))
    ppl = [p4.enter_context(nc.psum_tensor(f"ppl{i}", [128, 512], F32)) for i in range(2)]
    psl = [p4.enter_context(nc.psum_tensor(f"psl{i}", [128, D], F32)) for i in range(2)]

    W("tensor", "d_ph2", PHI_TGT)

    lev_hist = []
    sev_hist = []
    for qtr in range(4):
        for ti in range(NT):
            nl = 16 * qtr + ti
            if nl >= 2:
                W("tensor", "c_dv", lev_hist[nl - 2])

            def lmm(e, ti=ti, qtr=qtr):
                for dt in range(8):
                    mm = e.matmul(out=ppl[ti % 2][:, :],
                                  lhsT=xln3T[:, dt, 128 * ti:128 * ti + 128],
                                  rhs=phi_sb[:, dt, 512 * qtr:512 * qtr + 512],
                                  start=(dt == 0), stop=(dt == 7))
                    if dt == 7:
                        mm.then_inc(sems["c_pe"], 1)
            v.do("tensor", lmm)
            pe_l = v.inc("c_pe")
            W("vector", "c_pe", pe_l)
            v.do("vector", lambda e, ti=ti: e.tensor_copy(
                out=logits_q[:, ti, :], in_=ppl[ti % 2][:, :]).then_inc(sems["c_dv"], 1))
            lev_hist.append(v.inc("c_dv"))
        CD_LQ = v.n["c_dv"]
        for tp in range(8):
            def dsp1(e, tp=tp):
                e.drain()
                e.tensor_tensor(out=dsc[:], in0=logits_q[:, tp, :],
                                in1=logits_q[:, tp + 8, :],
                                op=ALU.subtract).then_inc(sems["c_dv"], 1)
            v.do("vector", dsp1)
            cd_d = v.inc("c_dv")
            W("scalar", "c_dv", cd_d)
            W("scalar", "c_ac", v.n["c_ac"])

            def dsp2(e):
                e.activation(out=th[:], in_=dsc[:], func=AF.Tanh,
                             bias=zero_b[:, 0:1], scale=0.5).then_inc(sems["c_ac"], 1)
            v.do("scalar", dsp2)
            ac_t = v.inc("c_ac")
            W("vector", "c_ac", ac_t)

            def dsp3(e, tp=tp):
                e.tensor_scalar(out=disp_q[:, tp, :], in0=th[:], scalar1=1.0,
                                scalar2=None, op0=ALU.add)
                e.tensor_scalar(out=disp_q[:, tp + 8, :], in0=th[:], scalar1=-1.0,
                                scalar2=1.0, op0=ALU.mult,
                                op1=ALU.add).then_inc(sems["c_dv"], 1)
            v.do("vector", dsp3)
            v.inc("c_dv")
        CD_DISP = v.n["c_dv"]
        W("tensor", "c_dv", CD_DISP)
        for b in range(B):
            for nlt in range(4):
                ns4 = 8 * qtr + 4 * b + nlt
                if ns4 >= 2:
                    W("tensor", "c_dv", sev_hist[ns4 - 2])

                def smm(e, b=b, nlt=nlt):
                    for c in range(2):
                        for mt in range(8):
                            mm = e.matmul(out=psl[nlt % 2][:, 512 * c:512 * c + 512],
                                          lhsT=disp_q[:, 8 * b + mt, 128 * nlt:128 * nlt + 128],
                                          rhs=xln3[:, 8 * b + mt, 512 * c:512 * c + 512],
                                          start=(mt == 0), stop=(mt == 7))
                            if c == 1 and mt == 7:
                                mm.then_inc(sems["c_pe"], 1)
                v.do("tensor", smm)
                pe_s = v.inc("c_pe")
                W("vector", "c_pe", pe_s)
                W("vector", "d_sl", v.n["d_sl"])
                v.do("vector", lambda e, nlt=nlt: e.tensor_copy(
                    out=slscr[:], in_=psl[nlt % 2][:, :]).then_inc(sems["c_dv"], 1))
                cd_s = v.inc("c_dv")
                sev_hist.append(cd_s)
                W("sync", "c_dv", cd_s)
                npg = 4 * qtr + nlt
                for eg in range(4):
                    row0 = 64 * (4 * npg + eg) + 32 * b
                    dma("sync", "d_sl", slots_in[row0:row0 + 32, :],
                        slscr[32 * eg:32 * eg + 32, :])
    W("gpsimd", "d_sl", v.n["d_sl"])

    def rs(e):
        e.collective_compute("ReduceScatter", ALU.add, replica_groups=rg,
                             ins=[slots_in[0:NP, :].opt()],
                             outs=[slots_out[0:NP // NCORE, :].opt()]).then_inc(sems["cc"])
        e.collective_compute("ReduceScatter", ALU.add, replica_groups=rg,
                             ins=[slots_in[NP:2 * NP, :].opt()],
                             outs=[slots_out[NP // NCORE:2 * NP // NCORE, :].opt()]).then_inc(sems["cc"])
    v.do("gpsimd", rs)
    v.inc("cc", 2)

    if debug and phases == 4:
        v.do("gpsimd", lambda e, x=v.n["cc"]: e.wait_ge(sems["cc"], x))
        dma("gpsimd", "d_dbg", dbg["dbg_disp"][:, :], slots_out[0:128, :])
        W("gpsimd", "d_dbg", v.n["d_dbg"])

    barrier_all(4)
    v.flush(nc)
    p4.close()
    p34.close()

    if phases < 5:
        p36.close()
        es.close()
        return nc, dbg

    p5 = ExitStack()
    slots_sb = p5.enter_context(nc.sbuf_tensor("slots_sb", [128, 4, D], BF16))
    slotsT = p5.enter_context(nc.sbuf_tensor("slotsT", [128, 8, 512], BF16))
    wbuf = [p5.enter_context(nc.sbuf_tensor(f"wbuf{i}", [128, 8, D], BF16)) for i in range(2)]
    b2row = p5.enter_context(nc.sbuf_tensor("b2row", [1, EPC * D], BF16))
    ldb = p5.enter_context(nc.sbuf_tensor("ldb", [1, 2048], F32))
    ytmp = p5.enter_context(nc.sbuf_tensor("ytmp", [64, D], BF16))
    ptr5 = [p5.enter_context(nc.psum_tensor(f"ptr5{i}", [128, 512], BF16)) for i in range(2)]
    py = [p5.enter_context(nc.psum_tensor(f"py{i}", [64, D], F32)) for i in range(2)]

    W("gpsimd", "cc", 4)
    dma("gpsimd", "d_g", slots_sb[:, 0:2, :], slots_out[0:256, :].rearrange("(r p) d -> p r d", p=128))
    SLO_TGT1 = v.n["d_g"]
    W("gpsimd", "cc", 5)
    dma("gpsimd", "d_g2", slots_sb[:, 2:4, :], slots_out[256:512, :].rearrange("(r p) d -> p r d", p=128))
    SLO_TGT = v.n["d_g2"]
    for ch in range(4):
        if ch > 0:
            W("sync", "c_dv", v.n["c_dv"])
        dma("sync", "d_m", ldb[:], bsl[:, 2048 * ch:2048 * ch + 2048])
        W("vector", "d_m", v.n["d_m"])
        v.do("vector", lambda e, ch=ch: e.tensor_scalar(
            out=b2row[0:1, 2048 * ch:2048 * ch + 2048], in0=ldb[:], scalar1=2.0,
            scalar2=None, op0=ALU.mult).then_inc(sems["c_dv"], 1))
        v.inc("c_dv")
    CD_B2 = v.n["c_dv"]
    W("tensor", "d_g", SLO_TGT1)
    for r4 in range(4):
        if r4 == 2:
            W("tensor", "d_g2", SLO_TGT)
        for dh in range(2):
            W("tensor", "c_dv", v.n["c_dv"])

            def tr5(e, r4=r4, dh=dh, p=ptr5[dh]):
                for jj in range(4):
                    dt = 4 * dh + jj
                    mm = e.transpose(out=p[:, 128 * jj:128 * jj + 128],
                                     in_=slots_sb[:, r4, 128 * dt:128 * dt + 128],
                                     identity=ident[:])
                    if jj == 3:
                        mm.then_inc(sems["c_pe"], 1)
            v.do("tensor", tr5)
            pe_t5 = v.inc("c_pe")
            W("vector", "c_pe", pe_t5)

            def ev5(e, r4=r4, dh=dh, p=ptr5[dh]):
                e.tensor_copy(
                    out=slotsT[:, 4 * dh:4 * dh + 4, 128 * r4:128 * r4 + 128],
                    in_=p[:].rearrange("p (dt c) -> p dt c", c=128),
                ).then_inc(sems["c_dv"], 1)
            v.do("vector", ev5)
            v.inc("c_dv")
    CD_SLT = v.n["c_dv"]
    W("gpsimd", "d_w", v.n["d_w"])
    W("tensor", "c_dv", CD_SLT)
    W("tensor", "c_dv", CD_B2)
    for eI in range(EPC):
        wsem = "d_w0" if eI % 2 == 0 else "d_w1"
        if eI >= 2:
            W("gpsimd", "c_pe", v.n["c_pe"])
        dma("gpsimd", wsem, wbuf[eI % 2][:], wbf[eI].rearrange("(dt p) d2 -> p dt d2", p=128))
        W("tensor", wsem, v.n[wsem])
        if eI >= 2:
            W("tensor", "c_dv", v.n["c_dv"])

        def ymm(e, eI=eI):
            for c in range(2):
                for dt in range(8):
                    e.matmul(out=py[eI % 2][:, 512 * c:512 * c + 512],
                             lhsT=slotsT[:, dt, 64 * eI:64 * eI + 64],
                             rhs=wbuf[eI % 2][:, dt, 512 * c:512 * c + 512],
                             start=(dt == 0), stop=False)
                mm = e.matmul(out=py[eI % 2][:, 512 * c:512 * c + 512],
                              lhsT=ones_row[0:1, 0:64],
                              rhs=b2row[0:1, 1024 * eI + 512 * c:1024 * eI + 512 * c + 512],
                              start=False, stop=True)
                if c == 1:
                    mm.then_inc(sems["c_pe"], 1)
        v.do("tensor", ymm)
        pe_y = v.inc("c_pe")
        W("vector", "c_pe", pe_y)
        W("vector", "d_y", v.n["d_y"])

        def yev(e, eI=eI):
            e.tensor_copy(out=ytmp[:], in_=py[eI % 2][:, :]).then_inc(sems["c_dv"], 1)
        v.do("vector", yev)
        cd_y = v.inc("c_dv")
        W("sync", "c_dv", cd_y)
        dma("sync", "d_y", y_in[64 * eI:64 * eI + 64, :], ytmp[:])
        if eI == 3 or eI == 7:
            W("gpsimd", "d_y", v.n["d_y"])
            lo = 0 if eI == 3 else 256

            def agy(e, lo=lo):
                e.collective_compute("AllGather", ALU.bypass, replica_groups=rg,
                                     ins=[y_in[lo:lo + 256, :].opt()],
                                     outs=[y_ag[8 * lo:8 * lo + 2048, :].opt()]).then_inc(sems["cc"])
            v.do("gpsimd", agy)
            v.inc("cc")

    if debug and phases == 5:
        v.do("gpsimd", lambda e, x=v.n["cc"]: e.wait_ge(sems["cc"], x))
        dma("gpsimd", "d_dbg", dbg["dbg_y"][:, :], y_ag[0:128, :])
        W("gpsimd", "d_dbg", v.n["d_dbg"])

    barrier_all(5)
    v.flush(nc)
    p5.close()

    if phases < 6:
        p36.close()
        es.close()
        return nc, dbg

    p6 = ExitStack()
    y_b = p6.enter_context(nc.sbuf_tensor("y_b", [128, 16, D], BF16))
    ct_sb = p6.enter_context(nc.sbuf_tensor("ct_sb", [128, 16, D], BF16))
    co_f32 = p6.enter_context(nc.sbuf_tensor("co_f32", [128, 2, D], F32))
    scr4 = sq_scr[:].bitcast(F32)
    pct = [p6.enter_context(nc.psum_tensor(f"pct{i}", [128, D], F32)) for i in range(2)]
    pco = [p6.enter_context(nc.psum_tensor(f"pco{i}", [128, D], F32)) for i in range(2)]

    cte_hist = []
    cev_hist = []
    for b in range(B):
        W("gpsimd", "cc", 7)
        if b == 1:
            W("gpsimd", "c_pe", v.n["c_pe"])
        for g in range(16):
            for eg in range(4):
                e_glob = 4 * g + eg
                if e_glob < 32:
                    r_, l_ = e_glob // 4, e_glob % 4
                    row0 = 256 * r_ + 64 * l_ + 32 * b
                else:
                    r_, l_ = (e_glob - 32) // 4, (e_glob - 32) % 4
                    row0 = 2048 + 256 * r_ + 64 * l_ + 32 * b
                dma("gpsimd", "d_yb", y_b[32 * eg:32 * eg + 32, g, :],
                    y_ag[row0:row0 + 32, :])
        YB_TGT = v.n["d_yb"]
        for npt in range(NT):
            nc6 = 16 * b + npt
            if nc6 >= 2:
                W("tensor", "c_ac", cte_hist[nc6 - 2])

            def ctm(e, npt=npt, b=b):
                for c in range(2):
                    for dt in range(8):
                        mm = e.matmul(out=pct[npt % 2][:, 512 * c:512 * c + 512],
                                      lhsT=phi_sb[:, dt, 128 * npt:128 * npt + 128],
                                      rhs=xln3T[:, dt, MLOC * b + 512 * c:MLOC * b + 512 * c + 512],
                                      start=(dt == 0), stop=(dt == 7))
                        if c == 1 and dt == 7:
                            mm.then_inc(sems["c_pe"], 1)
            v.do("tensor", ctm)
            pe_c = v.inc("c_pe")
            W("scalar", "c_pe", pe_c)
            W("scalar", "c_ac", v.n["c_ac"])

            def cte(e, npt=npt):
                e.activation(out=ct_sb[:, npt, :], in_=pct[npt % 2][:, :], func=AF.Exp,
                             bias=negc[:, 0:1], scale=1.0).then_inc(sems["c_ac"], 1)
            v.do("scalar", cte)
            cte_hist.append(v.inc("c_ac"))
        AC_CT = v.n["c_ac"]
        W("tensor", "c_ac", AC_CT)
        W("tensor", "d_yb", YB_TGT)
        for ti8 in range(8):
            t_glob = 8 * b + ti8
            n6 = 8 * b + ti8
            if n6 >= 2:
                W("tensor", "c_dv", cev_hist[n6 - 2])

            def com(e, ti8=ti8):
                for c in range(2):
                    for npt in range(NT):
                        mm = e.matmul(out=pco[ti8 % 2][:, 512 * c:512 * c + 512],
                                      lhsT=ct_sb[:, npt, 128 * ti8:128 * ti8 + 128],
                                      rhs=y_b[:, npt, 512 * c:512 * c + 512],
                                      start=(npt == 0), stop=(npt == 15))
                        if c == 1 and npt == 15:
                            mm.then_inc(sems["c_pe"], 1)
            v.do("tensor", com)
            pe_co = v.inc("c_pe")
            W("vector", "c_pe", pe_co)
            W("vector", "c_ac", v.n["c_ac"])

            def cev(e, ti8=ti8):
                e.tensor_copy(out=co_f32[:, ti8 % 2, :],
                              in_=pco[ti8 % 2][:, :]).then_inc(sems["c_dv"], 1)
            v.do("vector", cev)
            cd_co = v.inc("c_dv")
            cev_hist.append(cd_co)
            W("vector", "c_dv", cd_co)
            v.do("vector", lambda e, ti8=ti8: e.tensor_reduce(
                out=stat[:, ti8:ti8 + 1], in_=co_f32[:, ti8 % 2, :], axis=AX, op=ALU.add))
            W("scalar", "c_dv", cd_co)
            W("scalar", "c_ac", v.n["c_ac"])

            W("scalar", "c_dv", v.n["c_dv"])

            def csq(e, ti8=ti8):
                e.activation(out=sq_scr[:, 0:D], in_=co_f32[:, ti8 % 2, :], func=AF.Square,
                             bias=zero_b[:, 0:1], scale=1.0,
                             accum_out=stat2[:, ti8:ti8 + 1]).then_inc(sems["c_ac"], 1)
            v.do("scalar", csq)
            ac_c = v.inc("c_ac")
            if ti8 % 2 == 1:
                W("vector", "c_ac", ac_c)
                ln_rstd(2, c0=ti8 - 1)

                def capply(e, ti8=ti8, b=b):
                    for k2 in range(2):
                        tt_ = ti8 - 1 + k2
                        tg = 8 * b + tt_
                        e.drain()
                        e.tensor_scalar(out=scr4, in0=co_f32[:, tt_ % 2, :],
                                        scalar1=mean_t[:, tt_:tt_ + 1],
                                        scalar2=rstd_t[:, tt_:tt_ + 1],
                                        op0=ALU.subtract, op1=ALU.mult)
                        e.drain()
                        e.tensor_tensor(out=Rt[:, tg, :], in0=Rt[:, tg, :],
                                        in1=scr4, op=ALU.add).then_inc(sems["c_dv"], 1)
                    e.drain()
                v.do("vector", capply)
                v.inc("c_dv", 2)
                cd_fin = v.n["c_dv"]
                W("sync", "c_dv", cd_fin)
                tg0 = 8 * b + ti8 - 1
                dma("sync", "d_out", yout[128 * tg0:128 * tg0 + 128, :], Rt[:, tg0, :])
                dma("sync", "d_out", yout[128 * tg0 + 128:128 * tg0 + 256, :], Rt[:, tg0 + 1, :])

    W("sync", "d_out", v.n["d_out"])

    barrier_all(6)
    v.flush(nc)
    p6.close()
    p36.close()
    es.close()
    return nc, dbg


def host_prep(L0_attn_x, L0_attn_qkv_w, L0_attn_qkv_b, L0_attn_proj_w,
              L0_attn_proj_b, L0_ffn_phi, L0_ffn_W, L0_ffn_b):
    x = np.asarray(L0_attn_x, np.float32)
    qkv_w = np.asarray(L0_attn_qkv_w, np.float32)
    qkv_b = np.asarray(L0_attn_qkv_b, np.float32)
    proj_w = np.asarray(L0_attn_proj_w, np.float32)
    proj_b = np.asarray(L0_attn_proj_b, np.float32)
    phi = np.asarray(L0_ffn_phi, np.float32)
    Wx = np.asarray(L0_ffn_W, np.float32)
    bx = np.asarray(L0_ffn_b, np.float32)

    qkv_wT = np.ascontiguousarray(qkv_w.T)
    proj_wT = np.ascontiguousarray(proj_w.T)
    phi2 = np.ascontiguousarray(phi.reshape(D, NP))
    ident = np.eye(128, dtype=ml_dtypes.bfloat16)
    in_maps = []
    for c in range(NCORE):
        xsl = np.ascontiguousarray(x[:, MLOC * c:MLOC * (c + 1), :].reshape(TLOC, D))
        in_maps.append({
            "xs": xsl,
            "qkv_wT": qkv_wT,
            "proj_wT": proj_wT,
            "phi2": phi2,
            "wsl": np.ascontiguousarray(np.concatenate(
                [Wx[4 * c:4 * c + 4], Wx[32 + 4 * c:32 + 4 * c + 4]], axis=0)),
            "bsl": np.ascontiguousarray(np.concatenate(
                [bx[4 * c:4 * c + 4], bx[32 + 4 * c:32 + 4 * c + 4]], axis=0).reshape(1, EPC * D)),
            "qkvb": qkv_b.reshape(1, 3 * D),
            "projb": proj_b.reshape(1, D),
            "ident": ident,
        })
    return in_maps


def kernel(**inputs):
    in_maps = host_prep(**inputs)
    nc, _ = build_nc(debug=False)
    res = run_bass_kernel_spmd(nc, in_maps, core_ids=list(range(NCORE)))
    out = np.empty((B, NCORE * MLOC, D), np.float32)
    for c in range(NCORE):
        o = res.results[c]["yout"].reshape(B, MLOC, D)
        out[:, MLOC * c:MLOC * (c + 1), :] = o
    return out


# version 12
# speedup vs baseline: 1.0641x; 1.0641x over previous
import numpy as np
import ml_dtypes
from collections import defaultdict
from contextlib import ExitStack
import concourse.bass as bass
import concourse.mybir as mybir
import concourse.bass_utils as _bass_utils
from concourse.bass_utils import run_bass_kernel_spmd

_orig_run_command = _bass_utils.run_command

def _run_command_ldw(cmd, *a, **kw):
    if isinstance(cmd, list):
        cmd = ["--enable-ldw-opt=true" if c == "--enable-ldw-opt=false" else c for c in cmd]
    return _orig_run_command(cmd, *a, **kw)

if getattr(_bass_utils.run_command, "__name__", "") != "_run_command_ldw":
    _bass_utils.run_command = _run_command_ldw

F32 = mybir.dt.float32
BF16 = mybir.dt.bfloat16
I32 = mybir.dt.int32
AF = mybir.ActivationFunctionType
ALU = mybir.AluOpType
AX = mybir.AxisListType.X

NCORE = 8
B = 2
D = 1024
H = 16
HP = 8
MLOC = 1024
TLOC = 2048
NT = 16
NKT = 64
NP = 2048
EPC = 8
SHIFT = 600.0
CFFN = 4.0
EPS = 1e-5
MAGIC = 0x5F3759DF

ENGS = ("sync", "gpsimd", "vector", "scalar", "tensor")


class Sched:

    def __init__(self):
        self.q = {e: [] for e in ENGS}
        self.n = defaultdict(int)

    def do(self, eng, fn):
        self.q[eng].append(fn)

    def inc(self, sem, k=1):
        self.n[sem] += k
        return self.n[sem]

    def flush(self, nc):
        with nc.Block() as blk:
            for name in ENGS:
                fns = self.q[name]

                def runner(e, fns=fns):
                    for f in fns:
                        f(e)
                getattr(blk, name)(runner)
        self.q = {e: [] for e in ENGS}


def newton_rsqrt(v, vec_ap, y_ap, t1_ap, ti1, ti2, iters=4):
    def f(e):
        e.drain()
        e.tensor_scalar(out=ti1, in0=vec_ap.bitcast(I32), scalar1=1, scalar2=None,
                        op0=ALU.arith_shift_right)
        e.drain()
        e.tensor_scalar(out=ti2, in0=ti1, scalar1=-1, scalar2=int(MAGIC),
                        op0=ALU.mult, op1=ALU.add)
        e.drain()
        e.tensor_copy(out=y_ap.bitcast(I32), in_=ti2)
        e.drain()
        for _ in range(iters):
            e.tensor_tensor(out=t1_ap, in0=y_ap, in1=y_ap, op=ALU.mult)
            e.drain()
            e.tensor_tensor(out=t1_ap, in0=t1_ap, in1=vec_ap, op=ALU.mult)
            e.drain()
            e.tensor_scalar(out=t1_ap, in0=t1_ap, scalar1=-0.5, scalar2=1.5,
                            op0=ALU.mult, op1=ALU.add)
            e.drain()
            e.tensor_tensor(out=y_ap, in0=y_ap, in1=t1_ap, op=ALU.mult)
            e.drain()
    v.do("vector", f)


def build_nc(debug=False, phases=99, stage=99, lite=0):
    nc = bass.Bass("TRN2", target_bir_lowering=False)

    xs = nc.dram_tensor("xs", [TLOC, D], F32, kind="ExternalInput").ap()
    qkv_wT = nc.dram_tensor("qkv_wT", [D, 3 * D], F32, kind="ExternalInput").ap()
    proj_wT_d = nc.dram_tensor("proj_wT", [D, D], F32, kind="ExternalInput").ap()
    phi_d = nc.dram_tensor("phi2", [D, NP], F32, kind="ExternalInput").ap()
    wsl = nc.dram_tensor("wsl", [EPC, D, D], F32, kind="ExternalInput").ap()
    bsl = nc.dram_tensor("bsl", [1, EPC * D], F32, kind="ExternalInput").ap()
    qkvb = nc.dram_tensor("qkvb", [1, 3 * D], F32, kind="ExternalInput").ap()
    projb_d = nc.dram_tensor("projb", [1, D], F32, kind="ExternalInput").ap()
    ident_in = nc.dram_tensor("ident", [128, 128], BF16, kind="ExternalInput").ap()
    yout = nc.dram_tensor("yout", [TLOC, D], F32, kind="ExternalOutput").ap()

    kvkin = nc.dram_tensor("kvkin", [H * 65, TLOC], BF16).ap()
    kvkout = nc.dram_tensor("kvkout", [NCORE * H * 65, TLOC], BF16, addr_space="Shared").ap()
    kvvin = nc.dram_tensor("kvvin", [TLOC, D], BF16).ap()
    kvvout = nc.dram_tensor("kvvout", [NCORE * TLOC, D], BF16, addr_space="Shared").ap()
    q_d = nc.dram_tensor("q_d", [D, TLOC], BF16).ap()
    m8_d = nc.dram_tensor("m8_d", [H, TLOC], BF16).ap()
    kmax_in = nc.dram_tensor("kmax_in", [2, H], F32).ap()
    kmax_out = nc.dram_tensor("kmax_out", [2, H], F32).ap()
    attn_d = nc.dram_tensor("attn_d", [B, D, MLOC], BF16).ap()
    wbf = nc.dram_tensor("wbf", [EPC, D, D], BF16).ap()
    slots_in = nc.dram_tensor("slots_in", [NP * B, D], F32).ap()
    slots_out = nc.dram_tensor("slots_out", [NP * B // NCORE, D], F32).ap()
    y_in = nc.dram_tensor("y_in", [EPC * B * 32, D], BF16).ap()
    y_ag = nc.dram_tensor("y_ag", [NP * B, D], BF16, addr_space="Shared").ap()
    bc_d = nc.dram_tensor("bc_d", [1, 1024], F32).ap()

    dbg = {}
    if debug:
        for nm, shp in [("dbg_xln1T", [128, 2048]), ("dbg_k", [128, 2048]),
                        ("dbg_q", [128, 2048]), ("dbg_m8", [2, 2048]),
                        ("dbg_attn", [128, 1024]), ("dbg_xln3", [128, 1024]),
                        ("dbg_disp", [128, 1024]), ("dbg_ct", [128, 1024]),
                        ("dbg_y", [128, 1024])]:
            dbg[nm] = nc.dram_tensor(nm, shp, F32, kind="ExternalOutput").ap()

    rg = [list(range(NCORE))]
    es = ExitStack()
    E = es.enter_context

    sems = {}
    for s in ["ph", "d_x", "d_w", "d_m", "d_i", "d_kmx", "d_kmg", "d_kst", "d_vst", "d_qst", "d_qs1", "d_kld", "d_kld1",
              "d_vld", "d_vld1", "d_qld", "d_qld1", "d_at", "d_sl", "d_y", "d_yb", "d_out", "d_dbg",
              "d_bc", "d_g", "d_g2", "d_pw", "d_ph2", "d_p0", "d_p1", "d_w0", "d_w1", "cc", "c_pe", "c_dv", "c_ac", "c_sc", "c_ex", "c_av", "c_nr",
              "c_rc", "c_bc", "c_mul"]:
        sems[s] = E(nc.semaphore(s))

    v = Sched()

    def W(eng, sem, val):
        v.do(eng, lambda e, s=sems[sem], x=val: e.wait_ge(s, x))

    def dma(eng, sem, out, in_):
        def f(e, o=out, i=in_, s=sems[sem]):
            e.dma_start(out=o, in_=i).then_inc(s, 16)
        v.do(eng, f)
        return v.inc(sem, 16)


    def emit_rstd_sqrt(n, c0=0):
        def alg(e):
            sl = slice(c0, c0 + n)
            e.drain()
            e.tensor_scalar(out=mean_t[:, sl], in0=stat[:, sl], scalar1=1.0 / D,
                            scalar2=None, op0=ALU.mult)
            e.drain()
            e.tensor_tensor(out=var_t[:, sl], in0=mean_t[:, sl], in1=mean_t[:, sl], op=ALU.mult)
            e.drain()
            e.tensor_scalar(out=nsc1[:, sl], in0=stat2[:, sl], scalar1=1.0 / D,
                            scalar2=None, op0=ALU.mult)
            e.drain()
            e.tensor_tensor(out=var_t[:, sl], in0=nsc1[:, sl], in1=var_t[:, sl], op=ALU.subtract)
            e.drain()
            e.tensor_scalar(out=var_t[:, sl], in0=var_t[:, sl], scalar1=EPS,
                            scalar2=None, op0=ALU.add).then_inc(sems["c_dv"], 1)
        v.do("vector", alg)
        cd_v = v.inc("c_dv")
        W("scalar", "c_dv", cd_v)
        W("scalar", "c_ac", v.n["c_ac"])

        def sq_(e):
            e.activation(out=rstd_t[:, c0:c0 + n], in_=var_t[:, c0:c0 + n], func=AF.Sqrt,
                         bias=zero_b[:, 0:1], scale=1.0).then_inc(sems["c_ac"], 1)
        v.do("scalar", sq_)
        ac_s = v.inc("c_ac")
        W("vector", "c_ac", ac_s)

        def rc(e):
            e.reciprocal(out=rstd_t[:, c0:c0 + n], in_=rstd_t[:, c0:c0 + n])
            e.drain()
        v.do("vector", rc)

    def barrier_all(phase_idx):
        for e in ENGS:
            v.do(e, lambda x, s=sems["ph"]: x.sem_inc(s, 1))
        v.inc("ph", 5)
        tgt = v.n["ph"]
        for e in ENGS:
            v.do(e, lambda x, s=sems["ph"], t=tgt: x.wait_ge(s, t))

    Rt = E(nc.sbuf_tensor("Rt", [128, NT, D], F32))
    ident = E(nc.sbuf_tensor("identb", [128, 128], BF16))
    ones_row = E(nc.sbuf_tensor("ones_row", [1, 512], BF16))
    negc = E(nc.sbuf_tensor("negc", [128, 1], F32))
    zero_b = E(nc.sbuf_tensor("zero_b", [128, 1], F32))
    stat = E(nc.sbuf_tensor("stat", [128, 16], F32))
    stat2 = E(nc.sbuf_tensor("stat2", [128, 16], F32))
    mean_t = E(nc.sbuf_tensor("mean_t", [128, 16], F32))
    var_t = E(nc.sbuf_tensor("var_t", [128, 16], F32))
    rstd_t = E(nc.sbuf_tensor("rstd_t", [128, 16], F32))
    nsc1 = E(nc.sbuf_tensor("nsc1", [128, 16], F32))
    nsi1 = E(nc.sbuf_tensor("nsi1", [128, 16], I32))
    nsi2 = E(nc.sbuf_tensor("nsi2", [128, 16], I32))
    sq_scr = E(nc.sbuf_tensor("sq_scr", [128, 2048], BF16))

    p1 = ExitStack()
    qwT = p1.enter_context(nc.sbuf_tensor("qwT", [128, 8, 3 * D], BF16))
    xln1 = p1.enter_context(nc.sbuf_tensor("xln1", [128, 2, D], BF16))
    xln1T = p1.enter_context(nc.sbuf_tensor("xln1T", [128, 8, TLOC], BF16))
    ktmp = p1.enter_context(nc.sbuf_tensor("ktmp", [128, 2, TLOC], BF16))
    qtmp = p1.enter_context(nc.sbuf_tensor("qtmp", [128, 2, TLOC], BF16))
    vtmp = qtmp[:, :, 0:D]
    hind = p1.enter_context(nc.sbuf_tensor("hind", [128, 2], BF16))
    qkb_row = p1.enter_context(nc.sbuf_tensor("qkb_row", [1, 3 * D], BF16))
    kml = p1.enter_context(nc.sbuf_tensor("kml", [2, H], F32))
    kmt = p1.enter_context(nc.sbuf_tensor("kmt", [2, 2], F32))
    kmg = p1.enter_context(nc.sbuf_tensor("kmg", [2, H], F32))
    kng = p1.enter_context(nc.sbuf_tensor("kng", [2, H], F32))
    qn2 = p1.enter_context(nc.sbuf_tensor("qn2", [2, TLOC], F32))
    qnw = p1.enter_context(nc.sbuf_tensor("qnw", [2, MLOC], F32))
    qns = p1.enter_context(nc.sbuf_tensor("qns", [2, MLOC], F32))
    qni1 = p1.enter_context(nc.sbuf_tensor("qni1", [2, MLOC], I32))
    m8s = p1.enter_context(nc.sbuf_tensor("m8s", [2, 2, TLOC], BF16))

    psq = [p1.enter_context(nc.psum_tensor(f"psq{i}", [128, TLOC], F32)) for i in range(2)]
    ptrv = psq[1][:].bitcast(BF16)
    ptr0 = ptrv[:, 0:512]
    ptr1 = ptrv[:, 1024:1536]

    def _cut():
        barrier_all(1)
        v.flush(nc)
        p1.close()
        es.close()
        return nc, dbg

    xsr = xs.rearrange("(t p) d -> p t d", p=128)
    dma("sync", "d_x", Rt[:, 0:8, :], xsr[:, 0:8, :])
    dma("sync", "d_x", Rt[:, 8:16, :], xsr[:, 8:16, :])
    dma("sync", "d_i", ident[:], ident_in[:, :])
    qwr = qkv_wT.rearrange("(dt p) e -> p dt e", p=128)
    for i in range(4):
        dma("gpsimd", "d_w", qwT[:, 2 * i:2 * i + 2, :], qwr[:, 2 * i:2 * i + 2, :])

    def consts(e):
        e.memset(ones_row[:], 1.0).then_inc(sems["c_dv"], 1)
        e.memset(negc[:], -CFFN).then_inc(sems["c_dv"], 1)
        e.memset(zero_b[:], 0.0).then_inc(sems["c_dv"], 1)
        e.memset(hind[:], 0.0)
        e.drain()
        e.memset(hind[0:64, 0:1], 1.0).then_inc(sems["c_dv"], 1)
        e.memset(hind[64:128, 1:2], 1.0).then_inc(sems["c_dv"], 1)
        e.drain()
    v.do("vector", consts)
    CD_CONST = v.inc("c_dv", 5)

    dma("gpsimd", "d_g", qkb_row[:], qkvb[:, :])
    QKB_TGT = v.n["d_g"]

    W("sync", "c_dv", CD_CONST)
    for h in range(H):
        for ch4 in range(4):
            dma("sync", "d_kst", kvkin[h * 65 + 64:h * 65 + 65, 512 * ch4:512 * ch4 + 512],
                ones_row[0:1, :])

    W("vector", "d_x", 32)
    W("scalar", "d_x", 32)
    W("scalar", "c_dv", CD_CONST)
    for t in range(NT):
        v.do("vector", lambda e, t=t: e.tensor_reduce(
            out=stat[:, t:t + 1], in_=Rt[:, t, :], axis=AX, op=ALU.add))
        if t > 0:
            W("scalar", "c_ac", v.n["c_ac"])
        def sq(e, t=t):
            e.activation(out=sq_scr[:, 0:D], in_=Rt[:, t, :], func=AF.Square,
                         bias=zero_b[:, 0:1], scale=1.0,
                         accum_out=stat2[:, t:t + 1]).then_inc(sems["c_ac"], 1)
        v.do("scalar", sq)
        v.inc("c_ac", 1)
    AC_LN1 = v.n["c_ac"]

    def ln_finish(e, n=NT):
        e.drain()
        e.tensor_scalar(out=mean_t[:, 0:n], in0=stat[:, 0:n], scalar1=1.0 / D,
                        scalar2=None, op0=ALU.mult)
        e.drain()
        e.tensor_tensor(out=var_t[:, 0:n], in0=mean_t[:, 0:n], in1=mean_t[:, 0:n], op=ALU.mult)
        e.drain()
        e.tensor_scalar(out=nsc1[:, 0:n], in0=stat2[:, 0:n], scalar1=1.0 / D,
                        scalar2=None, op0=ALU.mult)
        e.drain()
        e.tensor_tensor(out=var_t[:, 0:n], in0=nsc1[:, 0:n], in1=var_t[:, 0:n], op=ALU.subtract)
        e.drain()
        e.tensor_scalar(out=var_t[:, 0:n], in0=var_t[:, 0:n], scalar1=EPS, scalar2=None, op0=ALU.add)
        e.drain()
        e.tensor_scalar(out=nsi1[:, 0:n], in0=var_t[:, 0:n].bitcast(I32), scalar1=1,
                        scalar2=None, op0=ALU.arith_shift_right)
        e.drain()
        e.tensor_scalar(out=nsi2[:, 0:n], in0=nsi1[:, 0:n], scalar1=-1, scalar2=int(MAGIC),
                        op0=ALU.mult, op1=ALU.add)
        e.drain()
        e.tensor_copy(out=rstd_t[:, 0:n].bitcast(I32), in_=nsi2[:, 0:n])
        e.drain()
        for _ in range(4):
            e.tensor_tensor(out=nsc1[:, 0:n], in0=rstd_t[:, 0:n], in1=rstd_t[:, 0:n], op=ALU.mult)
            e.drain()
            e.tensor_tensor(out=nsc1[:, 0:n], in0=nsc1[:, 0:n], in1=var_t[:, 0:n], op=ALU.mult)
            e.drain()
            e.tensor_scalar(out=nsc1[:, 0:n], in0=nsc1[:, 0:n], scalar1=-0.5, scalar2=1.5,
                            op0=ALU.mult, op1=ALU.add)
            e.drain()
            e.tensor_tensor(out=rstd_t[:, 0:n], in0=rstd_t[:, 0:n], in1=nsc1[:, 0:n], op=ALU.mult)
            e.drain()

    if stage < 2:
        return _cut()
    W("vector", "c_ac", AC_LN1)
    emit_rstd_sqrt(NT)
    W("tensor", "d_i", 16)
    ptrs = [ptr0, ptr1]
    tr1_hist = []
    ev1_hist = []
    for t in range(NT):
        if t >= 2:
            W("vector", "c_pe", tr1_hist[2 * (t - 2) + 1])
        v.do("vector", lambda e, t=t: e.tensor_scalar(
            out=xln1[:, t % 2, :], in0=Rt[:, t, :], scalar1=mean_t[:, t:t + 1],
            scalar2=rstd_t[:, t:t + 1], op0=ALU.subtract,
            op1=ALU.mult).then_inc(sems["c_dv"], 1))
        cd_ap = v.inc("c_dv")
        W("tensor", "c_dv", cd_ap)
        for dh in range(2):
            g1 = 2 * t + dh
            if g1 >= 2:
                W("tensor", "c_dv", ev1_hist[g1 - 2])

            def tr(e, t=t, dh=dh, p=ptrs[dh]):
                for j in range(4):
                    dt = 4 * dh + j
                    mm = e.transpose(out=p[:, 128 * j:128 * j + 128],
                                     in_=xln1[:, t % 2, 128 * dt:128 * dt + 128],
                                     identity=ident[:])
                    if j == 3:
                        mm.then_inc(sems["c_pe"], 1)
            v.do("tensor", tr)
            pe_g = v.inc("c_pe")
            tr1_hist.append(pe_g)
            W("vector", "c_pe", pe_g)

            def ev(e, t=t, dh=dh, p=ptrs[dh]):
                e.tensor_copy(
                    out=xln1T[:, 4 * dh:4 * dh + 4, 128 * t:128 * t + 128],
                    in_=p.rearrange("p (dt c) -> p dt c", c=128),
                ).then_inc(sems["c_dv"], 1)
            v.do("vector", ev)
            ev1_hist.append(v.inc("c_dv"))
    CD_TRE = v.n["c_dv"]

    if stage < 3:
        return _cut()
    W("tensor", "c_dv", CD_TRE)
    W("tensor", "d_w", 64)
    W("tensor", "d_g", QKB_TGT)

    def qk_eti(eti_list, tmp, kind):
        kn_hist = []
        ev_hist = []
        for idx, eti in enumerate(eti_list):
            if idx >= 2:
                W("tensor", "c_dv", ev_hist[idx - 2])
            else:
                W("tensor", "c_dv", v.n["c_dv"])

            def mms(e, eti=eti, i=idx):
                for dt in range(8):
                    for c in range(4):
                        e.matmul(out=psq[i % 2][:, 512 * c:512 * c + 512],
                                 lhsT=qwT[:, dt, 128 * eti:128 * eti + 128],
                                 rhs=xln1T[:, dt, 512 * c:512 * c + 512],
                                 start=(dt == 0), stop=False)
                for c in range(4):
                    mm = e.matmul(out=psq[i % 2][:, 512 * c:512 * c + 512],
                                  lhsT=qkb_row[0:1, 128 * eti:128 * eti + 128],
                                  rhs=ones_row[0:1, 0:512],
                                  start=False, stop=True)
                    if c == 3:
                        mm.then_inc(sems["c_pe"], 1)
            v.do("tensor", mms)
            pe_m = v.inc("c_pe")
            W("vector", "c_pe", pe_m)
            if kind == "Q" and idx < 2:
                W("vector", "d_vst", 16 * NT)
            if idx >= 2:
                if kind == "K":
                    W("vector", "d_kst", v.n["d_kst"])
                else:
                    W("vector", qst_hist[idx - 2][0], qst_hist[idx - 2][1])
                W("vector", "c_ac", v.n["c_ac"])
            v.do("vector", lambda e, t=tmp, i=idx: e.tensor_copy(
                out=t[:, i % 2, :], in_=psq[i % 2][:, :]).then_inc(sems["c_dv"], 1))
            cd_e = v.inc("c_dv")
            if lite < 2:
                W("scalar", "c_dv", cd_e)
                W("scalar", "c_ac", v.n["c_ac"])

                def sq(e, t=tmp, i=idx):
                    e.activation(out=sq_scr[:, :], in_=t[:, i % 2, :], func=AF.Square,
                                 bias=zero_b[:, 0:1], scale=1.0).then_inc(sems["c_ac"], 1)
                v.do("scalar", sq)
                ac_m = v.inc("c_ac")
            else:
                ac_m = v.n["c_ac"]
            if lite >= 1:
                ev_hist.append(cd_e)
                yield eti, cd_e
                continue
            W("tensor", "c_ac", ac_m)
            W("tensor", "c_dv", cd_e)
            for c in range(4):
                if c >= 2:
                    W("tensor", "c_dv", kn_hist[-2])

                def knm(e, c=c, i=idx):
                    e.matmul(out=psq[i % 2][0:2, 1024 * (c % 2):1024 * (c % 2) + 512],
                             lhsT=hind[:, :],
                             rhs=sq_scr[:, 512 * c:512 * c + 512],
                             start=True, stop=True).then_inc(sems["c_pe"], 1)
                v.do("tensor", knm)
                pe_k = v.inc("c_pe")
                W("vector", "c_pe", pe_k)
                pkn_src = lambda c_, i=idx: psq[i % 2][0:2, 1024 * (c_ % 2):1024 * (c_ % 2) + 512]
                if kind == "K":
                    def red(e, c=c, eti=eti, pkn_src=pkn_src):
                        e.drain()
                        b = c // 2
                        e.tensor_reduce(out=kmt[:, 0:1],
                                        in_=pkn_src(c), axis=AX, op=ALU.max)
                        e.drain()
                        hp = eti - 8
                        col = 2 * hp + b
                        if c % 2 == 0:
                            e.tensor_copy(out=kml[:, col:col + 1],
                                          in_=kmt[:, 0:1]).then_inc(sems["c_dv"], 1)
                        else:
                            e.tensor_tensor(out=kml[:, col:col + 1], in0=kml[:, col:col + 1],
                                            in1=kmt[:, 0:1], op=ALU.max).then_inc(sems["c_dv"], 1)
                        e.drain()
                    v.do("vector", red)
                    kn_hist.append(v.inc("c_dv"))
                else:
                    def cp(e, c=c, pkn_src=pkn_src):
                        e.tensor_copy(out=qn2[:, 512 * c:512 * c + 512],
                                      in_=pkn_src(c)).then_inc(sems["c_dv"], 1)
                    v.do("vector", cp)
                    kn_hist.append(v.inc("c_dv"))
            ev_hist.append(v.n["c_dv"])
            yield eti, cd_e

    k_ev = []
    for eti, cd_e in qk_eti(range(8, 16), ktmp, "K"):
        k_ev.append((eti, cd_e))
        hp = eti - 8
        W("sync", "c_dv", cd_e)
        dma("sync", "d_kst", kvkin[(2 * hp) * 65:(2 * hp) * 65 + 64, :], ktmp[0:64, (eti - 8) % 2, :])
        dma("sync", "d_kst", kvkin[(2 * hp + 1) * 65:(2 * hp + 1) * 65 + 64, :], ktmp[64:128, (eti - 8) % 2, :])
        if hp >= 1:
            pass
    KST_TOTAL = v.n["d_kst"]
    W("sync", "c_dv", v.n["c_dv"])
    dma("sync", "d_kmx", kmax_in[:, :], kml[:])
    if stage < 4:
        return _cut()
    W("gpsimd", "d_kmx", 16)

    def arm(e):
        e.collective_compute("AllReduce", ALU.max, replica_groups=rg,
                             ins=[kmax_in.opt()], outs=[kmax_out.opt()]).then_inc(sems["cc"])
    v.do("gpsimd", arm)
    v.inc("cc")
    W("gpsimd", "d_kst", KST_TOTAL)

    def agk(e):
        e.collective_compute("AllGather", ALU.bypass, replica_groups=rg,
                             ins=[kvkin.opt()], outs=[kvkout.opt()]).then_inc(sems["cc"])
    v.do("gpsimd", agk)
    v.inc("cc")

    if stage < 5:
        return _cut()
    W("tensor", "c_dv", v.n["c_dv"])
    W("tensor", "c_ac", v.n["c_ac"])
    vev_hist = []
    for t in range(NT):
        if t >= 2:
            W("tensor", "c_dv", vev_hist[t - 2])

        def vmm(e, t=t):
            for dt in range(8):
                for c in range(2):
                    e.matmul(out=psq[t % 2][:, 512 * c:512 * c + 512],
                             lhsT=xln1T[:, dt, 128 * t:128 * t + 128],
                             rhs=qwT[:, dt, 2 * D + 512 * c:2 * D + 512 * c + 512],
                             start=(dt == 0), stop=False)
            for c in range(2):
                mm = e.matmul(out=psq[t % 2][:, 512 * c:512 * c + 512],
                              lhsT=ones_row[0:1, 0:128],
                              rhs=qkb_row[0:1, 2 * D + 512 * c:2 * D + 512 * c + 512],
                              start=False, stop=True)
                if c == 1:
                    mm.then_inc(sems["c_pe"], 1)
        v.do("tensor", vmm)
        pe_v = v.inc("c_pe")
        W("vector", "c_pe", pe_v)
        if t >= 2:
            W("vector", "d_vst", v.n["d_vst"])

        def vev(e, t=t):
            e.tensor_copy(out=vtmp[:, t % 2, :], in_=psq[t % 2][:, 0:1024]).then_inc(sems["c_dv"], 1)
        v.do("vector", vev)
        cd_v = v.inc("c_dv")
        vev_hist.append(cd_v)
        W("sync", "c_dv", cd_v)
        dma("sync", "d_vst", kvvin[128 * t:128 * t + 128, :], vtmp[:, t % 2, :])
    W("gpsimd", "d_vst", 16 * NT)

    def agv(e):
        e.collective_compute("AllGather", ALU.bypass, replica_groups=rg,
                             ins=[kvvin.opt()], outs=[kvvout.opt()]).then_inc(sems["cc"])
    v.do("gpsimd", agv)
    v.inc("cc")
    for eI in range(EPC):
        dma("gpsimd", "d_w", wbf[eI, :, :], wsl[eI, :, :])
    W("gpsimd", "cc", 1)
    dma("gpsimd", "d_kmg", kmg[:], kmax_out[:, :])
    if stage < 6:
        return _cut()
    W("scalar", "d_kmg", 16)
    W("scalar", "c_ac", v.n["c_ac"])

    def kngs(e):
        e.activation(out=kng[:], in_=kmg[:], func=AF.Sqrt,
                     bias=zero_b[0:2, 0:1], scale=1.0).then_inc(sems["c_ac"], 1)
    v.do("scalar", kngs)
    AC_KNG = v.inc("c_ac")
    W("vector", "c_ac", AC_KNG)

    def kngn(e):
        e.drain()
        e.tensor_scalar(out=kng[:], in0=kng[:], scalar1=-1.0, scalar2=None, op0=ALU.mult)
        e.drain()
    v.do("vector", kngn)

    if stage < 7:
        return _cut()
    qst_hist = []
    for eti, cd_e in qk_eti(range(0, 8), qtmp, "Q"):
        hp = eti
        W("scalar", "c_dv", v.n["c_dv"])
        W("scalar", "c_ac", v.n["c_ac"])

        def m8sq(e):
            e.activation(out=qnw[0:2, :], in_=qn2[:, 0:MLOC], func=AF.Sqrt,
                         bias=zero_b[0:2, 0:1], scale=1.0)
            e.activation(out=qns[0:2, :], in_=qn2[:, MLOC:TLOC], func=AF.Sqrt,
                         bias=zero_b[0:2, 0:1], scale=1.0).then_inc(sems["c_ac"], 1)
        v.do("scalar", m8sq)
        ac_m8 = v.inc("c_ac")
        W("vector", "c_ac", ac_m8)

        def m8chain(e, hp=hp):
            e.drain()
            e.tensor_scalar(out=m8s[:, hp % 2, 0:MLOC], in0=qnw[0:2, :],
                            scalar1=kng[:, 2 * hp:2 * hp + 1], scalar2=SHIFT,
                            op0=ALU.mult, op1=ALU.add)
            e.tensor_scalar(out=m8s[:, hp % 2, MLOC:TLOC], in0=qns[0:2, :],
                            scalar1=kng[:, 2 * hp + 1:2 * hp + 2], scalar2=SHIFT,
                            op0=ALU.mult, op1=ALU.add).then_inc(sems["c_dv"], 1)
            e.drain()
        v.do("vector", m8chain)
        cd_m8 = v.inc("c_dv")
        W("sync", "c_dv", cd_m8)
        qsem2 = "d_qst" if hp % 2 == 0 else "d_qs1"
        dma("sync", qsem2, q_d[128 * hp:128 * hp + 128, :], qtmp[:, hp % 2, :])
        qst = dma("sync", qsem2, m8_d[2 * hp:2 * hp + 2, :], m8s[:, hp % 2, :])
        qst_hist.append((qsem2, qst))
        if hp >= 1:
            pass

    if debug and stage >= 7 and phases == 1:
        W("sync", "c_dv", v.n["c_dv"])
        W("sync", "d_qst", v.n["d_qst"])
        W("sync", "cc", 3)
        v.do("gpsimd", lambda e, x=v.n["c_dv"]: e.wait_ge(sems["c_dv"], x))
        dma("gpsimd", "d_dbg", dbg["dbg_xln1T"][:, :], xln1T[:, 0, :])
        dma("gpsimd", "d_dbg", dbg["dbg_q"][:, :], q_d[0:128, :])
        dma("gpsimd", "d_dbg", dbg["dbg_m8"][:, :], m8_d[0:2, :])
        dma("gpsimd", "d_dbg", dbg["dbg_k"][:, :], kvkout[0:128, :])
        W("gpsimd", "d_dbg", v.n["d_dbg"])

    barrier_all(1)
    v.flush(nc)
    p1.close()

    if phases < 2:
        es.close()
        return nc, dbg

    p2 = ExitStack()
    kbufA = [p2.enter_context(nc.sbuf_tensor(f"kbufA{i}", [65, 8192], BF16)) for i in range(2)]
    kbufB = [p2.enter_context(nc.sbuf_tensor(f"kbufB{i}", [65, 8192], BF16)) for i in range(2)]
    vbufA = [p2.enter_context(nc.sbuf_tensor(f"vbufA{i}", [128, 65 * 64], BF16)) for i in range(2)]
    vbufB = [p2.enter_context(nc.sbuf_tensor(f"vbufB{i}", [128, 65 * 64], BF16)) for i in range(2)]
    qbufA = [p2.enter_context(nc.sbuf_tensor(f"qbufA{i}", [65, 512], BF16)) for i in range(2)]
    qbufB = [p2.enter_context(nc.sbuf_tensor(f"qbufB{i}", [65, 512], BF16)) for i in range(2)]
    pt = [p2.enter_context(nc.sbuf_tensor(f"pt{i}", [128, 1024], BF16)) for i in range(4)]
    uA = p2.enter_context(nc.sbuf_tensor("uA", [64, 512], F32))
    uB = p2.enter_context(nc.sbuf_tensor("uB", [64, 512], F32))
    s65 = p2.enter_context(nc.sbuf_tensor("s65", [65, 1024], F32))
    sbcA = p2.enter_context(nc.sbuf_tensor("sbcA", [64, 512], F32))
    sbcB = p2.enter_context(nc.sbuf_tensor("sbcB", [64, 512], F32))
    astA = p2.enter_context(nc.sbuf_tensor("astA", [64, 512], BF16))
    astB = p2.enter_context(nc.sbuf_tensor("astB", [64, 512], BF16))
    psum_s = [p2.enter_context(nc.psum_tensor(f"psum_s{i}", [128, 1024], F32)) for i in range(2)]
    pavA = [p2.enter_context(nc.psum_tensor(f"pavA{i}", [128, 512], F32)) for i in range(2)]
    pavB = [p2.enter_context(nc.psum_tensor(f"pavB{i}", [128, 512], F32)) for i in range(2)]

    kvk_r = kvkout.rearrange("(r h j2) c -> h j2 r c", h=H, j2=65)
    kvv_r = kvvout.rearrange("(r bb i) d -> bb r i d", bb=2, i=MLOC)

    for s_ in ("d_kst", "d_vst", "d_qst", "d_qs1"):
        W("vector", s_, v.n[s_])

    def vones(e):
        for buf in (vbufA[0], vbufA[1], vbufB[0], vbufB[1]):
            e.memset(buf[:].rearrange("p (k c) -> p k c", c=65)[:, :, 64:65], 1.0).then_inc(sems["c_dv"], 1)
    v.do("vector", vones)
    CD_VONES = v.inc("c_dv", 4)

    NUNIT = 32
    W("sync", "cc", 3)
    for s_ in ("d_kst", "d_vst", "d_qst", "d_qs1", "d_kmx"):
        W("sync", s_, v.n[s_])
    for j in range(NUNIT):
        u, hp = divmod(j, 8)
        b, qbi = divmod(u, 2)
        colbase = b * MLOC + qbi * 512
        par = j % 2
        kldsem = "d_kld" if par == 0 else "d_kld"
        ksem = "d_kld" if par == 0 else "d_kld1"
        vsem = "d_vld" if par == 0 else "d_vld1"
        qsem = "d_qld" if par == 0 else "d_qld1"
        if j >= 2:
            W("sync", "c_av", 64 * (j - 1))
        dma("sync", ksem, kbufA[par][:].rearrange("p (r c) -> p r c", r=8),
            kvk_r[2 * hp][:, :, b * MLOC:(b + 1) * MLOC])
        dma("sync", ksem, kbufB[par][:].rearrange("p (r c) -> p r c", r=8),
            kvk_r[2 * hp + 1][:, :, b * MLOC:(b + 1) * MLOC])
        for c3 in range(8):
            src = kvv_r[b][c3].rearrange("(kt p) d -> p kt d", p=128)
            srcA = src[:, :, 128 * hp:128 * hp + 64]
            srcB = src[:, :, 128 * hp + 64:128 * hp + 128]
            dstA = vbufA[par][:].rearrange("p (k c) -> p k c", c=65)[:, 8 * c3:8 * c3 + 8, 0:64]
            dstB = vbufB[par][:].rearrange("p (k c) -> p k c", c=65)[:, 8 * c3:8 * c3 + 8, 0:64]
            dma("sync", vsem, dstA, srcA)
            dma("sync", vsem, dstB, srcB)
        dma("sync", qsem, qbufA[par][0:64, :], q_d[128 * hp:128 * hp + 64, colbase:colbase + 512])
        dma("sync", qsem, qbufA[par][64:65, :], m8_d[2 * hp:2 * hp + 1, colbase:colbase + 512])
        dma("sync", qsem, qbufB[par][0:64, :], q_d[128 * hp + 64:128 * hp + 128, colbase:colbase + 512])
        dma("sync", qsem, qbufB[par][64:65, :], m8_d[2 * hp + 1:2 * hp + 2, colbase:colbase + 512])
        kld_tgt = (ksem, v.n[ksem])
        vld_tgt = (vsem, v.n[vsem])
        qld_tgt = (qsem, v.n[qsem])

        W("tensor", kld_tgt[0], kld_tgt[1])
        W("tensor", qld_tgt[0], qld_tgt[1])

        def emit_scores(kt):
            I = 64 * j + kt
            if I >= 2:
                W("tensor", "c_ex", I - 1)

            def sc(e, par=par, kt=kt, I=I):
                e.matmul(out=psum_s[I % 2][:, 0:512],
                         lhsT=kbufA[par][:, 128 * kt:128 * kt + 128],
                         rhs=qbufA[par][:, :], start=True, stop=True)
                e.matmul(out=psum_s[I % 2][:, 512:1024],
                         lhsT=kbufB[par][:, 128 * kt:128 * kt + 128],
                         rhs=qbufB[par][:, :], start=True,
                         stop=True).then_inc(sems["c_sc"], 1)
            v.do("tensor", sc)
            v.inc("c_sc")
            W("scalar", "c_sc", I + 1)
            if I >= 4:
                W("scalar", "c_av", I - 3)

            def ex(e, kt=kt, I=I):
                e.activation(out=pt[I % 4][:, :], in_=psum_s[I % 2][:, :], func=AF.Exp,
                             bias=zero_b[:, 0:1], scale=0.125).then_inc(sems["c_ex"], 1)
            v.do("scalar", ex)
            v.inc("c_ex")

        def emit_attnv(kt):
            I = 64 * j + kt
            W("tensor", "c_ex", I + 1)
            if kt == 0:
                W("tensor", vld_tgt[0], vld_tgt[1])
                if j == 0:
                    W("tensor", "c_dv", CD_VONES)
                if j >= 2:
                    W("tensor", "c_nr", j - 1)

            def av(e, par=par, kt=kt, I=I, j=j):
                e.matmul(out=pavA[j % 2][0:65, :], lhsT=vbufA[par][:, 65 * kt:65 * kt + 65],
                         rhs=pt[I % 4][:, 0:512], start=(kt == 0), stop=(kt == 63))
                e.matmul(out=pavB[j % 2][0:65, :], lhsT=vbufB[par][:, 65 * kt:65 * kt + 65],
                         rhs=pt[I % 4][:, 512:1024], start=(kt == 0),
                         stop=(kt == 63)).then_inc(sems["c_av"], 1)
            v.do("tensor", av)
            v.inc("c_av")

        emit_scores(0)
        for kt in range(1, NKT):
            emit_scores(kt)
            emit_attnv(kt - 1)
        emit_attnv(NKT - 1)

        W("vector", "c_av", 64 * (j + 1))
        if j >= 1:
            W("vector", "d_at", 32 * j)

        def nrm1(e, j=j):
            e.tensor_scalar(out=s65[64:65, 0:512], in0=pavA[j % 2][64:65, :], scalar1=1e-30,
                            scalar2=None, op0=ALU.add)
            e.tensor_scalar(out=s65[64:65, 512:1024], in0=pavB[j % 2][64:65, :], scalar1=1e-30,
                            scalar2=None, op0=ALU.add).then_inc(sems["c_rc"], 1)
            e.tensor_copy(out=uA[:], in_=pavA[j % 2][0:64, :])
            e.tensor_copy(out=uB[:], in_=pavB[j % 2][0:64, :]).then_inc(sems["c_nr"], 1)
        v.do("vector", nrm1)
        v.inc("c_rc")
        v.inc("c_nr")
        W("gpsimd", "c_rc", v.n["c_rc"])

        def bc(e, n0=v.n["d_bc"]):
            e.dma_start(out=bc_d[:, :], in_=s65[64:65, :]).then_inc(sems["d_bc"], 16)
            e.wait_ge(sems["d_bc"], n0 + 16)
            e.dma_start(out=sbcA[:], in_=bc_d[0:1, 0:512].broadcast_to([64, 512])).then_inc(sems["d_bc"], 16)
            e.dma_start(out=sbcB[:], in_=bc_d[0:1, 512:1024].broadcast_to([64, 512])).then_inc(sems["d_bc"], 16)
        v.do("gpsimd", bc)
        v.inc("d_bc", 48)
        W("vector", "d_bc", v.n["d_bc"])

        def nrm2(e):
            e.drain()
            e.reciprocal(out=sbcA[:], in_=sbcA[:])
            e.reciprocal(out=sbcB[:], in_=sbcB[:])
            e.drain()
            e.tensor_tensor(out=astA[:], in0=uA[:], in1=sbcA[:], op=ALU.mult).then_inc(sems["c_mul"], 1)
            e.tensor_tensor(out=astB[:], in0=uB[:], in1=sbcB[:], op=ALU.mult).then_inc(sems["c_mul"], 1)
        v.do("vector", nrm2)
        v.inc("c_mul", 2)
        W("gpsimd", "c_mul", v.n["c_mul"])
        dma("gpsimd", "d_at", attn_d[b, 128 * hp:128 * hp + 64, qbi * 512:qbi * 512 + 512], astA[:])
        dma("gpsimd", "d_at", attn_d[b, 128 * hp + 64:128 * hp + 128, qbi * 512:qbi * 512 + 512], astB[:])

    if debug and phases == 2:
        v.do("gpsimd", lambda e, x=v.n["d_at"]: e.wait_ge(sems["d_at"], x))
        dma("gpsimd", "d_dbg", dbg["dbg_attn"][:, :], attn_d[0, 0:128, :])
        W("gpsimd", "d_dbg", v.n["d_dbg"])

    barrier_all(2)
    v.flush(nc)
    p2.close()

    if phases < 3:
        es.close()
        return nc, dbg

    p36 = ExitStack()
    xln3T = p36.enter_context(nc.sbuf_tensor("xln3T", [128, 8, TLOC], BF16))
    phi_sb = p36.enter_context(nc.sbuf_tensor("phi_sb", [128, 8, NP], BF16))
    p34 = ExitStack()
    xln3 = p34.enter_context(nc.sbuf_tensor("xln3", [128, NT, D], BF16))
    p3 = ExitStack()
    attn_t = p3.enter_context(nc.sbuf_tensor("attn_t", [128, 2, 8, 128], BF16))
    pwT = p3.enter_context(nc.sbuf_tensor("pwT", [128, 8, D], BF16))
    prow = p3.enter_context(nc.sbuf_tensor("prow", [1, D], BF16))
    o_bf = p3.enter_context(nc.sbuf_tensor("o_bf", [128, 8, D], BF16))
    scrA = p3.enter_context(nc.sbuf_tensor("scrA", [128, D], F32))
    scrB_ap = sq_scr[:].bitcast(F32)
    ppo = [p3.enter_context(nc.psum_tensor(f"ppo{i}", [128, D], F32)) for i in range(2)]
    ptr2 = [p3.enter_context(nc.psum_tensor(f"ptr2{i}", [128, 512], BF16)) for i in range(2)]

    def lnstats_sbuf(src_ap_fn, ntiles, base_col=0):
        for t in range(ntiles):
            v.do("vector", lambda e, t=t: e.tensor_reduce(
                out=stat[:, base_col + t:base_col + t + 1], in_=src_ap_fn(t), axis=AX, op=ALU.add))
            W("scalar", "c_ac", v.n["c_ac"])

            def sqf(e, t=t):
                e.activation(out=sq_scr[:, 0:D], in_=src_ap_fn(t), func=AF.Square,
                             bias=zero_b[:, 0:1], scale=1.0,
                             accum_out=stat2[:, base_col + t:base_col + t + 1]).then_inc(sems["c_ac"], 1)
            v.do("scalar", sqf)
            v.inc("c_ac")

    def ln_rstd(n, c0=0):
        def f(e):
            sl = slice(c0, c0 + n)
            e.drain()
            e.tensor_scalar(out=mean_t[:, sl], in0=stat[:, sl], scalar1=1.0 / D,
                            scalar2=None, op0=ALU.mult)
            e.drain()
            e.tensor_tensor(out=var_t[:, sl], in0=mean_t[:, sl], in1=mean_t[:, sl], op=ALU.mult)
            e.drain()
            e.tensor_scalar(out=nsc1[:, sl], in0=stat2[:, sl], scalar1=1.0 / D,
                            scalar2=None, op0=ALU.mult)
            e.drain()
            e.tensor_tensor(out=var_t[:, sl], in0=nsc1[:, sl], in1=var_t[:, sl], op=ALU.subtract)
            e.drain()
            e.tensor_scalar(out=var_t[:, sl], in0=var_t[:, sl], scalar1=EPS, scalar2=None, op0=ALU.add)
            e.drain()
            e.tensor_scalar(out=nsi1[:, sl], in0=var_t[:, sl].bitcast(I32), scalar1=1,
                            scalar2=None, op0=ALU.arith_shift_right)
            e.drain()
            e.tensor_scalar(out=nsi2[:, sl], in0=nsi1[:, sl], scalar1=-1, scalar2=int(MAGIC),
                            op0=ALU.mult, op1=ALU.add)
            e.drain()
            e.tensor_copy(out=rstd_t[:, sl].bitcast(I32), in_=nsi2[:, sl])
            e.drain()
            for _ in range(4):
                e.tensor_tensor(out=nsc1[:, sl], in0=rstd_t[:, sl], in1=rstd_t[:, sl], op=ALU.mult)
                e.drain()
                e.tensor_tensor(out=nsc1[:, sl], in0=nsc1[:, sl], in1=var_t[:, sl], op=ALU.mult)
                e.drain()
                e.tensor_scalar(out=nsc1[:, sl], in0=nsc1[:, sl], scalar1=-0.5, scalar2=1.5,
                                op0=ALU.mult, op1=ALU.add)
                e.drain()
                e.tensor_tensor(out=rstd_t[:, sl], in0=rstd_t[:, sl], in1=nsc1[:, sl], op=ALU.mult)
                e.drain()
        v.do("vector", f)

    dma("gpsimd", "d_pw", pwT[:], proj_wT_d.rearrange("(dt p) e -> p dt e", p=128))
    PWT_TGT = v.n["d_pw"]
    for i in range(2):
        dma("gpsimd", "d_ph2", phi_sb[:, 4 * i:4 * i + 4, :],
            phi_d.rearrange("(dt p) n -> p dt n", p=128)[:, 4 * i:4 * i + 4, :])
    PHI_TGT = v.n["d_ph2"]
    dma("gpsimd", "d_g", prow[:], projb_d[:, :])
    PROW_TGT = v.n["d_g"]
    W("sync", "d_at", v.n["d_at"])

    pmm_hist = []
    evac3_hist = []
    for b in range(B):
        if b == 0:
            W("tensor", "d_pw", PWT_TGT)
            W("tensor", "d_g", PROW_TGT)
        for ti8 in range(8):
            t_glob = 8 * b + ti8
            n3 = 8 * b + ti8
            psem = "d_p0" if n3 % 2 == 0 else "d_p1"
            if n3 >= 2:
                W("sync", "c_pe", pmm_hist[n3 - 2])
            dma("sync", psem, attn_t[:, ti8 % 2],
                attn_d[b].rearrange("(dt p) q -> p dt q", p=128)[:, :, 128 * ti8:128 * ti8 + 128])
            W("tensor", psem, v.n[psem])
            if n3 >= 2:
                W("tensor", "c_dv", evac3_hist[n3 - 2])

            def pmm(e, ti8=ti8):
                for c in range(2):
                    for dt in range(8):
                        e.matmul(out=ppo[ti8 % 2][:, 512 * c:512 * c + 512],
                                 lhsT=attn_t[:, ti8 % 2, dt, :],
                                 rhs=pwT[:, dt, 512 * c:512 * c + 512],
                                 start=(dt == 0), stop=False)
                    mm = e.matmul(out=ppo[ti8 % 2][:, 512 * c:512 * c + 512],
                                  lhsT=ones_row[0:1, 0:128],
                                  rhs=prow[0:1, 512 * c:512 * c + 512],
                                  start=False, stop=True)
                    if c == 1:
                        mm.then_inc(sems["c_pe"], 1)
            v.do("tensor", pmm)
            pe_p = v.inc("c_pe")
            pmm_hist.append(pe_p)
            W("vector", "c_pe", pe_p)
            v.do("vector", lambda e, ti8=ti8: e.tensor_copy(
                out=o_bf[:, ti8, :], in_=ppo[ti8 % 2][:, :]).then_inc(sems["c_dv"], 1))
            evac3_hist.append(v.inc("c_dv"))
        CD_EV = v.n["c_dv"]
        W("vector", "c_dv", CD_EV)
        W("scalar", "c_dv", CD_EV)
        lnstats_sbuf(lambda t: o_bf[:, t, :], 8)
        AC_LN2 = v.n["c_ac"]
        W("vector", "c_ac", AC_LN2)
        emit_rstd_sqrt(8)
        for pair in range(4):
            def appl(e, pair=pair, b=b):
                for k2 in range(2):
                    t = 2 * pair + k2
                    e.tensor_scalar(out=(scrA[:] if k2 == 0 else scrB_ap),
                                    in0=o_bf[:, t, :], scalar1=mean_t[:, t:t + 1],
                                    scalar2=rstd_t[:, t:t + 1], op0=ALU.subtract, op1=ALU.mult)
                e.drain()
                for k2 in range(2):
                    t = 2 * pair + k2
                    mm = e.tensor_tensor(out=Rt[:, 8 * b + t, :], in0=Rt[:, 8 * b + t, :],
                                         in1=(scrA[:] if k2 == 0 else scrB_ap), op=ALU.add)
                    if k2 == 1:
                        mm.then_inc(sems["c_dv"], 1)
                e.drain()
            v.do("vector", appl)
            v.inc("c_dv")
        W("scalar", "c_dv", v.n["c_dv"])
        lnstats_sbuf(lambda t, b=b: Rt[:, 8 * b + t, :], 8)
        W("vector", "c_ac", v.n["c_ac"])
        emit_rstd_sqrt(8)
        for ti8 in range(8):
            t_glob = 8 * b + ti8
            if ti8 >= 1:
                W("vector", "c_pe", v.n["c_pe"])
            v.do("vector", lambda e, ti8=ti8, t_glob=t_glob: e.tensor_scalar(
                out=xln3[:, t_glob, :], in0=Rt[:, t_glob, :], scalar1=mean_t[:, ti8:ti8 + 1],
                scalar2=rstd_t[:, ti8:ti8 + 1], op0=ALU.subtract,
                op1=ALU.mult).then_inc(sems["c_dv"], 1))
            cd_x3 = v.inc("c_dv")
            W("tensor", "c_dv", cd_x3)
            for dh in range(2):
                W("tensor", "c_dv", v.n["c_dv"])

                def tr3b(e, t_glob=t_glob, dh=dh, p=ptr2[dh]):
                    for jj in range(4):
                        dt = 4 * dh + jj
                        mm = e.transpose(out=p[:, 128 * jj:128 * jj + 128],
                                         in_=xln3[:, t_glob, 128 * dt:128 * dt + 128],
                                         identity=ident[:])
                        if jj == 3:
                            mm.then_inc(sems["c_pe"], 1)
                v.do("tensor", tr3b)
                pe_t3 = v.inc("c_pe")
                W("vector", "c_pe", pe_t3)

                def ev3(e, t_glob=t_glob, dh=dh, p=ptr2[dh]):
                    e.tensor_copy(
                        out=xln3T[:, 4 * dh:4 * dh + 4, 128 * t_glob:128 * t_glob + 128],
                        in_=p[:].rearrange("p (dt c) -> p dt c", c=128),
                    ).then_inc(sems["c_dv"], 1)
                v.do("vector", ev3)
                v.inc("c_dv")

    if debug and phases == 3:
        v.do("gpsimd", lambda e, x=v.n["c_dv"]: e.wait_ge(sems["c_dv"], x))
        dma("gpsimd", "d_dbg", dbg["dbg_xln3"][:, :], xln3[:, 0, :])
        W("gpsimd", "d_dbg", v.n["d_dbg"])

    barrier_all(3)
    v.flush(nc)
    p3.close()

    if phases < 4:
        p34.close()
        p36.close()
        es.close()
        return nc, dbg

    p4 = ExitStack()
    logits_q = p4.enter_context(nc.sbuf_tensor("logits_q", [128, 16, 512], BF16))
    disp_q = p4.enter_context(nc.sbuf_tensor("disp_q", [128, 16, 512], BF16))
    dsc = p4.enter_context(nc.sbuf_tensor("dsc", [128, 512], BF16))
    th = p4.enter_context(nc.sbuf_tensor("th", [128, 512], BF16))
    slscr = p4.enter_context(nc.sbuf_tensor("slscr", [128, D], F32))
    ppl = [p4.enter_context(nc.psum_tensor(f"ppl{i}", [128, 512], F32)) for i in range(2)]
    psl = [p4.enter_context(nc.psum_tensor(f"psl{i}", [128, D], F32)) for i in range(2)]

    W("tensor", "d_ph2", PHI_TGT)

    lev_hist = []
    sev_hist = []
    for qtr in range(4):
        for ti in range(NT):
            nl = 16 * qtr + ti
            if nl >= 2:
                W("tensor", "c_dv", lev_hist[nl - 2])

            def lmm(e, ti=ti, qtr=qtr):
                for dt in range(8):
                    mm = e.matmul(out=ppl[ti % 2][:, :],
                                  lhsT=xln3T[:, dt, 128 * ti:128 * ti + 128],
                                  rhs=phi_sb[:, dt, 512 * qtr:512 * qtr + 512],
                                  start=(dt == 0), stop=(dt == 7))
                    if dt == 7:
                        mm.then_inc(sems["c_pe"], 1)
            v.do("tensor", lmm)
            pe_l = v.inc("c_pe")
            W("vector", "c_pe", pe_l)
            v.do("vector", lambda e, ti=ti: e.tensor_copy(
                out=logits_q[:, ti, :], in_=ppl[ti % 2][:, :]).then_inc(sems["c_dv"], 1))
            lev_hist.append(v.inc("c_dv"))
        CD_LQ = v.n["c_dv"]
        for tp in range(8):
            def dsp1(e, tp=tp):
                e.drain()
                e.tensor_tensor(out=dsc[:], in0=logits_q[:, tp, :],
                                in1=logits_q[:, tp + 8, :],
                                op=ALU.subtract).then_inc(sems["c_dv"], 1)
            v.do("vector", dsp1)
            cd_d = v.inc("c_dv")
            W("scalar", "c_dv", cd_d)
            W("scalar", "c_ac", v.n["c_ac"])

            def dsp2(e):
                e.activation(out=th[:], in_=dsc[:], func=AF.Tanh,
                             bias=zero_b[:, 0:1], scale=0.5).then_inc(sems["c_ac"], 1)
            v.do("scalar", dsp2)
            ac_t = v.inc("c_ac")
            W("vector", "c_ac", ac_t)

            def dsp3(e, tp=tp):
                e.tensor_scalar(out=disp_q[:, tp, :], in0=th[:], scalar1=1.0,
                                scalar2=None, op0=ALU.add)
                e.tensor_scalar(out=disp_q[:, tp + 8, :], in0=th[:], scalar1=-1.0,
                                scalar2=1.0, op0=ALU.mult,
                                op1=ALU.add).then_inc(sems["c_dv"], 1)
            v.do("vector", dsp3)
            v.inc("c_dv")
        CD_DISP = v.n["c_dv"]
        W("tensor", "c_dv", CD_DISP)
        for b in range(B):
            for nlt in range(4):
                ns4 = 8 * qtr + 4 * b + nlt
                if ns4 >= 2:
                    W("tensor", "c_dv", sev_hist[ns4 - 2])

                def smm(e, b=b, nlt=nlt):
                    for c in range(2):
                        for mt in range(8):
                            mm = e.matmul(out=psl[nlt % 2][:, 512 * c:512 * c + 512],
                                          lhsT=disp_q[:, 8 * b + mt, 128 * nlt:128 * nlt + 128],
                                          rhs=xln3[:, 8 * b + mt, 512 * c:512 * c + 512],
                                          start=(mt == 0), stop=(mt == 7))
                            if c == 1 and mt == 7:
                                mm.then_inc(sems["c_pe"], 1)
                v.do("tensor", smm)
                pe_s = v.inc("c_pe")
                W("vector", "c_pe", pe_s)
                W("vector", "d_sl", v.n["d_sl"])
                v.do("vector", lambda e, nlt=nlt: e.tensor_copy(
                    out=slscr[:], in_=psl[nlt % 2][:, :]).then_inc(sems["c_dv"], 1))
                cd_s = v.inc("c_dv")
                sev_hist.append(cd_s)
                W("sync", "c_dv", cd_s)
                npg = 4 * qtr + nlt
                for eg in range(4):
                    row0 = 64 * (4 * npg + eg) + 32 * b
                    dma("sync", "d_sl", slots_in[row0:row0 + 32, :],
                        slscr[32 * eg:32 * eg + 32, :])
    W("gpsimd", "d_sl", v.n["d_sl"])

    def rs(e):
        e.collective_compute("ReduceScatter", ALU.add, replica_groups=rg,
                             ins=[slots_in[0:NP, :].opt()],
                             outs=[slots_out[0:NP // NCORE, :].opt()]).then_inc(sems["cc"])
        e.collective_compute("ReduceScatter", ALU.add, replica_groups=rg,
                             ins=[slots_in[NP:2 * NP, :].opt()],
                             outs=[slots_out[NP // NCORE:2 * NP // NCORE, :].opt()]).then_inc(sems["cc"])
    v.do("gpsimd", rs)
    v.inc("cc", 2)

    if debug and phases == 4:
        v.do("gpsimd", lambda e, x=v.n["cc"]: e.wait_ge(sems["cc"], x))
        dma("gpsimd", "d_dbg", dbg["dbg_disp"][:, :], slots_out[0:128, :])
        W("gpsimd", "d_dbg", v.n["d_dbg"])

    barrier_all(4)
    v.flush(nc)
    p4.close()
    p34.close()

    if phases < 5:
        p36.close()
        es.close()
        return nc, dbg

    p5 = ExitStack()
    slots_sb = p5.enter_context(nc.sbuf_tensor("slots_sb", [128, 4, D], BF16))
    slotsT = p5.enter_context(nc.sbuf_tensor("slotsT", [128, 8, 512], BF16))
    wbuf = [p5.enter_context(nc.sbuf_tensor(f"wbuf{i}", [128, 8, D], BF16)) for i in range(2)]
    b2row = p5.enter_context(nc.sbuf_tensor("b2row", [1, EPC * D], BF16))
    ldb = p5.enter_context(nc.sbuf_tensor("ldb", [1, 2048], F32))
    ytmp = p5.enter_context(nc.sbuf_tensor("ytmp", [64, D], BF16))
    ptr5 = [p5.enter_context(nc.psum_tensor(f"ptr5{i}", [128, 512], BF16)) for i in range(2)]
    py = [p5.enter_context(nc.psum_tensor(f"py{i}", [64, D], F32)) for i in range(2)]

    W("gpsimd", "cc", 4)
    dma("gpsimd", "d_g", slots_sb[:, 0:2, :], slots_out[0:256, :].rearrange("(r p) d -> p r d", p=128))
    SLO_TGT1 = v.n["d_g"]
    W("gpsimd", "cc", 5)
    dma("gpsimd", "d_g2", slots_sb[:, 2:4, :], slots_out[256:512, :].rearrange("(r p) d -> p r d", p=128))
    SLO_TGT = v.n["d_g2"]
    for ch in range(4):
        if ch > 0:
            W("sync", "c_dv", v.n["c_dv"])
        dma("sync", "d_m", ldb[:], bsl[:, 2048 * ch:2048 * ch + 2048])
        W("vector", "d_m", v.n["d_m"])
        v.do("vector", lambda e, ch=ch: e.tensor_scalar(
            out=b2row[0:1, 2048 * ch:2048 * ch + 2048], in0=ldb[:], scalar1=2.0,
            scalar2=None, op0=ALU.mult).then_inc(sems["c_dv"], 1))
        v.inc("c_dv")
    CD_B2 = v.n["c_dv"]
    W("tensor", "d_g", SLO_TGT1)
    for r4 in range(4):
        if r4 == 2:
            W("tensor", "d_g2", SLO_TGT)
        for dh in range(2):
            W("tensor", "c_dv", v.n["c_dv"])

            def tr5(e, r4=r4, dh=dh, p=ptr5[dh]):
                for jj in range(4):
                    dt = 4 * dh + jj
                    mm = e.transpose(out=p[:, 128 * jj:128 * jj + 128],
                                     in_=slots_sb[:, r4, 128 * dt:128 * dt + 128],
                                     identity=ident[:])
                    if jj == 3:
                        mm.then_inc(sems["c_pe"], 1)
            v.do("tensor", tr5)
            pe_t5 = v.inc("c_pe")
            W("vector", "c_pe", pe_t5)

            def ev5(e, r4=r4, dh=dh, p=ptr5[dh]):
                e.tensor_copy(
                    out=slotsT[:, 4 * dh:4 * dh + 4, 128 * r4:128 * r4 + 128],
                    in_=p[:].rearrange("p (dt c) -> p dt c", c=128),
                ).then_inc(sems["c_dv"], 1)
            v.do("vector", ev5)
            v.inc("c_dv")
    CD_SLT = v.n["c_dv"]
    W("gpsimd", "d_w", v.n["d_w"])
    W("tensor", "c_dv", CD_SLT)
    W("tensor", "c_dv", CD_B2)
    for eI in range(EPC):
        wsem = "d_w0" if eI % 2 == 0 else "d_w1"
        if eI >= 2:
            W("gpsimd", "c_pe", v.n["c_pe"])
        dma("gpsimd", wsem, wbuf[eI % 2][:], wbf[eI].rearrange("(dt p) d2 -> p dt d2", p=128))
        W("tensor", wsem, v.n[wsem])
        if eI >= 2:
            W("tensor", "c_dv", v.n["c_dv"])

        def ymm(e, eI=eI):
            for c in range(2):
                for dt in range(8):
                    e.matmul(out=py[eI % 2][:, 512 * c:512 * c + 512],
                             lhsT=slotsT[:, dt, 64 * eI:64 * eI + 64],
                             rhs=wbuf[eI % 2][:, dt, 512 * c:512 * c + 512],
                             start=(dt == 0), stop=False)
                mm = e.matmul(out=py[eI % 2][:, 512 * c:512 * c + 512],
                              lhsT=ones_row[0:1, 0:64],
                              rhs=b2row[0:1, 1024 * eI + 512 * c:1024 * eI + 512 * c + 512],
                              start=False, stop=True)
                if c == 1:
                    mm.then_inc(sems["c_pe"], 1)
        v.do("tensor", ymm)
        pe_y = v.inc("c_pe")
        W("vector", "c_pe", pe_y)
        W("vector", "d_y", v.n["d_y"])

        def yev(e, eI=eI):
            e.tensor_copy(out=ytmp[:], in_=py[eI % 2][:, :]).then_inc(sems["c_dv"], 1)
        v.do("vector", yev)
        cd_y = v.inc("c_dv")
        W("sync", "c_dv", cd_y)
        dma("sync", "d_y", y_in[64 * eI:64 * eI + 64, :], ytmp[:])
        if eI == 3 or eI == 7:
            W("gpsimd", "d_y", v.n["d_y"])
            lo = 0 if eI == 3 else 256

            def agy(e, lo=lo):
                e.collective_compute("AllGather", ALU.bypass, replica_groups=rg,
                                     ins=[y_in[lo:lo + 256, :].opt()],
                                     outs=[y_ag[8 * lo:8 * lo + 2048, :].opt()]).then_inc(sems["cc"])
            v.do("gpsimd", agy)
            v.inc("cc")

    if debug and phases == 5:
        v.do("gpsimd", lambda e, x=v.n["cc"]: e.wait_ge(sems["cc"], x))
        dma("gpsimd", "d_dbg", dbg["dbg_y"][:, :], y_ag[0:128, :])
        W("gpsimd", "d_dbg", v.n["d_dbg"])

    barrier_all(5)
    v.flush(nc)
    p5.close()

    if phases < 6:
        p36.close()
        es.close()
        return nc, dbg

    p6 = ExitStack()
    y_b = p6.enter_context(nc.sbuf_tensor("y_b", [128, 16, D], BF16))
    ct_sb = p6.enter_context(nc.sbuf_tensor("ct_sb", [128, 16, D], BF16))
    co_f32 = p6.enter_context(nc.sbuf_tensor("co_f32", [128, 2, D], F32))
    scr4 = sq_scr[:].bitcast(F32)
    pct = [p6.enter_context(nc.psum_tensor(f"pct{i}", [128, D], F32)) for i in range(2)]
    pco = [p6.enter_context(nc.psum_tensor(f"pco{i}", [128, D], F32)) for i in range(2)]

    cte_hist = []
    cev_hist = []
    for b in range(B):
        W("gpsimd", "cc", 7)
        if b == 1:
            W("gpsimd", "c_pe", v.n["c_pe"])
        for g in range(16):
            for eg in range(4):
                e_glob = 4 * g + eg
                if e_glob < 32:
                    r_, l_ = e_glob // 4, e_glob % 4
                    row0 = 256 * r_ + 64 * l_ + 32 * b
                else:
                    r_, l_ = (e_glob - 32) // 4, (e_glob - 32) % 4
                    row0 = 2048 + 256 * r_ + 64 * l_ + 32 * b
                dma("gpsimd", "d_yb", y_b[32 * eg:32 * eg + 32, g, :],
                    y_ag[row0:row0 + 32, :])
        YB_TGT = v.n["d_yb"]
        for npt in range(NT):
            nc6 = 16 * b + npt
            if nc6 >= 2:
                W("tensor", "c_ac", cte_hist[nc6 - 2])

            def ctm(e, npt=npt, b=b):
                for c in range(2):
                    for dt in range(8):
                        mm = e.matmul(out=pct[npt % 2][:, 512 * c:512 * c + 512],
                                      lhsT=phi_sb[:, dt, 128 * npt:128 * npt + 128],
                                      rhs=xln3T[:, dt, MLOC * b + 512 * c:MLOC * b + 512 * c + 512],
                                      start=(dt == 0), stop=(dt == 7))
                        if c == 1 and dt == 7:
                            mm.then_inc(sems["c_pe"], 1)
            v.do("tensor", ctm)
            pe_c = v.inc("c_pe")
            W("scalar", "c_pe", pe_c)
            W("scalar", "c_ac", v.n["c_ac"])

            def cte(e, npt=npt):
                e.activation(out=ct_sb[:, npt, :], in_=pct[npt % 2][:, :], func=AF.Exp,
                             bias=negc[:, 0:1], scale=1.0).then_inc(sems["c_ac"], 1)
            v.do("scalar", cte)
            cte_hist.append(v.inc("c_ac"))
        AC_CT = v.n["c_ac"]
        W("tensor", "c_ac", AC_CT)
        W("tensor", "d_yb", YB_TGT)
        for ti8 in range(8):
            t_glob = 8 * b + ti8
            n6 = 8 * b + ti8
            if n6 >= 2:
                W("tensor", "c_dv", cev_hist[n6 - 2])

            def com(e, ti8=ti8):
                for c in range(2):
                    for npt in range(NT):
                        mm = e.matmul(out=pco[ti8 % 2][:, 512 * c:512 * c + 512],
                                      lhsT=ct_sb[:, npt, 128 * ti8:128 * ti8 + 128],
                                      rhs=y_b[:, npt, 512 * c:512 * c + 512],
                                      start=(npt == 0), stop=(npt == 15))
                        if c == 1 and npt == 15:
                            mm.then_inc(sems["c_pe"], 1)
            v.do("tensor", com)
            pe_co = v.inc("c_pe")
            W("vector", "c_pe", pe_co)
            W("vector", "c_ac", v.n["c_ac"])

            def cev(e, ti8=ti8):
                e.tensor_copy(out=co_f32[:, ti8 % 2, :],
                              in_=pco[ti8 % 2][:, :]).then_inc(sems["c_dv"], 1)
            v.do("vector", cev)
            cd_co = v.inc("c_dv")
            cev_hist.append(cd_co)
            W("vector", "c_dv", cd_co)
            v.do("vector", lambda e, ti8=ti8: e.tensor_reduce(
                out=stat[:, ti8:ti8 + 1], in_=co_f32[:, ti8 % 2, :], axis=AX, op=ALU.add))
            W("scalar", "c_dv", cd_co)
            W("scalar", "c_ac", v.n["c_ac"])

            W("scalar", "c_dv", v.n["c_dv"])

            def csq(e, ti8=ti8):
                e.activation(out=sq_scr[:, 0:D], in_=co_f32[:, ti8 % 2, :], func=AF.Square,
                             bias=zero_b[:, 0:1], scale=1.0,
                             accum_out=stat2[:, ti8:ti8 + 1]).then_inc(sems["c_ac"], 1)
            v.do("scalar", csq)
            ac_c = v.inc("c_ac")
            if ti8 % 2 == 1:
                W("vector", "c_ac", ac_c)
                ln_rstd(2, c0=ti8 - 1)

                def capply(e, ti8=ti8, b=b):
                    for k2 in range(2):
                        tt_ = ti8 - 1 + k2
                        tg = 8 * b + tt_
                        e.drain()
                        e.tensor_scalar(out=scr4, in0=co_f32[:, tt_ % 2, :],
                                        scalar1=mean_t[:, tt_:tt_ + 1],
                                        scalar2=rstd_t[:, tt_:tt_ + 1],
                                        op0=ALU.subtract, op1=ALU.mult)
                        e.drain()
                        e.tensor_tensor(out=Rt[:, tg, :], in0=Rt[:, tg, :],
                                        in1=scr4, op=ALU.add).then_inc(sems["c_dv"], 1)
                    e.drain()
                v.do("vector", capply)
                v.inc("c_dv", 2)
                cd_fin = v.n["c_dv"]
                W("sync", "c_dv", cd_fin)
                tg0 = 8 * b + ti8 - 1
                dma("sync", "d_out", yout[128 * tg0:128 * tg0 + 128, :], Rt[:, tg0, :])
                dma("sync", "d_out", yout[128 * tg0 + 128:128 * tg0 + 256, :], Rt[:, tg0 + 1, :])

    W("sync", "d_out", v.n["d_out"])

    barrier_all(6)
    v.flush(nc)
    p6.close()
    p36.close()
    es.close()
    return nc, dbg


def host_prep(L0_attn_x, L0_attn_qkv_w, L0_attn_qkv_b, L0_attn_proj_w,
              L0_attn_proj_b, L0_ffn_phi, L0_ffn_W, L0_ffn_b):
    x = np.asarray(L0_attn_x, np.float32)
    qkv_w = np.asarray(L0_attn_qkv_w, np.float32)
    qkv_b = np.asarray(L0_attn_qkv_b, np.float32)
    proj_w = np.asarray(L0_attn_proj_w, np.float32)
    proj_b = np.asarray(L0_attn_proj_b, np.float32)
    phi = np.asarray(L0_ffn_phi, np.float32)
    Wx = np.asarray(L0_ffn_W, np.float32)
    bx = np.asarray(L0_ffn_b, np.float32)

    qkv_wT = np.ascontiguousarray(qkv_w.T)
    proj_wT = np.ascontiguousarray(proj_w.T)
    phi2 = np.ascontiguousarray(phi.reshape(D, NP))
    ident = np.eye(128, dtype=ml_dtypes.bfloat16)
    in_maps = []
    for c in range(NCORE):
        xsl = np.ascontiguousarray(x[:, MLOC * c:MLOC * (c + 1), :].reshape(TLOC, D))
        in_maps.append({
            "xs": xsl,
            "qkv_wT": qkv_wT,
            "proj_wT": proj_wT,
            "phi2": phi2,
            "wsl": np.ascontiguousarray(np.concatenate(
                [Wx[4 * c:4 * c + 4], Wx[32 + 4 * c:32 + 4 * c + 4]], axis=0)),
            "bsl": np.ascontiguousarray(np.concatenate(
                [bx[4 * c:4 * c + 4], bx[32 + 4 * c:32 + 4 * c + 4]], axis=0).reshape(1, EPC * D)),
            "qkvb": qkv_b.reshape(1, 3 * D),
            "projb": proj_b.reshape(1, D),
            "ident": ident,
        })
    return in_maps


def kernel(**inputs):
    in_maps = host_prep(**inputs)
    nc, _ = build_nc(debug=False)
    res = run_bass_kernel_spmd(nc, in_maps, core_ids=list(range(NCORE)))
    out = np.empty((B, NCORE * MLOC, D), np.float32)
    for c in range(NCORE):
        o = res.results[c]["yout"].reshape(B, MLOC, D)
        out[:, MLOC * c:MLOC * (c + 1), :] = o
    return out


# version 13
# speedup vs baseline: 1.1588x; 1.0889x over previous
import numpy as np
import ml_dtypes
from collections import defaultdict
from contextlib import ExitStack
import concourse.bass as bass
import concourse.mybir as mybir
import concourse.bass_utils as _bass_utils
from concourse.bass_utils import run_bass_kernel_spmd

_orig_run_command = _bass_utils.run_command

def _run_command_ldw(cmd, *a, **kw):
    if isinstance(cmd, list):
        cmd = ["--enable-ldw-opt=true" if c == "--enable-ldw-opt=false" else c for c in cmd]
    return _orig_run_command(cmd, *a, **kw)

if getattr(_bass_utils.run_command, "__name__", "") != "_run_command_ldw":
    _bass_utils.run_command = _run_command_ldw

F32 = mybir.dt.float32
BF16 = mybir.dt.bfloat16
I32 = mybir.dt.int32
AF = mybir.ActivationFunctionType
ALU = mybir.AluOpType
AX = mybir.AxisListType.X

NCORE = 8
B = 2
D = 1024
H = 16
HP = 8
MLOC = 1024
TLOC = 2048
NT = 16
NKT = 64
NP = 2048
EPC = 8
SHIFT = 600.0
CFFN = 4.0
EPS = 1e-5
MAGIC = 0x5F3759DF

ENGS = ("sync", "gpsimd", "vector", "scalar", "tensor")


class Sched:

    def __init__(self):
        self.q = {e: [] for e in ENGS}
        self.n = defaultdict(int)

    def do(self, eng, fn):
        self.q[eng].append(fn)

    def inc(self, sem, k=1):
        self.n[sem] += k
        return self.n[sem]

    def flush(self, nc):
        with nc.Block() as blk:
            for name in ENGS:
                fns = self.q[name]

                def runner(e, fns=fns):
                    for f in fns:
                        f(e)
                getattr(blk, name)(runner)
        self.q = {e: [] for e in ENGS}


def newton_rsqrt(v, vec_ap, y_ap, t1_ap, ti1, ti2, iters=4):
    def f(e):
        e.drain()
        e.tensor_scalar(out=ti1, in0=vec_ap.bitcast(I32), scalar1=1, scalar2=None,
                        op0=ALU.arith_shift_right)
        e.drain()
        e.tensor_scalar(out=ti2, in0=ti1, scalar1=-1, scalar2=int(MAGIC),
                        op0=ALU.mult, op1=ALU.add)
        e.drain()
        e.tensor_copy(out=y_ap.bitcast(I32), in_=ti2)
        e.drain()
        for _ in range(iters):
            e.tensor_tensor(out=t1_ap, in0=y_ap, in1=y_ap, op=ALU.mult)
            e.drain()
            e.tensor_tensor(out=t1_ap, in0=t1_ap, in1=vec_ap, op=ALU.mult)
            e.drain()
            e.tensor_scalar(out=t1_ap, in0=t1_ap, scalar1=-0.5, scalar2=1.5,
                            op0=ALU.mult, op1=ALU.add)
            e.drain()
            e.tensor_tensor(out=y_ap, in0=y_ap, in1=t1_ap, op=ALU.mult)
            e.drain()
    v.do("vector", f)


def build_nc(debug=False, phases=99, stage=99, lite=0):
    nc = bass.Bass("TRN2", target_bir_lowering=False)

    xs = nc.dram_tensor("xs", [TLOC, D], F32, kind="ExternalInput").ap()
    qkv_wT = nc.dram_tensor("qkv_wT", [D, 3 * D], F32, kind="ExternalInput").ap()
    proj_wT_d = nc.dram_tensor("proj_wT", [D, D], F32, kind="ExternalInput").ap()
    phi_d = nc.dram_tensor("phi2", [D, NP], F32, kind="ExternalInput").ap()
    wsl = nc.dram_tensor("wsl", [EPC, D, D], F32, kind="ExternalInput").ap()
    bsl = nc.dram_tensor("bsl", [1, EPC * D], F32, kind="ExternalInput").ap()
    qkvb = nc.dram_tensor("qkvb", [1, 3 * D], F32, kind="ExternalInput").ap()
    projb_d = nc.dram_tensor("projb", [1, D], F32, kind="ExternalInput").ap()
    ident_in = nc.dram_tensor("ident", [128, 128], BF16, kind="ExternalInput").ap()
    yout = nc.dram_tensor("yout", [TLOC, D], F32, kind="ExternalOutput").ap()

    kvkin = nc.dram_tensor("kvkin", [H * 65, TLOC], BF16).ap()
    kvkout = nc.dram_tensor("kvkout", [NCORE * H * 65, TLOC], BF16, addr_space="Shared").ap()
    kvvin = nc.dram_tensor("kvvin", [TLOC, D], BF16).ap()
    kvvout = nc.dram_tensor("kvvout", [NCORE * TLOC, D], BF16, addr_space="Shared").ap()
    q_d = nc.dram_tensor("q_d", [D, TLOC], BF16).ap()
    m8_d = nc.dram_tensor("m8_d", [H, TLOC], BF16).ap()
    kmax_in = nc.dram_tensor("kmax_in", [2, H], F32).ap()
    kmax_out = nc.dram_tensor("kmax_out", [2, H], F32).ap()
    attn_d = nc.dram_tensor("attn_d", [B, D, MLOC], BF16).ap()
    wbf = nc.dram_tensor("wbf", [EPC, D, D], BF16).ap()
    slots_in = nc.dram_tensor("slots_in", [NP * B, D], F32).ap()
    slots_out = nc.dram_tensor("slots_out", [NP * B // NCORE, D], F32).ap()
    y_in = nc.dram_tensor("y_in", [EPC * B * 32, D], BF16).ap()
    y_ag = nc.dram_tensor("y_ag", [NP * B, D], BF16, addr_space="Shared").ap()
    bc_d = nc.dram_tensor("bc_d", [1, 1024], F32).ap()

    dbg = {}
    if debug:
        for nm, shp in [("dbg_xln1T", [128, 2048]), ("dbg_k", [128, 2048]),
                        ("dbg_q", [128, 2048]), ("dbg_m8", [2, 2048]),
                        ("dbg_attn", [128, 1024]), ("dbg_xln3", [128, 1024]),
                        ("dbg_disp", [128, 1024]), ("dbg_ct", [128, 1024]),
                        ("dbg_y", [128, 1024])]:
            dbg[nm] = nc.dram_tensor(nm, shp, F32, kind="ExternalOutput").ap()

    rg = [list(range(NCORE))]
    es = ExitStack()
    E = es.enter_context

    sems = {}
    for s in ["ph", "d_x", "d_w", "d_m", "d_i", "d_kmx", "d_kmg", "d_kst", "d_vst", "d_qst", "d_qs1", "d_kld", "d_kld1",
              "d_vld", "d_vld1", "d_qld", "d_qld1", "d_at", "d_sl", "d_y", "d_yb", "d_out", "d_dbg",
              "d_bc", "d_g", "d_g2", "d_pw", "d_ph2", "d_p0", "d_p1", "d_w0", "d_w1", "cc", "c_pe", "c_dv", "c_ac", "c_sc", "c_ex", "c_av", "c_nr",
              "c_rc", "c_bc", "c_mul"]:
        sems[s] = E(nc.semaphore(s))

    v = Sched()

    def W(eng, sem, val):
        v.do(eng, lambda e, s=sems[sem], x=val: e.wait_ge(s, x))

    def dma(eng, sem, out, in_):
        def f(e, o=out, i=in_, s=sems[sem]):
            e.dma_start(out=o, in_=i).then_inc(s, 16)
        v.do(eng, f)
        return v.inc(sem, 16)


    def emit_rstd_sqrt(n, c0=0):
        def alg(e):
            sl = slice(c0, c0 + n)
            e.drain()
            e.tensor_scalar(out=mean_t[:, sl], in0=stat[:, sl], scalar1=1.0 / D,
                            scalar2=None, op0=ALU.mult)
            e.drain()
            e.tensor_tensor(out=var_t[:, sl], in0=mean_t[:, sl], in1=mean_t[:, sl], op=ALU.mult)
            e.drain()
            e.tensor_scalar(out=nsc1[:, sl], in0=stat2[:, sl], scalar1=1.0 / D,
                            scalar2=None, op0=ALU.mult)
            e.drain()
            e.tensor_tensor(out=var_t[:, sl], in0=nsc1[:, sl], in1=var_t[:, sl], op=ALU.subtract)
            e.drain()
            e.tensor_scalar(out=var_t[:, sl], in0=var_t[:, sl], scalar1=EPS,
                            scalar2=None, op0=ALU.add).then_inc(sems["c_dv"], 1)
        v.do("vector", alg)
        cd_v = v.inc("c_dv")
        W("scalar", "c_dv", cd_v)
        W("scalar", "c_ac", v.n["c_ac"])

        def sq_(e):
            e.activation(out=rstd_t[:, c0:c0 + n], in_=var_t[:, c0:c0 + n], func=AF.Sqrt,
                         bias=zero_b[:, 0:1], scale=1.0).then_inc(sems["c_ac"], 1)
        v.do("scalar", sq_)
        ac_s = v.inc("c_ac")
        W("vector", "c_ac", ac_s)

        def rc(e):
            e.reciprocal(out=rstd_t[:, c0:c0 + n], in_=rstd_t[:, c0:c0 + n])
            e.drain()
        v.do("vector", rc)

    def barrier_all(phase_idx):
        for e in ENGS:
            v.do(e, lambda x, s=sems["ph"]: x.sem_inc(s, 1))
        v.inc("ph", 5)
        tgt = v.n["ph"]
        for e in ENGS:
            v.do(e, lambda x, s=sems["ph"], t=tgt: x.wait_ge(s, t))

    Rt = E(nc.sbuf_tensor("Rt", [128, NT, D], F32))
    ident = E(nc.sbuf_tensor("identb", [128, 128], BF16))
    ones_row = E(nc.sbuf_tensor("ones_row", [1, 512], BF16))
    negc = E(nc.sbuf_tensor("negc", [128, 1], F32))
    zero_b = E(nc.sbuf_tensor("zero_b", [128, 1], F32))
    stat = E(nc.sbuf_tensor("stat", [128, 16], F32))
    stat2 = E(nc.sbuf_tensor("stat2", [128, 16], F32))
    mean_t = E(nc.sbuf_tensor("mean_t", [128, 16], F32))
    var_t = E(nc.sbuf_tensor("var_t", [128, 16], F32))
    rstd_t = E(nc.sbuf_tensor("rstd_t", [128, 16], F32))
    nsc1 = E(nc.sbuf_tensor("nsc1", [128, 16], F32))
    nsi1 = E(nc.sbuf_tensor("nsi1", [128, 16], I32))
    nsi2 = E(nc.sbuf_tensor("nsi2", [128, 16], I32))
    sq_scr = E(nc.sbuf_tensor("sq_scr", [128, 2048], BF16))

    p1 = ExitStack()
    qwT = p1.enter_context(nc.sbuf_tensor("qwT", [128, 8, 3 * D], BF16))
    xln1 = p1.enter_context(nc.sbuf_tensor("xln1", [128, 2, D], BF16))
    xln1T = p1.enter_context(nc.sbuf_tensor("xln1T", [128, 8, TLOC], BF16))
    ktmp = p1.enter_context(nc.sbuf_tensor("ktmp", [128, 2, TLOC], BF16))
    qtmp = p1.enter_context(nc.sbuf_tensor("qtmp", [128, 2, TLOC], BF16))
    vtmp = qtmp[:, :, 0:D]
    hind = p1.enter_context(nc.sbuf_tensor("hind", [128, 2], BF16))
    qkb_row = p1.enter_context(nc.sbuf_tensor("qkb_row", [1, 3 * D], BF16))
    kml = p1.enter_context(nc.sbuf_tensor("kml", [2, H], F32))
    kmt = p1.enter_context(nc.sbuf_tensor("kmt", [2, 2], F32))
    kmg = p1.enter_context(nc.sbuf_tensor("kmg", [2, H], F32))
    kng = p1.enter_context(nc.sbuf_tensor("kng", [2, H], F32))
    qn2 = p1.enter_context(nc.sbuf_tensor("qn2", [2, TLOC], F32))
    qnw = p1.enter_context(nc.sbuf_tensor("qnw", [2, MLOC], F32))
    qns = p1.enter_context(nc.sbuf_tensor("qns", [2, MLOC], F32))
    qni1 = p1.enter_context(nc.sbuf_tensor("qni1", [2, MLOC], I32))
    m8s = p1.enter_context(nc.sbuf_tensor("m8s", [2, 2, TLOC], BF16))

    psq = [p1.enter_context(nc.psum_tensor(f"psq{i}", [128, TLOC], F32)) for i in range(2)]
    ptrv = psq[1][:].bitcast(BF16)
    ptr0 = ptrv[:, 0:512]
    ptr1 = ptrv[:, 1024:1536]

    def _cut():
        barrier_all(1)
        v.flush(nc)
        p1.close()
        es.close()
        return nc, dbg

    xsr = xs.rearrange("(t p) d -> p t d", p=128)
    dma("sync", "d_x", Rt[:, 0:8, :], xsr[:, 0:8, :])
    dma("sync", "d_x", Rt[:, 8:16, :], xsr[:, 8:16, :])
    dma("sync", "d_i", ident[:], ident_in[:, :])
    qwr = qkv_wT.rearrange("(dt p) e -> p dt e", p=128)
    for i in range(4):
        dma("gpsimd", "d_w", qwT[:, 2 * i:2 * i + 2, :], qwr[:, 2 * i:2 * i + 2, :])

    def consts(e):
        e.memset(ones_row[:], 1.0).then_inc(sems["c_dv"], 1)
        e.memset(negc[:], -CFFN).then_inc(sems["c_dv"], 1)
        e.memset(zero_b[:], 0.0).then_inc(sems["c_dv"], 1)
        e.memset(hind[:], 0.0)
        e.drain()
        e.memset(hind[0:64, 0:1], 1.0).then_inc(sems["c_dv"], 1)
        e.memset(hind[64:128, 1:2], 1.0).then_inc(sems["c_dv"], 1)
        e.drain()
    v.do("vector", consts)
    CD_CONST = v.inc("c_dv", 5)

    dma("gpsimd", "d_g", qkb_row[:], qkvb[:, :])
    QKB_TGT = v.n["d_g"]

    W("sync", "c_dv", CD_CONST)
    for h in range(H):
        for ch4 in range(4):
            dma("sync", "d_kst", kvkin[h * 65 + 64:h * 65 + 65, 512 * ch4:512 * ch4 + 512],
                ones_row[0:1, :])

    W("vector", "d_x", 32)
    W("scalar", "d_x", 32)
    W("scalar", "c_dv", CD_CONST)
    for t in range(NT):
        v.do("vector", lambda e, t=t: e.tensor_reduce(
            out=stat[:, t:t + 1], in_=Rt[:, t, :], axis=AX, op=ALU.add))
        if t > 0:
            W("scalar", "c_ac", v.n["c_ac"])
        def sq(e, t=t):
            e.activation(out=sq_scr[:, 0:D], in_=Rt[:, t, :], func=AF.Square,
                         bias=zero_b[:, 0:1], scale=1.0,
                         accum_out=stat2[:, t:t + 1]).then_inc(sems["c_ac"], 1)
        v.do("scalar", sq)
        v.inc("c_ac", 1)
    AC_LN1 = v.n["c_ac"]

    def ln_finish(e, n=NT):
        e.drain()
        e.tensor_scalar(out=mean_t[:, 0:n], in0=stat[:, 0:n], scalar1=1.0 / D,
                        scalar2=None, op0=ALU.mult)
        e.drain()
        e.tensor_tensor(out=var_t[:, 0:n], in0=mean_t[:, 0:n], in1=mean_t[:, 0:n], op=ALU.mult)
        e.drain()
        e.tensor_scalar(out=nsc1[:, 0:n], in0=stat2[:, 0:n], scalar1=1.0 / D,
                        scalar2=None, op0=ALU.mult)
        e.drain()
        e.tensor_tensor(out=var_t[:, 0:n], in0=nsc1[:, 0:n], in1=var_t[:, 0:n], op=ALU.subtract)
        e.drain()
        e.tensor_scalar(out=var_t[:, 0:n], in0=var_t[:, 0:n], scalar1=EPS, scalar2=None, op0=ALU.add)
        e.drain()
        e.tensor_scalar(out=nsi1[:, 0:n], in0=var_t[:, 0:n].bitcast(I32), scalar1=1,
                        scalar2=None, op0=ALU.arith_shift_right)
        e.drain()
        e.tensor_scalar(out=nsi2[:, 0:n], in0=nsi1[:, 0:n], scalar1=-1, scalar2=int(MAGIC),
                        op0=ALU.mult, op1=ALU.add)
        e.drain()
        e.tensor_copy(out=rstd_t[:, 0:n].bitcast(I32), in_=nsi2[:, 0:n])
        e.drain()
        for _ in range(4):
            e.tensor_tensor(out=nsc1[:, 0:n], in0=rstd_t[:, 0:n], in1=rstd_t[:, 0:n], op=ALU.mult)
            e.drain()
            e.tensor_tensor(out=nsc1[:, 0:n], in0=nsc1[:, 0:n], in1=var_t[:, 0:n], op=ALU.mult)
            e.drain()
            e.tensor_scalar(out=nsc1[:, 0:n], in0=nsc1[:, 0:n], scalar1=-0.5, scalar2=1.5,
                            op0=ALU.mult, op1=ALU.add)
            e.drain()
            e.tensor_tensor(out=rstd_t[:, 0:n], in0=rstd_t[:, 0:n], in1=nsc1[:, 0:n], op=ALU.mult)
            e.drain()

    if stage < 2:
        return _cut()
    W("vector", "c_ac", AC_LN1)
    emit_rstd_sqrt(NT)
    W("tensor", "d_i", 16)
    ptrs = [ptr0, ptr1]
    tr1_hist = []
    ev1_hist = []
    for t in range(NT):
        if t >= 2:
            W("vector", "c_pe", tr1_hist[2 * (t - 2) + 1])
        v.do("vector", lambda e, t=t: e.tensor_scalar(
            out=xln1[:, t % 2, :], in0=Rt[:, t, :], scalar1=mean_t[:, t:t + 1],
            scalar2=rstd_t[:, t:t + 1], op0=ALU.subtract,
            op1=ALU.mult).then_inc(sems["c_dv"], 1))
        cd_ap = v.inc("c_dv")
        W("tensor", "c_dv", cd_ap)
        for dh in range(2):
            g1 = 2 * t + dh
            if g1 >= 2:
                W("tensor", "c_dv", ev1_hist[g1 - 2])

            def tr(e, t=t, dh=dh, p=ptrs[dh]):
                for j in range(4):
                    dt = 4 * dh + j
                    mm = e.transpose(out=p[:, 128 * j:128 * j + 128],
                                     in_=xln1[:, t % 2, 128 * dt:128 * dt + 128],
                                     identity=ident[:])
                    if j == 3:
                        mm.then_inc(sems["c_pe"], 1)
            v.do("tensor", tr)
            pe_g = v.inc("c_pe")
            tr1_hist.append(pe_g)
            W("vector", "c_pe", pe_g)

            def ev(e, t=t, dh=dh, p=ptrs[dh]):
                e.tensor_copy(
                    out=xln1T[:, 4 * dh:4 * dh + 4, 128 * t:128 * t + 128],
                    in_=p.rearrange("p (dt c) -> p dt c", c=128),
                ).then_inc(sems["c_dv"], 1)
            v.do("vector", ev)
            ev1_hist.append(v.inc("c_dv"))
    CD_TRE = v.n["c_dv"]

    if stage < 3:
        return _cut()
    W("tensor", "c_dv", CD_TRE)
    W("tensor", "d_w", 64)
    W("tensor", "d_g", QKB_TGT)

    def qk_eti(eti_list, tmp, kind):
        kn_hist = []
        ev_hist = []
        for idx, eti in enumerate(eti_list):
            if idx >= 2:
                W("tensor", "c_dv", ev_hist[idx - 2])
            else:
                W("tensor", "c_dv", v.n["c_dv"])

            def mms(e, eti=eti, i=idx):
                for dt in range(8):
                    for c in range(4):
                        e.matmul(out=psq[i % 2][:, 512 * c:512 * c + 512],
                                 lhsT=qwT[:, dt, 128 * eti:128 * eti + 128],
                                 rhs=xln1T[:, dt, 512 * c:512 * c + 512],
                                 start=(dt == 0), stop=False)
                for c in range(4):
                    mm = e.matmul(out=psq[i % 2][:, 512 * c:512 * c + 512],
                                  lhsT=qkb_row[0:1, 128 * eti:128 * eti + 128],
                                  rhs=ones_row[0:1, 0:512],
                                  start=False, stop=True)
                    if c == 3:
                        mm.then_inc(sems["c_pe"], 1)
            v.do("tensor", mms)
            pe_m = v.inc("c_pe")
            W("vector", "c_pe", pe_m)
            if kind == "Q" and idx < 2:
                W("vector", "d_vst", 16 * NT)
            if idx >= 2:
                if kind == "K":
                    W("vector", "d_kst", v.n["d_kst"])
                else:
                    W("vector", qst_hist[idx - 2][0], qst_hist[idx - 2][1])
                W("vector", "c_ac", v.n["c_ac"])
            v.do("vector", lambda e, t=tmp, i=idx: e.tensor_copy(
                out=t[:, i % 2, :], in_=psq[i % 2][:, :]).then_inc(sems["c_dv"], 1))
            cd_e = v.inc("c_dv")
            if lite < 2:
                W("scalar", "c_dv", cd_e)
                W("scalar", "c_ac", v.n["c_ac"])

                def sq(e, t=tmp, i=idx):
                    e.activation(out=sq_scr[:, :], in_=t[:, i % 2, :], func=AF.Square,
                                 bias=zero_b[:, 0:1], scale=1.0).then_inc(sems["c_ac"], 1)
                v.do("scalar", sq)
                ac_m = v.inc("c_ac")
            else:
                ac_m = v.n["c_ac"]
            if lite >= 1:
                ev_hist.append(cd_e)
                yield eti, cd_e
                continue
            W("tensor", "c_ac", ac_m)
            W("tensor", "c_dv", cd_e)
            for c in range(4):
                if c >= 2:
                    W("tensor", "c_dv", kn_hist[-2])

                def knm(e, c=c, i=idx):
                    e.matmul(out=psq[i % 2][0:2, 1024 * (c % 2):1024 * (c % 2) + 512],
                             lhsT=hind[:, :],
                             rhs=sq_scr[:, 512 * c:512 * c + 512],
                             start=True, stop=True).then_inc(sems["c_pe"], 1)
                v.do("tensor", knm)
                pe_k = v.inc("c_pe")
                W("vector", "c_pe", pe_k)
                pkn_src = lambda c_, i=idx: psq[i % 2][0:2, 1024 * (c_ % 2):1024 * (c_ % 2) + 512]
                if kind == "K":
                    def red(e, c=c, eti=eti, pkn_src=pkn_src):
                        e.drain()
                        b = c // 2
                        e.tensor_reduce(out=kmt[:, 0:1],
                                        in_=pkn_src(c), axis=AX, op=ALU.max)
                        e.drain()
                        hp = eti - 8
                        col = 2 * hp + b
                        if c % 2 == 0:
                            e.tensor_copy(out=kml[:, col:col + 1],
                                          in_=kmt[:, 0:1]).then_inc(sems["c_dv"], 1)
                        else:
                            e.tensor_tensor(out=kml[:, col:col + 1], in0=kml[:, col:col + 1],
                                            in1=kmt[:, 0:1], op=ALU.max).then_inc(sems["c_dv"], 1)
                        e.drain()
                    v.do("vector", red)
                    kn_hist.append(v.inc("c_dv"))
                else:
                    def cp(e, c=c, pkn_src=pkn_src):
                        e.tensor_copy(out=qn2[:, 512 * c:512 * c + 512],
                                      in_=pkn_src(c)).then_inc(sems["c_dv"], 1)
                    v.do("vector", cp)
                    kn_hist.append(v.inc("c_dv"))
            ev_hist.append(v.n["c_dv"])
            yield eti, cd_e

    k_ev = []
    for eti, cd_e in qk_eti(range(8, 16), ktmp, "K"):
        k_ev.append((eti, cd_e))
        hp = eti - 8
        W("sync", "c_dv", cd_e)
        dma("sync", "d_kst", kvkin[(2 * hp) * 65:(2 * hp) * 65 + 64, :], ktmp[0:64, (eti - 8) % 2, :])
        dma("sync", "d_kst", kvkin[(2 * hp + 1) * 65:(2 * hp + 1) * 65 + 64, :], ktmp[64:128, (eti - 8) % 2, :])
        if hp >= 1:
            pass
    KST_TOTAL = v.n["d_kst"]
    W("sync", "c_dv", v.n["c_dv"])
    dma("sync", "d_kmx", kmax_in[:, :], kml[:])
    if stage < 4:
        return _cut()
    W("gpsimd", "d_kmx", 16)

    def arm(e):
        e.collective_compute("AllReduce", ALU.max, replica_groups=rg,
                             ins=[kmax_in.opt()], outs=[kmax_out.opt()]).then_inc(sems["cc"])
    v.do("gpsimd", arm)
    v.inc("cc")
    W("gpsimd", "d_kst", KST_TOTAL)

    def agk(e):
        e.collective_compute("AllGather", ALU.bypass, replica_groups=rg,
                             ins=[kvkin.opt()], outs=[kvkout.opt()]).then_inc(sems["cc"])
    v.do("gpsimd", agk)
    v.inc("cc")

    if stage < 5:
        return _cut()
    W("tensor", "c_dv", v.n["c_dv"])
    W("tensor", "c_ac", v.n["c_ac"])
    vev_hist = []
    for t in range(NT):
        if t >= 2:
            W("tensor", "c_dv", vev_hist[t - 2])

        def vmm(e, t=t):
            for dt in range(8):
                for c in range(2):
                    e.matmul(out=psq[t % 2][:, 512 * c:512 * c + 512],
                             lhsT=xln1T[:, dt, 128 * t:128 * t + 128],
                             rhs=qwT[:, dt, 2 * D + 512 * c:2 * D + 512 * c + 512],
                             start=(dt == 0), stop=False)
            for c in range(2):
                mm = e.matmul(out=psq[t % 2][:, 512 * c:512 * c + 512],
                              lhsT=ones_row[0:1, 0:128],
                              rhs=qkb_row[0:1, 2 * D + 512 * c:2 * D + 512 * c + 512],
                              start=False, stop=True)
                if c == 1:
                    mm.then_inc(sems["c_pe"], 1)
        v.do("tensor", vmm)
        pe_v = v.inc("c_pe")
        W("vector", "c_pe", pe_v)
        if t >= 2:
            W("vector", "d_vst", v.n["d_vst"])

        def vev(e, t=t):
            e.tensor_copy(out=vtmp[:, t % 2, :], in_=psq[t % 2][:, 0:1024]).then_inc(sems["c_dv"], 1)
        v.do("vector", vev)
        cd_v = v.inc("c_dv")
        vev_hist.append(cd_v)
        W("sync", "c_dv", cd_v)
        dma("sync", "d_vst", kvvin[128 * t:128 * t + 128, :], vtmp[:, t % 2, :])
    W("gpsimd", "d_vst", 16 * NT)

    def agv(e):
        e.collective_compute("AllGather", ALU.bypass, replica_groups=rg,
                             ins=[kvvin.opt()], outs=[kvvout.opt()]).then_inc(sems["cc"])
    v.do("gpsimd", agv)
    v.inc("cc")
    for eI in range(EPC):
        dma("gpsimd", "d_w", wbf[eI, :, :], wsl[eI, :, :])
    W("gpsimd", "cc", 1)
    dma("gpsimd", "d_kmg", kmg[:], kmax_out[:, :])
    if stage < 6:
        return _cut()
    W("scalar", "d_kmg", 16)
    W("scalar", "c_ac", v.n["c_ac"])

    def kngs(e):
        e.activation(out=kng[:], in_=kmg[:], func=AF.Sqrt,
                     bias=zero_b[0:2, 0:1], scale=1.0).then_inc(sems["c_ac"], 1)
    v.do("scalar", kngs)
    AC_KNG = v.inc("c_ac")
    W("vector", "c_ac", AC_KNG)

    def kngn(e):
        e.drain()
        e.tensor_scalar(out=kng[:], in0=kng[:], scalar1=-1.0, scalar2=None, op0=ALU.mult)
        e.drain()
    v.do("vector", kngn)

    if stage < 7:
        return _cut()
    qst_hist = []
    for eti, cd_e in qk_eti(range(0, 8), qtmp, "Q"):
        hp = eti
        W("scalar", "c_dv", v.n["c_dv"])
        W("scalar", "c_ac", v.n["c_ac"])

        def m8sq(e):
            e.activation(out=qnw[0:2, :], in_=qn2[:, 0:MLOC], func=AF.Sqrt,
                         bias=zero_b[0:2, 0:1], scale=1.0)
            e.activation(out=qns[0:2, :], in_=qn2[:, MLOC:TLOC], func=AF.Sqrt,
                         bias=zero_b[0:2, 0:1], scale=1.0).then_inc(sems["c_ac"], 1)
        v.do("scalar", m8sq)
        ac_m8 = v.inc("c_ac")
        W("vector", "c_ac", ac_m8)

        def m8chain(e, hp=hp):
            e.drain()
            e.tensor_scalar(out=m8s[:, hp % 2, 0:MLOC], in0=qnw[0:2, :],
                            scalar1=kng[:, 2 * hp:2 * hp + 1], scalar2=SHIFT,
                            op0=ALU.mult, op1=ALU.add)
            e.tensor_scalar(out=m8s[:, hp % 2, MLOC:TLOC], in0=qns[0:2, :],
                            scalar1=kng[:, 2 * hp + 1:2 * hp + 2], scalar2=SHIFT,
                            op0=ALU.mult, op1=ALU.add).then_inc(sems["c_dv"], 1)
            e.drain()
        v.do("vector", m8chain)
        cd_m8 = v.inc("c_dv")
        W("sync", "c_dv", cd_m8)
        qsem2 = "d_qst" if hp % 2 == 0 else "d_qs1"
        dma("sync", qsem2, q_d[128 * hp:128 * hp + 128, :], qtmp[:, hp % 2, :])
        qst = dma("sync", qsem2, m8_d[2 * hp:2 * hp + 2, :], m8s[:, hp % 2, :])
        qst_hist.append((qsem2, qst))
        if hp >= 1:
            pass

    if debug and stage >= 7 and phases == 1:
        W("sync", "c_dv", v.n["c_dv"])
        W("sync", "d_qst", v.n["d_qst"])
        W("sync", "cc", 3)
        v.do("gpsimd", lambda e, x=v.n["c_dv"]: e.wait_ge(sems["c_dv"], x))
        dma("gpsimd", "d_dbg", dbg["dbg_xln1T"][:, :], xln1T[:, 0, :])
        dma("gpsimd", "d_dbg", dbg["dbg_q"][:, :], q_d[0:128, :])
        dma("gpsimd", "d_dbg", dbg["dbg_m8"][:, :], m8_d[0:2, :])
        dma("gpsimd", "d_dbg", dbg["dbg_k"][:, :], kvkout[0:128, :])
        W("gpsimd", "d_dbg", v.n["d_dbg"])

    barrier_all(1)
    v.flush(nc)
    p1.close()

    if phases < 2:
        es.close()
        return nc, dbg

    p2 = ExitStack()
    kbufA = [p2.enter_context(nc.sbuf_tensor(f"kbufA{i}", [65, 8192], BF16)) for i in range(2)]
    kbufB = [p2.enter_context(nc.sbuf_tensor(f"kbufB{i}", [65, 8192], BF16)) for i in range(2)]
    vbufA = [p2.enter_context(nc.sbuf_tensor(f"vbufA{i}", [128, 65 * 64], BF16)) for i in range(2)]
    vbufB = [p2.enter_context(nc.sbuf_tensor(f"vbufB{i}", [128, 65 * 64], BF16)) for i in range(2)]
    qbufA = [p2.enter_context(nc.sbuf_tensor(f"qbufA{i}", [65, 512], BF16)) for i in range(2)]
    qbufB = [p2.enter_context(nc.sbuf_tensor(f"qbufB{i}", [65, 512], BF16)) for i in range(2)]
    pt = [p2.enter_context(nc.sbuf_tensor(f"pt{i}", [128, 1024], BF16)) for i in range(4)]
    uA = p2.enter_context(nc.sbuf_tensor("uA", [64, 512], F32))
    uB = p2.enter_context(nc.sbuf_tensor("uB", [64, 512], F32))
    s65 = p2.enter_context(nc.sbuf_tensor("s65", [65, 1024], F32))
    sbcA = p2.enter_context(nc.sbuf_tensor("sbcA", [64, 512], F32))
    sbcB = p2.enter_context(nc.sbuf_tensor("sbcB", [64, 512], F32))
    astA = p2.enter_context(nc.sbuf_tensor("astA", [64, 512], BF16))
    astB = p2.enter_context(nc.sbuf_tensor("astB", [64, 512], BF16))
    psum_s = [p2.enter_context(nc.psum_tensor(f"psum_s{i}", [128, 1024], F32)) for i in range(3)]
    pavA = [p2.enter_context(nc.psum_tensor("pavA0", [128, 512], F32))] * 2
    pavB = [p2.enter_context(nc.psum_tensor("pavB0", [128, 512], F32))] * 2

    kvk_r = kvkout.rearrange("(r h j2) c -> h j2 r c", h=H, j2=65)
    kvv_r = kvvout.rearrange("(r bb i) d -> bb r i d", bb=2, i=MLOC)

    for s_ in ("d_kst", "d_vst", "d_qst", "d_qs1"):
        W("vector", s_, v.n[s_])

    def vones(e):
        for buf in (vbufA[0], vbufA[1], vbufB[0], vbufB[1]):
            e.memset(buf[:].rearrange("p (k c) -> p k c", c=65)[:, :, 64:65], 1.0).then_inc(sems["c_dv"], 1)
    v.do("vector", vones)
    CD_VONES = v.inc("c_dv", 4)

    NUNIT = 32
    W("sync", "cc", 3)
    for s_ in ("d_kst", "d_vst", "d_qst", "d_qs1", "d_kmx"):
        W("sync", s_, v.n[s_])
    for j in range(NUNIT):
        u, hp = divmod(j, 8)
        b, qbi = divmod(u, 2)
        colbase = b * MLOC + qbi * 512
        par = j % 2
        kldsem = "d_kld" if par == 0 else "d_kld"
        ksem = "d_kld" if par == 0 else "d_kld1"
        vsem = "d_vld" if par == 0 else "d_vld1"
        qsem = "d_qld" if par == 0 else "d_qld1"
        if j >= 2:
            W("sync", "c_av", 64 * (j - 1))
        dma("sync", ksem, kbufA[par][:].rearrange("p (r c) -> p r c", r=8),
            kvk_r[2 * hp][:, :, b * MLOC:(b + 1) * MLOC])
        dma("sync", ksem, kbufB[par][:].rearrange("p (r c) -> p r c", r=8),
            kvk_r[2 * hp + 1][:, :, b * MLOC:(b + 1) * MLOC])
        for c3 in range(8):
            src = kvv_r[b][c3].rearrange("(kt p) d -> p kt d", p=128)
            srcA = src[:, :, 128 * hp:128 * hp + 64]
            srcB = src[:, :, 128 * hp + 64:128 * hp + 128]
            dstA = vbufA[par][:].rearrange("p (k c) -> p k c", c=65)[:, 8 * c3:8 * c3 + 8, 0:64]
            dstB = vbufB[par][:].rearrange("p (k c) -> p k c", c=65)[:, 8 * c3:8 * c3 + 8, 0:64]
            dma("sync", vsem, dstA, srcA)
            dma("sync", vsem, dstB, srcB)
        dma("sync", qsem, qbufA[par][0:64, :], q_d[128 * hp:128 * hp + 64, colbase:colbase + 512])
        dma("sync", qsem, qbufA[par][64:65, :], m8_d[2 * hp:2 * hp + 1, colbase:colbase + 512])
        dma("sync", qsem, qbufB[par][0:64, :], q_d[128 * hp + 64:128 * hp + 128, colbase:colbase + 512])
        dma("sync", qsem, qbufB[par][64:65, :], m8_d[2 * hp + 1:2 * hp + 2, colbase:colbase + 512])
        kld_tgt = (ksem, v.n[ksem])
        vld_tgt = (vsem, v.n[vsem])
        qld_tgt = (qsem, v.n[qsem])

        W("tensor", kld_tgt[0], kld_tgt[1])
        W("tensor", qld_tgt[0], qld_tgt[1])

        def emit_scores(kt):
            I = 64 * j + kt
            if I >= 3:
                W("tensor", "c_ex", I - 2)

            def sc(e, par=par, kt=kt, I=I):
                e.matmul(out=psum_s[I % 3][:, 0:512],
                         lhsT=kbufA[par][:, 128 * kt:128 * kt + 128],
                         rhs=qbufA[par][:, :], start=True, stop=True)
                e.matmul(out=psum_s[I % 3][:, 512:1024],
                         lhsT=kbufB[par][:, 128 * kt:128 * kt + 128],
                         rhs=qbufB[par][:, :], start=True,
                         stop=True).then_inc(sems["c_sc"], 1)
            v.do("tensor", sc)
            v.inc("c_sc")
            W("scalar", "c_sc", I + 1)
            if I >= 4:
                W("scalar", "c_av", I - 3)

            def ex(e, kt=kt, I=I):
                e.activation(out=pt[I % 4][:, :], in_=psum_s[I % 3][:, :], func=AF.Exp,
                             bias=zero_b[:, 0:1], scale=0.125).then_inc(sems["c_ex"], 1)
            v.do("scalar", ex)
            v.inc("c_ex")

        def emit_attnv(kt):
            I = 64 * j + kt
            W("tensor", "c_ex", I + 1)
            if kt == 0:
                W("tensor", vld_tgt[0], vld_tgt[1])
                if j == 0:
                    W("tensor", "c_dv", CD_VONES)
                if j >= 1:
                    W("tensor", "c_nr", j)

            def av(e, par=par, kt=kt, I=I, j=j):
                e.matmul(out=pavA[j % 2][0:65, :], lhsT=vbufA[par][:, 65 * kt:65 * kt + 65],
                         rhs=pt[I % 4][:, 0:512], start=(kt == 0), stop=(kt == 63))
                e.matmul(out=pavB[j % 2][0:65, :], lhsT=vbufB[par][:, 65 * kt:65 * kt + 65],
                         rhs=pt[I % 4][:, 512:1024], start=(kt == 0),
                         stop=(kt == 63)).then_inc(sems["c_av"], 1)
            v.do("tensor", av)
            v.inc("c_av")

        emit_scores(0)
        emit_scores(1)
        for kt in range(2, NKT):
            emit_scores(kt)
            emit_attnv(kt - 2)
        emit_attnv(NKT - 2)
        emit_attnv(NKT - 1)

        W("vector", "c_av", 64 * (j + 1))
        if j >= 1:
            W("vector", "d_at", 32 * j)

        def nrm1(e, j=j):
            e.tensor_scalar(out=s65[64:65, 0:512], in0=pavA[j % 2][64:65, :], scalar1=1e-30,
                            scalar2=None, op0=ALU.add)
            e.tensor_scalar(out=s65[64:65, 512:1024], in0=pavB[j % 2][64:65, :], scalar1=1e-30,
                            scalar2=None, op0=ALU.add).then_inc(sems["c_rc"], 1)
            e.tensor_copy(out=uA[:], in_=pavA[j % 2][0:64, :])
            e.tensor_copy(out=uB[:], in_=pavB[j % 2][0:64, :]).then_inc(sems["c_nr"], 1)
        v.do("vector", nrm1)
        v.inc("c_rc")
        v.inc("c_nr")
        W("gpsimd", "c_rc", v.n["c_rc"])

        def bc(e, n0=v.n["d_bc"]):
            e.dma_start(out=bc_d[:, :], in_=s65[64:65, :]).then_inc(sems["d_bc"], 16)
            e.wait_ge(sems["d_bc"], n0 + 16)
            e.dma_start(out=sbcA[:], in_=bc_d[0:1, 0:512].broadcast_to([64, 512])).then_inc(sems["d_bc"], 16)
            e.dma_start(out=sbcB[:], in_=bc_d[0:1, 512:1024].broadcast_to([64, 512])).then_inc(sems["d_bc"], 16)
        v.do("gpsimd", bc)
        v.inc("d_bc", 48)
        W("vector", "d_bc", v.n["d_bc"])

        def nrm2(e):
            e.drain()
            e.reciprocal(out=sbcA[:], in_=sbcA[:])
            e.reciprocal(out=sbcB[:], in_=sbcB[:])
            e.drain()
            e.tensor_tensor(out=astA[:], in0=uA[:], in1=sbcA[:], op=ALU.mult).then_inc(sems["c_mul"], 1)
            e.tensor_tensor(out=astB[:], in0=uB[:], in1=sbcB[:], op=ALU.mult).then_inc(sems["c_mul"], 1)
        v.do("vector", nrm2)
        v.inc("c_mul", 2)
        W("gpsimd", "c_mul", v.n["c_mul"])
        dma("gpsimd", "d_at", attn_d[b, 128 * hp:128 * hp + 64, qbi * 512:qbi * 512 + 512], astA[:])
        dma("gpsimd", "d_at", attn_d[b, 128 * hp + 64:128 * hp + 128, qbi * 512:qbi * 512 + 512], astB[:])

    if debug and phases == 2:
        v.do("gpsimd", lambda e, x=v.n["d_at"]: e.wait_ge(sems["d_at"], x))
        dma("gpsimd", "d_dbg", dbg["dbg_attn"][:, :], attn_d[0, 0:128, :])
        W("gpsimd", "d_dbg", v.n["d_dbg"])

    barrier_all(2)
    v.flush(nc)
    p2.close()

    if phases < 3:
        es.close()
        return nc, dbg

    p36 = ExitStack()
    xln3T = p36.enter_context(nc.sbuf_tensor("xln3T", [128, 8, TLOC], BF16))
    phi_sb = p36.enter_context(nc.sbuf_tensor("phi_sb", [128, 8, NP], BF16))
    p34 = ExitStack()
    xln3 = p34.enter_context(nc.sbuf_tensor("xln3", [128, NT, D], BF16))
    p3 = ExitStack()
    attn_t = p3.enter_context(nc.sbuf_tensor("attn_t", [128, 2, 8, 128], BF16))
    pwT = p3.enter_context(nc.sbuf_tensor("pwT", [128, 8, D], BF16))
    prow = p3.enter_context(nc.sbuf_tensor("prow", [1, D], BF16))
    o_bf = p3.enter_context(nc.sbuf_tensor("o_bf", [128, 8, D], BF16))
    scrA = p3.enter_context(nc.sbuf_tensor("scrA", [128, D], F32))
    scrB_ap = sq_scr[:].bitcast(F32)
    ppo = [p3.enter_context(nc.psum_tensor(f"ppo{i}", [128, D], F32)) for i in range(2)]
    ptr2 = [p3.enter_context(nc.psum_tensor(f"ptr2{i}", [128, 512], BF16)) for i in range(2)]

    def lnstats_sbuf(src_ap_fn, ntiles, base_col=0):
        for t in range(ntiles):
            v.do("vector", lambda e, t=t: e.tensor_reduce(
                out=stat[:, base_col + t:base_col + t + 1], in_=src_ap_fn(t), axis=AX, op=ALU.add))
            W("scalar", "c_ac", v.n["c_ac"])

            def sqf(e, t=t):
                e.activation(out=sq_scr[:, 0:D], in_=src_ap_fn(t), func=AF.Square,
                             bias=zero_b[:, 0:1], scale=1.0,
                             accum_out=stat2[:, base_col + t:base_col + t + 1]).then_inc(sems["c_ac"], 1)
            v.do("scalar", sqf)
            v.inc("c_ac")

    def ln_rstd(n, c0=0):
        def f(e):
            sl = slice(c0, c0 + n)
            e.drain()
            e.tensor_scalar(out=mean_t[:, sl], in0=stat[:, sl], scalar1=1.0 / D,
                            scalar2=None, op0=ALU.mult)
            e.drain()
            e.tensor_tensor(out=var_t[:, sl], in0=mean_t[:, sl], in1=mean_t[:, sl], op=ALU.mult)
            e.drain()
            e.tensor_scalar(out=nsc1[:, sl], in0=stat2[:, sl], scalar1=1.0 / D,
                            scalar2=None, op0=ALU.mult)
            e.drain()
            e.tensor_tensor(out=var_t[:, sl], in0=nsc1[:, sl], in1=var_t[:, sl], op=ALU.subtract)
            e.drain()
            e.tensor_scalar(out=var_t[:, sl], in0=var_t[:, sl], scalar1=EPS, scalar2=None, op0=ALU.add)
            e.drain()
            e.tensor_scalar(out=nsi1[:, sl], in0=var_t[:, sl].bitcast(I32), scalar1=1,
                            scalar2=None, op0=ALU.arith_shift_right)
            e.drain()
            e.tensor_scalar(out=nsi2[:, sl], in0=nsi1[:, sl], scalar1=-1, scalar2=int(MAGIC),
                            op0=ALU.mult, op1=ALU.add)
            e.drain()
            e.tensor_copy(out=rstd_t[:, sl].bitcast(I32), in_=nsi2[:, sl])
            e.drain()
            for _ in range(4):
                e.tensor_tensor(out=nsc1[:, sl], in0=rstd_t[:, sl], in1=rstd_t[:, sl], op=ALU.mult)
                e.drain()
                e.tensor_tensor(out=nsc1[:, sl], in0=nsc1[:, sl], in1=var_t[:, sl], op=ALU.mult)
                e.drain()
                e.tensor_scalar(out=nsc1[:, sl], in0=nsc1[:, sl], scalar1=-0.5, scalar2=1.5,
                                op0=ALU.mult, op1=ALU.add)
                e.drain()
                e.tensor_tensor(out=rstd_t[:, sl], in0=rstd_t[:, sl], in1=nsc1[:, sl], op=ALU.mult)
                e.drain()
        v.do("vector", f)

    dma("gpsimd", "d_pw", pwT[:], proj_wT_d.rearrange("(dt p) e -> p dt e", p=128))
    PWT_TGT = v.n["d_pw"]
    for i in range(2):
        dma("gpsimd", "d_ph2", phi_sb[:, 4 * i:4 * i + 4, :],
            phi_d.rearrange("(dt p) n -> p dt n", p=128)[:, 4 * i:4 * i + 4, :])
    PHI_TGT = v.n["d_ph2"]
    dma("gpsimd", "d_g", prow[:], projb_d[:, :])
    PROW_TGT = v.n["d_g"]
    W("sync", "d_at", v.n["d_at"])

    pmm_hist = []
    evac3_hist = []
    for b in range(B):
        if b == 0:
            W("tensor", "d_pw", PWT_TGT)
            W("tensor", "d_g", PROW_TGT)
        for ti8 in range(8):
            t_glob = 8 * b + ti8
            n3 = 8 * b + ti8
            psem = "d_p0" if n3 % 2 == 0 else "d_p1"
            if n3 >= 2:
                W("sync", "c_pe", pmm_hist[n3 - 2])
            dma("sync", psem, attn_t[:, ti8 % 2],
                attn_d[b].rearrange("(dt p) q -> p dt q", p=128)[:, :, 128 * ti8:128 * ti8 + 128])
            W("tensor", psem, v.n[psem])
            if n3 >= 2:
                W("tensor", "c_dv", evac3_hist[n3 - 2])

            def pmm(e, ti8=ti8):
                for c in range(2):
                    for dt in range(8):
                        e.matmul(out=ppo[ti8 % 2][:, 512 * c:512 * c + 512],
                                 lhsT=attn_t[:, ti8 % 2, dt, :],
                                 rhs=pwT[:, dt, 512 * c:512 * c + 512],
                                 start=(dt == 0), stop=False)
                    mm = e.matmul(out=ppo[ti8 % 2][:, 512 * c:512 * c + 512],
                                  lhsT=ones_row[0:1, 0:128],
                                  rhs=prow[0:1, 512 * c:512 * c + 512],
                                  start=False, stop=True)
                    if c == 1:
                        mm.then_inc(sems["c_pe"], 1)
            v.do("tensor", pmm)
            pe_p = v.inc("c_pe")
            pmm_hist.append(pe_p)
            W("vector", "c_pe", pe_p)
            v.do("vector", lambda e, ti8=ti8: e.tensor_copy(
                out=o_bf[:, ti8, :], in_=ppo[ti8 % 2][:, :]).then_inc(sems["c_dv"], 1))
            evac3_hist.append(v.inc("c_dv"))
        CD_EV = v.n["c_dv"]
        W("vector", "c_dv", CD_EV)
        W("scalar", "c_dv", CD_EV)
        lnstats_sbuf(lambda t: o_bf[:, t, :], 8)
        AC_LN2 = v.n["c_ac"]
        W("vector", "c_ac", AC_LN2)
        emit_rstd_sqrt(8)
        for pair in range(4):
            def appl(e, pair=pair, b=b):
                for k2 in range(2):
                    t = 2 * pair + k2
                    e.tensor_scalar(out=(scrA[:] if k2 == 0 else scrB_ap),
                                    in0=o_bf[:, t, :], scalar1=mean_t[:, t:t + 1],
                                    scalar2=rstd_t[:, t:t + 1], op0=ALU.subtract, op1=ALU.mult)
                e.drain()
                for k2 in range(2):
                    t = 2 * pair + k2
                    mm = e.tensor_tensor(out=Rt[:, 8 * b + t, :], in0=Rt[:, 8 * b + t, :],
                                         in1=(scrA[:] if k2 == 0 else scrB_ap), op=ALU.add)
                    if k2 == 1:
                        mm.then_inc(sems["c_dv"], 1)
                e.drain()
            v.do("vector", appl)
            v.inc("c_dv")
        W("scalar", "c_dv", v.n["c_dv"])
        lnstats_sbuf(lambda t, b=b: Rt[:, 8 * b + t, :], 8)
        W("vector", "c_ac", v.n["c_ac"])
        emit_rstd_sqrt(8)
        for ti8 in range(8):
            t_glob = 8 * b + ti8
            if ti8 >= 1:
                W("vector", "c_pe", v.n["c_pe"])
            v.do("vector", lambda e, ti8=ti8, t_glob=t_glob: e.tensor_scalar(
                out=xln3[:, t_glob, :], in0=Rt[:, t_glob, :], scalar1=mean_t[:, ti8:ti8 + 1],
                scalar2=rstd_t[:, ti8:ti8 + 1], op0=ALU.subtract,
                op1=ALU.mult).then_inc(sems["c_dv"], 1))
            cd_x3 = v.inc("c_dv")
            W("tensor", "c_dv", cd_x3)
            for dh in range(2):
                W("tensor", "c_dv", v.n["c_dv"])

                def tr3b(e, t_glob=t_glob, dh=dh, p=ptr2[dh]):
                    for jj in range(4):
                        dt = 4 * dh + jj
                        mm = e.transpose(out=p[:, 128 * jj:128 * jj + 128],
                                         in_=xln3[:, t_glob, 128 * dt:128 * dt + 128],
                                         identity=ident[:])
                        if jj == 3:
                            mm.then_inc(sems["c_pe"], 1)
                v.do("tensor", tr3b)
                pe_t3 = v.inc("c_pe")
                W("vector", "c_pe", pe_t3)

                def ev3(e, t_glob=t_glob, dh=dh, p=ptr2[dh]):
                    e.tensor_copy(
                        out=xln3T[:, 4 * dh:4 * dh + 4, 128 * t_glob:128 * t_glob + 128],
                        in_=p[:].rearrange("p (dt c) -> p dt c", c=128),
                    ).then_inc(sems["c_dv"], 1)
                v.do("vector", ev3)
                v.inc("c_dv")

    if debug and phases == 3:
        v.do("gpsimd", lambda e, x=v.n["c_dv"]: e.wait_ge(sems["c_dv"], x))
        dma("gpsimd", "d_dbg", dbg["dbg_xln3"][:, :], xln3[:, 0, :])
        W("gpsimd", "d_dbg", v.n["d_dbg"])

    barrier_all(3)
    v.flush(nc)
    p3.close()

    if phases < 4:
        p34.close()
        p36.close()
        es.close()
        return nc, dbg

    p4 = ExitStack()
    logits_q = p4.enter_context(nc.sbuf_tensor("logits_q", [128, 16, 512], BF16))
    disp_q = p4.enter_context(nc.sbuf_tensor("disp_q", [128, 16, 512], BF16))
    dsc = p4.enter_context(nc.sbuf_tensor("dsc", [128, 512], BF16))
    th = p4.enter_context(nc.sbuf_tensor("th", [128, 512], BF16))
    slscr = p4.enter_context(nc.sbuf_tensor("slscr", [128, D], F32))
    ppl = [p4.enter_context(nc.psum_tensor(f"ppl{i}", [128, 512], F32)) for i in range(2)]
    psl = [p4.enter_context(nc.psum_tensor(f"psl{i}", [128, D], F32)) for i in range(2)]

    W("tensor", "d_ph2", PHI_TGT)

    lev_hist = []
    sev_hist = []
    for qtr in range(4):
        for ti in range(NT):
            nl = 16 * qtr + ti
            if nl >= 2:
                W("tensor", "c_dv", lev_hist[nl - 2])

            def lmm(e, ti=ti, qtr=qtr):
                for dt in range(8):
                    mm = e.matmul(out=ppl[ti % 2][:, :],
                                  lhsT=xln3T[:, dt, 128 * ti:128 * ti + 128],
                                  rhs=phi_sb[:, dt, 512 * qtr:512 * qtr + 512],
                                  start=(dt == 0), stop=(dt == 7))
                    if dt == 7:
                        mm.then_inc(sems["c_pe"], 1)
            v.do("tensor", lmm)
            pe_l = v.inc("c_pe")
            W("vector", "c_pe", pe_l)
            v.do("vector", lambda e, ti=ti: e.tensor_copy(
                out=logits_q[:, ti, :], in_=ppl[ti % 2][:, :]).then_inc(sems["c_dv"], 1))
            lev_hist.append(v.inc("c_dv"))
        CD_LQ = v.n["c_dv"]
        for tp in range(8):
            def dsp1(e, tp=tp):
                e.drain()
                e.tensor_tensor(out=dsc[:], in0=logits_q[:, tp, :],
                                in1=logits_q[:, tp + 8, :],
                                op=ALU.subtract).then_inc(sems["c_dv"], 1)
            v.do("vector", dsp1)
            cd_d = v.inc("c_dv")
            W("scalar", "c_dv", cd_d)
            W("scalar", "c_ac", v.n["c_ac"])

            def dsp2(e):
                e.activation(out=th[:], in_=dsc[:], func=AF.Tanh,
                             bias=zero_b[:, 0:1], scale=0.5).then_inc(sems["c_ac"], 1)
            v.do("scalar", dsp2)
            ac_t = v.inc("c_ac")
            W("vector", "c_ac", ac_t)

            def dsp3(e, tp=tp):
                e.tensor_scalar(out=disp_q[:, tp, :], in0=th[:], scalar1=1.0,
                                scalar2=None, op0=ALU.add)
                e.tensor_scalar(out=disp_q[:, tp + 8, :], in0=th[:], scalar1=-1.0,
                                scalar2=1.0, op0=ALU.mult,
                                op1=ALU.add).then_inc(sems["c_dv"], 1)
            v.do("vector", dsp3)
            v.inc("c_dv")
        CD_DISP = v.n["c_dv"]
        W("tensor", "c_dv", CD_DISP)
        for b in range(B):
            for nlt in range(4):
                ns4 = 8 * qtr + 4 * b + nlt
                if ns4 >= 2:
                    W("tensor", "c_dv", sev_hist[ns4 - 2])

                def smm(e, b=b, nlt=nlt):
                    for c in range(2):
                        for mt in range(8):
                            mm = e.matmul(out=psl[nlt % 2][:, 512 * c:512 * c + 512],
                                          lhsT=disp_q[:, 8 * b + mt, 128 * nlt:128 * nlt + 128],
                                          rhs=xln3[:, 8 * b + mt, 512 * c:512 * c + 512],
                                          start=(mt == 0), stop=(mt == 7))
                            if c == 1 and mt == 7:
                                mm.then_inc(sems["c_pe"], 1)
                v.do("tensor", smm)
                pe_s = v.inc("c_pe")
                W("vector", "c_pe", pe_s)
                W("vector", "d_sl", v.n["d_sl"])
                v.do("vector", lambda e, nlt=nlt: e.tensor_copy(
                    out=slscr[:], in_=psl[nlt % 2][:, :]).then_inc(sems["c_dv"], 1))
                cd_s = v.inc("c_dv")
                sev_hist.append(cd_s)
                W("sync", "c_dv", cd_s)
                npg = 4 * qtr + nlt
                for eg in range(4):
                    row0 = 64 * (4 * npg + eg) + 32 * b
                    dma("sync", "d_sl", slots_in[row0:row0 + 32, :],
                        slscr[32 * eg:32 * eg + 32, :])
    W("gpsimd", "d_sl", v.n["d_sl"])

    def rs(e):
        e.collective_compute("ReduceScatter", ALU.add, replica_groups=rg,
                             ins=[slots_in[0:NP, :].opt()],
                             outs=[slots_out[0:NP // NCORE, :].opt()]).then_inc(sems["cc"])
        e.collective_compute("ReduceScatter", ALU.add, replica_groups=rg,
                             ins=[slots_in[NP:2 * NP, :].opt()],
                             outs=[slots_out[NP // NCORE:2 * NP // NCORE, :].opt()]).then_inc(sems["cc"])
    v.do("gpsimd", rs)
    v.inc("cc", 2)

    if debug and phases == 4:
        v.do("gpsimd", lambda e, x=v.n["cc"]: e.wait_ge(sems["cc"], x))
        dma("gpsimd", "d_dbg", dbg["dbg_disp"][:, :], slots_out[0:128, :])
        W("gpsimd", "d_dbg", v.n["d_dbg"])

    barrier_all(4)
    v.flush(nc)
    p4.close()
    p34.close()

    if phases < 5:
        p36.close()
        es.close()
        return nc, dbg

    p5 = ExitStack()
    slots_sb = p5.enter_context(nc.sbuf_tensor("slots_sb", [128, 4, D], BF16))
    slotsT = p5.enter_context(nc.sbuf_tensor("slotsT", [128, 8, 512], BF16))
    wbuf = [p5.enter_context(nc.sbuf_tensor(f"wbuf{i}", [128, 8, D], BF16)) for i in range(2)]
    b2row = p5.enter_context(nc.sbuf_tensor("b2row", [1, EPC * D], BF16))
    ldb = p5.enter_context(nc.sbuf_tensor("ldb", [1, 2048], F32))
    ytmp = p5.enter_context(nc.sbuf_tensor("ytmp", [64, D], BF16))
    ptr5 = [p5.enter_context(nc.psum_tensor(f"ptr5{i}", [128, 512], BF16)) for i in range(2)]
    py = [p5.enter_context(nc.psum_tensor(f"py{i}", [64, D], F32)) for i in range(2)]

    W("gpsimd", "cc", 4)
    dma("gpsimd", "d_g", slots_sb[:, 0:2, :], slots_out[0:256, :].rearrange("(r p) d -> p r d", p=128))
    SLO_TGT1 = v.n["d_g"]
    W("gpsimd", "cc", 5)
    dma("gpsimd", "d_g2", slots_sb[:, 2:4, :], slots_out[256:512, :].rearrange("(r p) d -> p r d", p=128))
    SLO_TGT = v.n["d_g2"]
    for ch in range(4):
        if ch > 0:
            W("sync", "c_dv", v.n["c_dv"])
        dma("sync", "d_m", ldb[:], bsl[:, 2048 * ch:2048 * ch + 2048])
        W("vector", "d_m", v.n["d_m"])
        v.do("vector", lambda e, ch=ch: e.tensor_scalar(
            out=b2row[0:1, 2048 * ch:2048 * ch + 2048], in0=ldb[:], scalar1=2.0,
            scalar2=None, op0=ALU.mult).then_inc(sems["c_dv"], 1))
        v.inc("c_dv")
    CD_B2 = v.n["c_dv"]
    W("tensor", "d_g", SLO_TGT1)
    for r4 in range(4):
        if r4 == 2:
            W("tensor", "d_g2", SLO_TGT)
        for dh in range(2):
            W("tensor", "c_dv", v.n["c_dv"])

            def tr5(e, r4=r4, dh=dh, p=ptr5[dh]):
                for jj in range(4):
                    dt = 4 * dh + jj
                    mm = e.transpose(out=p[:, 128 * jj:128 * jj + 128],
                                     in_=slots_sb[:, r4, 128 * dt:128 * dt + 128],
                                     identity=ident[:])
                    if jj == 3:
                        mm.then_inc(sems["c_pe"], 1)
            v.do("tensor", tr5)
            pe_t5 = v.inc("c_pe")
            W("vector", "c_pe", pe_t5)

            def ev5(e, r4=r4, dh=dh, p=ptr5[dh]):
                e.tensor_copy(
                    out=slotsT[:, 4 * dh:4 * dh + 4, 128 * r4:128 * r4 + 128],
                    in_=p[:].rearrange("p (dt c) -> p dt c", c=128),
                ).then_inc(sems["c_dv"], 1)
            v.do("vector", ev5)
            v.inc("c_dv")
    CD_SLT = v.n["c_dv"]
    W("gpsimd", "d_w", v.n["d_w"])
    W("tensor", "c_dv", CD_SLT)
    W("tensor", "c_dv", CD_B2)
    for eI in range(EPC):
        wsem = "d_w0" if eI % 2 == 0 else "d_w1"
        if eI >= 2:
            W("gpsimd", "c_pe", v.n["c_pe"])
        dma("gpsimd", wsem, wbuf[eI % 2][:], wbf[eI].rearrange("(dt p) d2 -> p dt d2", p=128))
        W("tensor", wsem, v.n[wsem])
        if eI >= 2:
            W("tensor", "c_dv", v.n["c_dv"])

        def ymm(e, eI=eI):
            for c in range(2):
                for dt in range(8):
                    e.matmul(out=py[eI % 2][:, 512 * c:512 * c + 512],
                             lhsT=slotsT[:, dt, 64 * eI:64 * eI + 64],
                             rhs=wbuf[eI % 2][:, dt, 512 * c:512 * c + 512],
                             start=(dt == 0), stop=False)
                mm = e.matmul(out=py[eI % 2][:, 512 * c:512 * c + 512],
                              lhsT=ones_row[0:1, 0:64],
                              rhs=b2row[0:1, 1024 * eI + 512 * c:1024 * eI + 512 * c + 512],
                              start=False, stop=True)
                if c == 1:
                    mm.then_inc(sems["c_pe"], 1)
        v.do("tensor", ymm)
        pe_y = v.inc("c_pe")
        W("vector", "c_pe", pe_y)
        W("vector", "d_y", v.n["d_y"])

        def yev(e, eI=eI):
            e.tensor_copy(out=ytmp[:], in_=py[eI % 2][:, :]).then_inc(sems["c_dv"], 1)
        v.do("vector", yev)
        cd_y = v.inc("c_dv")
        W("sync", "c_dv", cd_y)
        dma("sync", "d_y", y_in[64 * eI:64 * eI + 64, :], ytmp[:])
        if eI == 3 or eI == 7:
            W("gpsimd", "d_y", v.n["d_y"])
            lo = 0 if eI == 3 else 256

            def agy(e, lo=lo):
                e.collective_compute("AllGather", ALU.bypass, replica_groups=rg,
                                     ins=[y_in[lo:lo + 256, :].opt()],
                                     outs=[y_ag[8 * lo:8 * lo + 2048, :].opt()]).then_inc(sems["cc"])
            v.do("gpsimd", agy)
            v.inc("cc")

    if debug and phases == 5:
        v.do("gpsimd", lambda e, x=v.n["cc"]: e.wait_ge(sems["cc"], x))
        dma("gpsimd", "d_dbg", dbg["dbg_y"][:, :], y_ag[0:128, :])
        W("gpsimd", "d_dbg", v.n["d_dbg"])

    barrier_all(5)
    v.flush(nc)
    p5.close()

    if phases < 6:
        p36.close()
        es.close()
        return nc, dbg

    p6 = ExitStack()
    y_b = p6.enter_context(nc.sbuf_tensor("y_b", [128, 16, D], BF16))
    ct_sb = p6.enter_context(nc.sbuf_tensor("ct_sb", [128, 16, D], BF16))
    co_f32 = p6.enter_context(nc.sbuf_tensor("co_f32", [128, 2, D], F32))
    scr4 = sq_scr[:].bitcast(F32)
    pct = [p6.enter_context(nc.psum_tensor(f"pct{i}", [128, D], F32)) for i in range(2)]
    pco = [p6.enter_context(nc.psum_tensor(f"pco{i}", [128, D], F32)) for i in range(2)]

    cte_hist = []
    cev_hist = []
    for b in range(B):
        W("gpsimd", "cc", 7)
        if b == 1:
            W("gpsimd", "c_pe", v.n["c_pe"])
        for g in range(16):
            for eg in range(4):
                e_glob = 4 * g + eg
                if e_glob < 32:
                    r_, l_ = e_glob // 4, e_glob % 4
                    row0 = 256 * r_ + 64 * l_ + 32 * b
                else:
                    r_, l_ = (e_glob - 32) // 4, (e_glob - 32) % 4
                    row0 = 2048 + 256 * r_ + 64 * l_ + 32 * b
                dma("gpsimd", "d_yb", y_b[32 * eg:32 * eg + 32, g, :],
                    y_ag[row0:row0 + 32, :])
        YB_TGT = v.n["d_yb"]
        for npt in range(NT):
            nc6 = 16 * b + npt
            if nc6 >= 2:
                W("tensor", "c_ac", cte_hist[nc6 - 2])

            def ctm(e, npt=npt, b=b):
                for c in range(2):
                    for dt in range(8):
                        mm = e.matmul(out=pct[npt % 2][:, 512 * c:512 * c + 512],
                                      lhsT=phi_sb[:, dt, 128 * npt:128 * npt + 128],
                                      rhs=xln3T[:, dt, MLOC * b + 512 * c:MLOC * b + 512 * c + 512],
                                      start=(dt == 0), stop=(dt == 7))
                        if c == 1 and dt == 7:
                            mm.then_inc(sems["c_pe"], 1)
            v.do("tensor", ctm)
            pe_c = v.inc("c_pe")
            W("scalar", "c_pe", pe_c)
            W("scalar", "c_ac", v.n["c_ac"])

            def cte(e, npt=npt):
                e.activation(out=ct_sb[:, npt, :], in_=pct[npt % 2][:, :], func=AF.Exp,
                             bias=negc[:, 0:1], scale=1.0).then_inc(sems["c_ac"], 1)
            v.do("scalar", cte)
            cte_hist.append(v.inc("c_ac"))
        AC_CT = v.n["c_ac"]
        W("tensor", "c_ac", AC_CT)
        W("tensor", "d_yb", YB_TGT)
        for ti8 in range(8):
            t_glob = 8 * b + ti8
            n6 = 8 * b + ti8
            if n6 >= 2:
                W("tensor", "c_dv", cev_hist[n6 - 2])

            def com(e, ti8=ti8):
                for c in range(2):
                    for npt in range(NT):
                        mm = e.matmul(out=pco[ti8 % 2][:, 512 * c:512 * c + 512],
                                      lhsT=ct_sb[:, npt, 128 * ti8:128 * ti8 + 128],
                                      rhs=y_b[:, npt, 512 * c:512 * c + 512],
                                      start=(npt == 0), stop=(npt == 15))
                        if c == 1 and npt == 15:
                            mm.then_inc(sems["c_pe"], 1)
            v.do("tensor", com)
            pe_co = v.inc("c_pe")
            W("vector", "c_pe", pe_co)
            W("vector", "c_ac", v.n["c_ac"])

            def cev(e, ti8=ti8):
                e.tensor_copy(out=co_f32[:, ti8 % 2, :],
                              in_=pco[ti8 % 2][:, :]).then_inc(sems["c_dv"], 1)
            v.do("vector", cev)
            cd_co = v.inc("c_dv")
            cev_hist.append(cd_co)
            W("vector", "c_dv", cd_co)
            v.do("vector", lambda e, ti8=ti8: e.tensor_reduce(
                out=stat[:, ti8:ti8 + 1], in_=co_f32[:, ti8 % 2, :], axis=AX, op=ALU.add))
            W("scalar", "c_dv", cd_co)
            W("scalar", "c_ac", v.n["c_ac"])

            W("scalar", "c_dv", v.n["c_dv"])

            def csq(e, ti8=ti8):
                e.activation(out=sq_scr[:, 0:D], in_=co_f32[:, ti8 % 2, :], func=AF.Square,
                             bias=zero_b[:, 0:1], scale=1.0,
                             accum_out=stat2[:, ti8:ti8 + 1]).then_inc(sems["c_ac"], 1)
            v.do("scalar", csq)
            ac_c = v.inc("c_ac")
            if ti8 % 2 == 1:
                W("vector", "c_ac", ac_c)
                ln_rstd(2, c0=ti8 - 1)

                def capply(e, ti8=ti8, b=b):
                    for k2 in range(2):
                        tt_ = ti8 - 1 + k2
                        tg = 8 * b + tt_
                        e.drain()
                        e.tensor_scalar(out=scr4, in0=co_f32[:, tt_ % 2, :],
                                        scalar1=mean_t[:, tt_:tt_ + 1],
                                        scalar2=rstd_t[:, tt_:tt_ + 1],
                                        op0=ALU.subtract, op1=ALU.mult)
                        e.drain()
                        e.tensor_tensor(out=Rt[:, tg, :], in0=Rt[:, tg, :],
                                        in1=scr4, op=ALU.add).then_inc(sems["c_dv"], 1)
                    e.drain()
                v.do("vector", capply)
                v.inc("c_dv", 2)
                cd_fin = v.n["c_dv"]
                W("sync", "c_dv", cd_fin)
                tg0 = 8 * b + ti8 - 1
                dma("sync", "d_out", yout[128 * tg0:128 * tg0 + 128, :], Rt[:, tg0, :])
                dma("sync", "d_out", yout[128 * tg0 + 128:128 * tg0 + 256, :], Rt[:, tg0 + 1, :])

    W("sync", "d_out", v.n["d_out"])

    barrier_all(6)
    v.flush(nc)
    p6.close()
    p36.close()
    es.close()
    return nc, dbg


def host_prep(L0_attn_x, L0_attn_qkv_w, L0_attn_qkv_b, L0_attn_proj_w,
              L0_attn_proj_b, L0_ffn_phi, L0_ffn_W, L0_ffn_b):
    x = np.asarray(L0_attn_x, np.float32)
    qkv_w = np.asarray(L0_attn_qkv_w, np.float32)
    qkv_b = np.asarray(L0_attn_qkv_b, np.float32)
    proj_w = np.asarray(L0_attn_proj_w, np.float32)
    proj_b = np.asarray(L0_attn_proj_b, np.float32)
    phi = np.asarray(L0_ffn_phi, np.float32)
    Wx = np.asarray(L0_ffn_W, np.float32)
    bx = np.asarray(L0_ffn_b, np.float32)

    qkv_wT = np.ascontiguousarray(qkv_w.T)
    proj_wT = np.ascontiguousarray(proj_w.T)
    phi2 = np.ascontiguousarray(phi.reshape(D, NP))
    ident = np.eye(128, dtype=ml_dtypes.bfloat16)
    in_maps = []
    for c in range(NCORE):
        xsl = np.ascontiguousarray(x[:, MLOC * c:MLOC * (c + 1), :].reshape(TLOC, D))
        in_maps.append({
            "xs": xsl,
            "qkv_wT": qkv_wT,
            "proj_wT": proj_wT,
            "phi2": phi2,
            "wsl": np.ascontiguousarray(np.concatenate(
                [Wx[4 * c:4 * c + 4], Wx[32 + 4 * c:32 + 4 * c + 4]], axis=0)),
            "bsl": np.ascontiguousarray(np.concatenate(
                [bx[4 * c:4 * c + 4], bx[32 + 4 * c:32 + 4 * c + 4]], axis=0).reshape(1, EPC * D)),
            "qkvb": qkv_b.reshape(1, 3 * D),
            "projb": proj_b.reshape(1, D),
            "ident": ident,
        })
    return in_maps


def kernel(**inputs):
    in_maps = host_prep(**inputs)
    nc, _ = build_nc(debug=False)
    res = run_bass_kernel_spmd(nc, in_maps, core_ids=list(range(NCORE)))
    out = np.empty((B, NCORE * MLOC, D), np.float32)
    for c in range(NCORE):
        o = res.results[c]["yout"].reshape(B, MLOC, D)
        out[:, MLOC * c:MLOC * (c + 1), :] = o
    return out


# version 14
# speedup vs baseline: 1.1779x; 1.0165x over previous
import numpy as np
import ml_dtypes
from collections import defaultdict
from contextlib import ExitStack
import concourse.bass as bass
import concourse.mybir as mybir
import concourse.bass_utils as _bass_utils
from concourse.bass_utils import run_bass_kernel_spmd

_orig_run_command = _bass_utils.run_command

def _run_command_ldw(cmd, *a, **kw):
    if isinstance(cmd, list):
        cmd = ["--enable-ldw-opt=true" if c == "--enable-ldw-opt=false" else c for c in cmd]
    return _orig_run_command(cmd, *a, **kw)

if getattr(_bass_utils.run_command, "__name__", "") != "_run_command_ldw":
    _bass_utils.run_command = _run_command_ldw

F32 = mybir.dt.float32
BF16 = mybir.dt.bfloat16
I32 = mybir.dt.int32
AF = mybir.ActivationFunctionType
ALU = mybir.AluOpType
AX = mybir.AxisListType.X

NCORE = 8
B = 2
D = 1024
H = 16
HP = 8
MLOC = 1024
TLOC = 2048
NT = 16
NKT = 64
NP = 2048
EPC = 8
SHIFT = 600.0
CFFN = 4.0
EPS = 1e-5
MAGIC = 0x5F3759DF

ENGS = ("sync", "gpsimd", "vector", "scalar", "tensor")


class Sched:

    def __init__(self):
        self.q = {e: [] for e in ENGS}
        self.n = defaultdict(int)

    def do(self, eng, fn):
        self.q[eng].append(fn)

    def inc(self, sem, k=1):
        self.n[sem] += k
        return self.n[sem]

    def flush(self, nc):
        with nc.Block() as blk:
            for name in ENGS:
                fns = self.q[name]

                def runner(e, fns=fns):
                    for f in fns:
                        f(e)
                getattr(blk, name)(runner)
        self.q = {e: [] for e in ENGS}


def newton_rsqrt(v, vec_ap, y_ap, t1_ap, ti1, ti2, iters=4):
    def f(e):
        e.drain()
        e.tensor_scalar(out=ti1, in0=vec_ap.bitcast(I32), scalar1=1, scalar2=None,
                        op0=ALU.arith_shift_right)
        e.drain()
        e.tensor_scalar(out=ti2, in0=ti1, scalar1=-1, scalar2=int(MAGIC),
                        op0=ALU.mult, op1=ALU.add)
        e.drain()
        e.tensor_copy(out=y_ap.bitcast(I32), in_=ti2)
        e.drain()
        for _ in range(iters):
            e.tensor_tensor(out=t1_ap, in0=y_ap, in1=y_ap, op=ALU.mult)
            e.drain()
            e.tensor_tensor(out=t1_ap, in0=t1_ap, in1=vec_ap, op=ALU.mult)
            e.drain()
            e.tensor_scalar(out=t1_ap, in0=t1_ap, scalar1=-0.5, scalar2=1.5,
                            op0=ALU.mult, op1=ALU.add)
            e.drain()
            e.tensor_tensor(out=y_ap, in0=y_ap, in1=t1_ap, op=ALU.mult)
            e.drain()
    v.do("vector", f)


def build_nc(debug=False, phases=99, stage=99, lite=0):
    nc = bass.Bass("TRN2", target_bir_lowering=False)

    xs = nc.dram_tensor("xs", [TLOC, D], F32, kind="ExternalInput").ap()
    qkv_wT = nc.dram_tensor("qkv_wT", [D, 3 * D], F32, kind="ExternalInput").ap()
    proj_wT_d = nc.dram_tensor("proj_wT", [D, D], F32, kind="ExternalInput").ap()
    phi_d = nc.dram_tensor("phi2", [D, NP], F32, kind="ExternalInput").ap()
    wsl = nc.dram_tensor("wsl", [EPC, D, D], F32, kind="ExternalInput").ap()
    bsl = nc.dram_tensor("bsl", [1, EPC * D], F32, kind="ExternalInput").ap()
    qkvb = nc.dram_tensor("qkvb", [1, 3 * D], F32, kind="ExternalInput").ap()
    projb_d = nc.dram_tensor("projb", [1, D], F32, kind="ExternalInput").ap()
    ident_in = nc.dram_tensor("ident", [128, 128], BF16, kind="ExternalInput").ap()
    yout = nc.dram_tensor("yout", [TLOC, D], F32, kind="ExternalOutput").ap()

    kvkin = nc.dram_tensor("kvkin", [H * 65, TLOC], BF16).ap()
    kvkout = nc.dram_tensor("kvkout", [NCORE * H * 65, TLOC], BF16, addr_space="Shared").ap()
    kvvin = nc.dram_tensor("kvvin", [TLOC, D], BF16).ap()
    kvvout = nc.dram_tensor("kvvout", [NCORE * TLOC, D], BF16, addr_space="Shared").ap()
    q_d = nc.dram_tensor("q_d", [D, TLOC], BF16).ap()
    m8_d = nc.dram_tensor("m8_d", [H, TLOC], BF16).ap()
    kmax_in = nc.dram_tensor("kmax_in", [2, H], F32).ap()
    kmax_out = nc.dram_tensor("kmax_out", [2, H], F32).ap()
    attn_d = nc.dram_tensor("attn_d", [B, D, MLOC], BF16).ap()
    wbf = nc.dram_tensor("wbf", [EPC, D, D], BF16).ap()
    slots_in = nc.dram_tensor("slots_in", [NP * B, D], F32).ap()
    slots_out = nc.dram_tensor("slots_out", [NP * B // NCORE, D], F32).ap()
    y_in = nc.dram_tensor("y_in", [EPC * B * 32, D], BF16).ap()
    y_ag = nc.dram_tensor("y_ag", [NP * B, D], BF16, addr_space="Shared").ap()
    bc_d = nc.dram_tensor("bc_d", [1, 1024], F32).ap()

    dbg = {}
    if debug:
        for nm, shp in [("dbg_xln1T", [128, 2048]), ("dbg_k", [128, 2048]),
                        ("dbg_q", [128, 2048]), ("dbg_m8", [2, 2048]),
                        ("dbg_attn", [128, 1024]), ("dbg_xln3", [128, 1024]),
                        ("dbg_disp", [128, 1024]), ("dbg_ct", [128, 1024]),
                        ("dbg_y", [128, 1024])]:
            dbg[nm] = nc.dram_tensor(nm, shp, F32, kind="ExternalOutput").ap()

    rg = [list(range(NCORE))]
    es = ExitStack()
    E = es.enter_context

    sems = {}
    for s in ["ph", "d_x", "d_w", "d_m", "d_i", "d_kmx", "d_kmg", "d_kst", "d_vst", "d_qst", "d_qs1", "d_kld", "d_kld1",
              "d_vld", "d_vld1", "d_qld", "d_qld1", "d_at", "d_sl", "d_y", "d_yb", "d_out", "d_dbg",
              "d_bc", "d_g", "d_g2", "d_pw", "d_ph2", "d_p0", "d_p1", "d_w0", "d_w1", "cc", "c_pe", "c_dv", "c_ac", "c_sc", "c_ex", "c_av", "c_nr",
              "c_rc", "c_bc", "c_mul"]:
        sems[s] = E(nc.semaphore(s))

    v = Sched()

    def W(eng, sem, val):
        v.do(eng, lambda e, s=sems[sem], x=val: e.wait_ge(s, x))

    def dma(eng, sem, out, in_):
        def f(e, o=out, i=in_, s=sems[sem]):
            e.dma_start(out=o, in_=i).then_inc(s, 16)
        v.do(eng, f)
        return v.inc(sem, 16)


    def emit_rstd_sqrt(n, c0=0):
        def alg(e):
            sl = slice(c0, c0 + n)
            e.drain()
            e.tensor_scalar(out=mean_t[:, sl], in0=stat[:, sl], scalar1=1.0 / D,
                            scalar2=None, op0=ALU.mult)
            e.tensor_scalar(out=nsc1[:, sl], in0=stat2[:, sl], scalar1=1.0 / D,
                            scalar2=EPS, op0=ALU.mult, op1=ALU.add)
            e.drain()
            e.tensor_tensor(out=var_t[:, sl], in0=mean_t[:, sl], in1=mean_t[:, sl], op=ALU.mult)
            e.drain()
            e.tensor_tensor(out=var_t[:, sl], in0=nsc1[:, sl], in1=var_t[:, sl],
                            op=ALU.subtract).then_inc(sems["c_dv"], 1)
        v.do("vector", alg)
        cd_v = v.inc("c_dv")
        W("scalar", "c_dv", cd_v)
        W("scalar", "c_ac", v.n["c_ac"])

        def sq_(e):
            e.activation(out=rstd_t[:, c0:c0 + n], in_=var_t[:, c0:c0 + n], func=AF.Sqrt,
                         bias=zero_b[:, 0:1], scale=1.0).then_inc(sems["c_ac"], 1)
        v.do("scalar", sq_)
        ac_s = v.inc("c_ac")
        W("vector", "c_ac", ac_s)

        def rc(e):
            e.reciprocal(out=rstd_t[:, c0:c0 + n], in_=rstd_t[:, c0:c0 + n])
            e.drain()
        v.do("vector", rc)

    def barrier_all(phase_idx):
        for e in ENGS:
            v.do(e, lambda x, s=sems["ph"]: x.sem_inc(s, 1))
        v.inc("ph", 5)
        tgt = v.n["ph"]
        for e in ENGS:
            v.do(e, lambda x, s=sems["ph"], t=tgt: x.wait_ge(s, t))

    Rt = E(nc.sbuf_tensor("Rt", [128, NT, D], F32))
    ident = E(nc.sbuf_tensor("identb", [128, 128], BF16))
    ones_row = E(nc.sbuf_tensor("ones_row", [1, 512], BF16))
    negc = E(nc.sbuf_tensor("negc", [128, 1], F32))
    zero_b = E(nc.sbuf_tensor("zero_b", [128, 1], F32))
    stat = E(nc.sbuf_tensor("stat", [128, 16], F32))
    stat2 = E(nc.sbuf_tensor("stat2", [128, 16], F32))
    mean_t = E(nc.sbuf_tensor("mean_t", [128, 16], F32))
    var_t = E(nc.sbuf_tensor("var_t", [128, 16], F32))
    rstd_t = E(nc.sbuf_tensor("rstd_t", [128, 16], F32))
    nsc1 = E(nc.sbuf_tensor("nsc1", [128, 16], F32))
    nsi1 = E(nc.sbuf_tensor("nsi1", [128, 16], I32))
    nsi2 = E(nc.sbuf_tensor("nsi2", [128, 16], I32))
    sq_scr = E(nc.sbuf_tensor("sq_scr", [128, 2048], BF16))

    p1 = ExitStack()
    qwT = p1.enter_context(nc.sbuf_tensor("qwT", [128, 8, 3 * D], BF16))
    xln1 = p1.enter_context(nc.sbuf_tensor("xln1", [128, 2, D], BF16))
    xln1T = p1.enter_context(nc.sbuf_tensor("xln1T", [128, 8, TLOC], BF16))
    ktmp = p1.enter_context(nc.sbuf_tensor("ktmp", [128, 2, TLOC], BF16))
    qtmp = p1.enter_context(nc.sbuf_tensor("qtmp", [128, 2, TLOC], BF16))
    vtmp = qtmp[:, :, 0:D]
    hind = p1.enter_context(nc.sbuf_tensor("hind", [128, 2], BF16))
    qkb_row = p1.enter_context(nc.sbuf_tensor("qkb_row", [1, 3 * D], BF16))
    kml = p1.enter_context(nc.sbuf_tensor("kml", [2, H], F32))
    kmt = p1.enter_context(nc.sbuf_tensor("kmt", [2, 2], F32))
    kmg = p1.enter_context(nc.sbuf_tensor("kmg", [2, H], F32))
    kng = p1.enter_context(nc.sbuf_tensor("kng", [2, H], F32))
    qn2 = p1.enter_context(nc.sbuf_tensor("qn2", [2, TLOC], F32))
    qnw = p1.enter_context(nc.sbuf_tensor("qnw", [2, MLOC], F32))
    qns = p1.enter_context(nc.sbuf_tensor("qns", [2, MLOC], F32))
    qni1 = p1.enter_context(nc.sbuf_tensor("qni1", [2, MLOC], I32))
    m8s = p1.enter_context(nc.sbuf_tensor("m8s", [2, 2, TLOC], BF16))

    psq = [p1.enter_context(nc.psum_tensor(f"psq{i}", [128, TLOC], F32)) for i in range(2)]
    ptrv = psq[1][:].bitcast(BF16)
    ptr0 = ptrv[:, 0:512]
    ptr1 = ptrv[:, 1024:1536]

    def _cut():
        barrier_all(1)
        v.flush(nc)
        p1.close()
        es.close()
        return nc, dbg

    xsr = xs.rearrange("(t p) d -> p t d", p=128)
    dma("sync", "d_x", Rt[:, 0:8, :], xsr[:, 0:8, :])
    dma("sync", "d_x", Rt[:, 8:16, :], xsr[:, 8:16, :])
    dma("sync", "d_i", ident[:], ident_in[:, :])
    qwr = qkv_wT.rearrange("(dt p) e -> p dt e", p=128)
    for i in range(4):
        dma("gpsimd", "d_w", qwT[:, 2 * i:2 * i + 2, :], qwr[:, 2 * i:2 * i + 2, :])

    def consts(e):
        e.memset(ones_row[:], 1.0).then_inc(sems["c_dv"], 1)
        e.memset(negc[:], -CFFN).then_inc(sems["c_dv"], 1)
        e.memset(zero_b[:], 0.0).then_inc(sems["c_dv"], 1)
        e.memset(hind[:], 0.0)
        e.drain()
        e.memset(hind[0:64, 0:1], 1.0).then_inc(sems["c_dv"], 1)
        e.memset(hind[64:128, 1:2], 1.0).then_inc(sems["c_dv"], 1)
        e.drain()
    v.do("vector", consts)
    CD_CONST = v.inc("c_dv", 5)

    dma("gpsimd", "d_g", qkb_row[:], qkvb[:, :])
    QKB_TGT = v.n["d_g"]

    W("sync", "c_dv", CD_CONST)
    for h in range(H):
        for ch4 in range(4):
            dma("sync", "d_kst", kvkin[h * 65 + 64:h * 65 + 65, 512 * ch4:512 * ch4 + 512],
                ones_row[0:1, :])

    W("vector", "d_x", 32)
    W("scalar", "d_x", 32)
    W("scalar", "c_dv", CD_CONST)
    for t in range(NT):
        v.do("vector", lambda e, t=t: e.tensor_reduce(
            out=stat[:, t:t + 1], in_=Rt[:, t, :], axis=AX, op=ALU.add))
        if t > 0:
            W("scalar", "c_ac", v.n["c_ac"])
        def sq(e, t=t):
            e.activation(out=sq_scr[:, 0:D], in_=Rt[:, t, :], func=AF.Square,
                         bias=zero_b[:, 0:1], scale=1.0,
                         accum_out=stat2[:, t:t + 1]).then_inc(sems["c_ac"], 1)
        v.do("scalar", sq)
        v.inc("c_ac", 1)
    AC_LN1 = v.n["c_ac"]

    def ln_finish(e, n=NT):
        e.drain()
        e.tensor_scalar(out=mean_t[:, 0:n], in0=stat[:, 0:n], scalar1=1.0 / D,
                        scalar2=None, op0=ALU.mult)
        e.drain()
        e.tensor_tensor(out=var_t[:, 0:n], in0=mean_t[:, 0:n], in1=mean_t[:, 0:n], op=ALU.mult)
        e.drain()
        e.tensor_scalar(out=nsc1[:, 0:n], in0=stat2[:, 0:n], scalar1=1.0 / D,
                        scalar2=None, op0=ALU.mult)
        e.drain()
        e.tensor_tensor(out=var_t[:, 0:n], in0=nsc1[:, 0:n], in1=var_t[:, 0:n], op=ALU.subtract)
        e.drain()
        e.tensor_scalar(out=var_t[:, 0:n], in0=var_t[:, 0:n], scalar1=EPS, scalar2=None, op0=ALU.add)
        e.drain()
        e.tensor_scalar(out=nsi1[:, 0:n], in0=var_t[:, 0:n].bitcast(I32), scalar1=1,
                        scalar2=None, op0=ALU.arith_shift_right)
        e.drain()
        e.tensor_scalar(out=nsi2[:, 0:n], in0=nsi1[:, 0:n], scalar1=-1, scalar2=int(MAGIC),
                        op0=ALU.mult, op1=ALU.add)
        e.drain()
        e.tensor_copy(out=rstd_t[:, 0:n].bitcast(I32), in_=nsi2[:, 0:n])
        e.drain()
        for _ in range(4):
            e.tensor_tensor(out=nsc1[:, 0:n], in0=rstd_t[:, 0:n], in1=rstd_t[:, 0:n], op=ALU.mult)
            e.drain()
            e.tensor_tensor(out=nsc1[:, 0:n], in0=nsc1[:, 0:n], in1=var_t[:, 0:n], op=ALU.mult)
            e.drain()
            e.tensor_scalar(out=nsc1[:, 0:n], in0=nsc1[:, 0:n], scalar1=-0.5, scalar2=1.5,
                            op0=ALU.mult, op1=ALU.add)
            e.drain()
            e.tensor_tensor(out=rstd_t[:, 0:n], in0=rstd_t[:, 0:n], in1=nsc1[:, 0:n], op=ALU.mult)
            e.drain()

    if stage < 2:
        return _cut()
    W("vector", "c_ac", AC_LN1)
    emit_rstd_sqrt(NT)
    W("tensor", "d_i", 16)
    ptrs = [ptr0, ptr1]
    tr1_hist = []
    ev1_hist = []
    for t in range(NT):
        if t >= 2:
            W("vector", "c_pe", tr1_hist[2 * (t - 2) + 1])
        v.do("vector", lambda e, t=t: e.tensor_scalar(
            out=xln1[:, t % 2, :], in0=Rt[:, t, :], scalar1=mean_t[:, t:t + 1],
            scalar2=rstd_t[:, t:t + 1], op0=ALU.subtract,
            op1=ALU.mult).then_inc(sems["c_dv"], 1))
        cd_ap = v.inc("c_dv")
        W("tensor", "c_dv", cd_ap)
        for dh in range(2):
            g1 = 2 * t + dh
            if g1 >= 2:
                W("tensor", "c_dv", ev1_hist[g1 - 2])

            def tr(e, t=t, dh=dh, p=ptrs[dh]):
                for j in range(4):
                    dt = 4 * dh + j
                    mm = e.transpose(out=p[:, 128 * j:128 * j + 128],
                                     in_=xln1[:, t % 2, 128 * dt:128 * dt + 128],
                                     identity=ident[:])
                    if j == 3:
                        mm.then_inc(sems["c_pe"], 1)
            v.do("tensor", tr)
            pe_g = v.inc("c_pe")
            tr1_hist.append(pe_g)
            W("vector", "c_pe", pe_g)

            def ev(e, t=t, dh=dh, p=ptrs[dh]):
                e.tensor_copy(
                    out=xln1T[:, 4 * dh:4 * dh + 4, 128 * t:128 * t + 128],
                    in_=p.rearrange("p (dt c) -> p dt c", c=128),
                ).then_inc(sems["c_dv"], 1)
            v.do("vector", ev)
            ev1_hist.append(v.inc("c_dv"))
    CD_TRE = v.n["c_dv"]

    if stage < 3:
        return _cut()
    W("tensor", "c_dv", CD_TRE)
    W("tensor", "d_w", 64)
    W("tensor", "d_g", QKB_TGT)

    def qk_eti(eti_list, tmp, kind):
        kn_hist = []
        ev_hist = []
        for idx, eti in enumerate(eti_list):
            if idx >= 2:
                W("tensor", "c_dv", ev_hist[idx - 2])
            else:
                W("tensor", "c_dv", v.n["c_dv"])

            def mms(e, eti=eti, i=idx):
                for dt in range(8):
                    for c in range(4):
                        e.matmul(out=psq[i % 2][:, 512 * c:512 * c + 512],
                                 lhsT=qwT[:, dt, 128 * eti:128 * eti + 128],
                                 rhs=xln1T[:, dt, 512 * c:512 * c + 512],
                                 start=(dt == 0), stop=False)
                for c in range(4):
                    mm = e.matmul(out=psq[i % 2][:, 512 * c:512 * c + 512],
                                  lhsT=qkb_row[0:1, 128 * eti:128 * eti + 128],
                                  rhs=ones_row[0:1, 0:512],
                                  start=False, stop=True)
                    if c == 3:
                        mm.then_inc(sems["c_pe"], 1)
            v.do("tensor", mms)
            pe_m = v.inc("c_pe")
            W("vector", "c_pe", pe_m)
            if kind == "Q" and idx < 2:
                W("vector", "d_vst", 16 * NT)
            if idx >= 2:
                if kind == "K":
                    W("vector", "d_kst", v.n["d_kst"])
                else:
                    W("vector", qst_hist[idx - 2][0], qst_hist[idx - 2][1])
                W("vector", "c_ac", v.n["c_ac"])
            v.do("vector", lambda e, t=tmp, i=idx: e.tensor_copy(
                out=t[:, i % 2, :], in_=psq[i % 2][:, :]).then_inc(sems["c_dv"], 1))
            cd_e = v.inc("c_dv")
            if lite < 2:
                W("scalar", "c_dv", cd_e)
                W("scalar", "c_ac", v.n["c_ac"])

                def sq(e, t=tmp, i=idx):
                    e.activation(out=sq_scr[:, :], in_=t[:, i % 2, :], func=AF.Square,
                                 bias=zero_b[:, 0:1], scale=1.0).then_inc(sems["c_ac"], 1)
                v.do("scalar", sq)
                ac_m = v.inc("c_ac")
            else:
                ac_m = v.n["c_ac"]
            if lite >= 1:
                ev_hist.append(cd_e)
                yield eti, cd_e
                continue
            W("tensor", "c_ac", ac_m)
            W("tensor", "c_dv", cd_e)
            for c in range(4):
                if c >= 2:
                    W("tensor", "c_dv", kn_hist[-2])

                def knm(e, c=c, i=idx):
                    e.matmul(out=psq[i % 2][0:2, 1024 * (c % 2):1024 * (c % 2) + 512],
                             lhsT=hind[:, :],
                             rhs=sq_scr[:, 512 * c:512 * c + 512],
                             start=True, stop=True).then_inc(sems["c_pe"], 1)
                v.do("tensor", knm)
                pe_k = v.inc("c_pe")
                W("vector", "c_pe", pe_k)
                pkn_src = lambda c_, i=idx: psq[i % 2][0:2, 1024 * (c_ % 2):1024 * (c_ % 2) + 512]
                if kind == "K":
                    def red(e, c=c, eti=eti, pkn_src=pkn_src):
                        e.drain()
                        b = c // 2
                        e.tensor_reduce(out=kmt[:, 0:1],
                                        in_=pkn_src(c), axis=AX, op=ALU.max)
                        e.drain()
                        hp = eti - 8
                        col = 2 * hp + b
                        if c % 2 == 0:
                            e.tensor_copy(out=kml[:, col:col + 1],
                                          in_=kmt[:, 0:1]).then_inc(sems["c_dv"], 1)
                        else:
                            e.tensor_tensor(out=kml[:, col:col + 1], in0=kml[:, col:col + 1],
                                            in1=kmt[:, 0:1], op=ALU.max).then_inc(sems["c_dv"], 1)
                        e.drain()
                    v.do("vector", red)
                    kn_hist.append(v.inc("c_dv"))
                else:
                    def cp(e, c=c, pkn_src=pkn_src):
                        e.tensor_copy(out=qn2[:, 512 * c:512 * c + 512],
                                      in_=pkn_src(c)).then_inc(sems["c_dv"], 1)
                    v.do("vector", cp)
                    kn_hist.append(v.inc("c_dv"))
            ev_hist.append(v.n["c_dv"])
            yield eti, cd_e

    k_ev = []
    for eti, cd_e in qk_eti(range(8, 16), ktmp, "K"):
        k_ev.append((eti, cd_e))
        hp = eti - 8
        W("sync", "c_dv", cd_e)
        dma("sync", "d_kst", kvkin[(2 * hp) * 65:(2 * hp) * 65 + 64, :], ktmp[0:64, (eti - 8) % 2, :])
        dma("sync", "d_kst", kvkin[(2 * hp + 1) * 65:(2 * hp + 1) * 65 + 64, :], ktmp[64:128, (eti - 8) % 2, :])
        if hp >= 1:
            pass
    KST_TOTAL = v.n["d_kst"]
    W("sync", "c_dv", v.n["c_dv"])
    dma("sync", "d_kmx", kmax_in[:, :], kml[:])
    if stage < 4:
        return _cut()
    W("gpsimd", "d_kmx", 16)

    def arm(e):
        e.collective_compute("AllReduce", ALU.max, replica_groups=rg,
                             ins=[kmax_in.opt()], outs=[kmax_out.opt()]).then_inc(sems["cc"])
    v.do("gpsimd", arm)
    v.inc("cc")
    W("gpsimd", "d_kst", KST_TOTAL)

    def agk(e):
        e.collective_compute("AllGather", ALU.bypass, replica_groups=rg,
                             ins=[kvkin.opt()], outs=[kvkout.opt()]).then_inc(sems["cc"])
    v.do("gpsimd", agk)
    v.inc("cc")

    if stage < 5:
        return _cut()
    W("tensor", "c_dv", v.n["c_dv"])
    W("tensor", "c_ac", v.n["c_ac"])
    vev_hist = []
    for t in range(NT):
        if t >= 2:
            W("tensor", "c_dv", vev_hist[t - 2])

        def vmm(e, t=t):
            for dt in range(8):
                for c in range(2):
                    e.matmul(out=psq[t % 2][:, 512 * c:512 * c + 512],
                             lhsT=xln1T[:, dt, 128 * t:128 * t + 128],
                             rhs=qwT[:, dt, 2 * D + 512 * c:2 * D + 512 * c + 512],
                             start=(dt == 0), stop=False)
            for c in range(2):
                mm = e.matmul(out=psq[t % 2][:, 512 * c:512 * c + 512],
                              lhsT=ones_row[0:1, 0:128],
                              rhs=qkb_row[0:1, 2 * D + 512 * c:2 * D + 512 * c + 512],
                              start=False, stop=True)
                if c == 1:
                    mm.then_inc(sems["c_pe"], 1)
        v.do("tensor", vmm)
        pe_v = v.inc("c_pe")
        W("vector", "c_pe", pe_v)
        if t >= 2:
            W("vector", "d_vst", v.n["d_vst"])

        def vev(e, t=t):
            e.tensor_copy(out=vtmp[:, t % 2, :], in_=psq[t % 2][:, 0:1024]).then_inc(sems["c_dv"], 1)
        v.do("vector", vev)
        cd_v = v.inc("c_dv")
        vev_hist.append(cd_v)
        W("sync", "c_dv", cd_v)
        dma("sync", "d_vst", kvvin[128 * t:128 * t + 128, :], vtmp[:, t % 2, :])
    W("gpsimd", "d_vst", 16 * NT)

    def agv(e):
        e.collective_compute("AllGather", ALU.bypass, replica_groups=rg,
                             ins=[kvvin.opt()], outs=[kvvout.opt()]).then_inc(sems["cc"])
    v.do("gpsimd", agv)
    v.inc("cc")
    for eI in range(EPC):
        dma("gpsimd", "d_w", wbf[eI, :, :], wsl[eI, :, :])
    W("gpsimd", "cc", 1)
    dma("gpsimd", "d_kmg", kmg[:], kmax_out[:, :])
    if stage < 6:
        return _cut()
    W("scalar", "d_kmg", 16)
    W("scalar", "c_ac", v.n["c_ac"])

    def kngs(e):
        e.activation(out=kng[:], in_=kmg[:], func=AF.Sqrt,
                     bias=zero_b[0:2, 0:1], scale=1.0).then_inc(sems["c_ac"], 1)
    v.do("scalar", kngs)
    AC_KNG = v.inc("c_ac")
    W("vector", "c_ac", AC_KNG)

    def kngn(e):
        e.drain()
        e.tensor_scalar(out=kng[:], in0=kng[:], scalar1=-1.0, scalar2=None, op0=ALU.mult)
        e.drain()
    v.do("vector", kngn)

    if stage < 7:
        return _cut()
    qst_hist = []
    for eti, cd_e in qk_eti(range(0, 8), qtmp, "Q"):
        hp = eti
        W("scalar", "c_dv", v.n["c_dv"])
        W("scalar", "c_ac", v.n["c_ac"])

        def m8sq(e):
            e.activation(out=qnw[0:2, :], in_=qn2[:, 0:MLOC], func=AF.Sqrt,
                         bias=zero_b[0:2, 0:1], scale=1.0)
            e.activation(out=qns[0:2, :], in_=qn2[:, MLOC:TLOC], func=AF.Sqrt,
                         bias=zero_b[0:2, 0:1], scale=1.0).then_inc(sems["c_ac"], 1)
        v.do("scalar", m8sq)
        ac_m8 = v.inc("c_ac")
        W("vector", "c_ac", ac_m8)

        def m8chain(e, hp=hp):
            e.drain()
            e.tensor_scalar(out=m8s[:, hp % 2, 0:MLOC], in0=qnw[0:2, :],
                            scalar1=kng[:, 2 * hp:2 * hp + 1], scalar2=SHIFT,
                            op0=ALU.mult, op1=ALU.add)
            e.tensor_scalar(out=m8s[:, hp % 2, MLOC:TLOC], in0=qns[0:2, :],
                            scalar1=kng[:, 2 * hp + 1:2 * hp + 2], scalar2=SHIFT,
                            op0=ALU.mult, op1=ALU.add).then_inc(sems["c_dv"], 1)
            e.drain()
        v.do("vector", m8chain)
        cd_m8 = v.inc("c_dv")
        W("sync", "c_dv", cd_m8)
        qsem2 = "d_qst" if hp % 2 == 0 else "d_qs1"
        dma("sync", qsem2, q_d[128 * hp:128 * hp + 128, :], qtmp[:, hp % 2, :])
        qst = dma("sync", qsem2, m8_d[2 * hp:2 * hp + 2, :], m8s[:, hp % 2, :])
        qst_hist.append((qsem2, qst))
        if hp >= 1:
            pass

    if debug and stage >= 7 and phases == 1:
        W("sync", "c_dv", v.n["c_dv"])
        W("sync", "d_qst", v.n["d_qst"])
        W("sync", "cc", 3)
        v.do("gpsimd", lambda e, x=v.n["c_dv"]: e.wait_ge(sems["c_dv"], x))
        dma("gpsimd", "d_dbg", dbg["dbg_xln1T"][:, :], xln1T[:, 0, :])
        dma("gpsimd", "d_dbg", dbg["dbg_q"][:, :], q_d[0:128, :])
        dma("gpsimd", "d_dbg", dbg["dbg_m8"][:, :], m8_d[0:2, :])
        dma("gpsimd", "d_dbg", dbg["dbg_k"][:, :], kvkout[0:128, :])
        W("gpsimd", "d_dbg", v.n["d_dbg"])

    barrier_all(1)
    v.flush(nc)
    p1.close()

    if phases < 2:
        es.close()
        return nc, dbg

    p2 = ExitStack()
    kbufA = [p2.enter_context(nc.sbuf_tensor(f"kbufA{i}", [65, 8192], BF16)) for i in range(2)]
    kbufB = [p2.enter_context(nc.sbuf_tensor(f"kbufB{i}", [65, 8192], BF16)) for i in range(2)]
    vbufA = [p2.enter_context(nc.sbuf_tensor(f"vbufA{i}", [128, 65 * 64], BF16)) for i in range(2)]
    vbufB = [p2.enter_context(nc.sbuf_tensor(f"vbufB{i}", [128, 65 * 64], BF16)) for i in range(2)]
    qbufA = [p2.enter_context(nc.sbuf_tensor(f"qbufA{i}", [65, 512], BF16)) for i in range(2)]
    qbufB = [p2.enter_context(nc.sbuf_tensor(f"qbufB{i}", [65, 512], BF16)) for i in range(2)]
    pt = [p2.enter_context(nc.sbuf_tensor(f"pt{i}", [128, 1024], BF16)) for i in range(4)]
    uA = p2.enter_context(nc.sbuf_tensor("uA", [64, 512], F32))
    uB = p2.enter_context(nc.sbuf_tensor("uB", [64, 512], F32))
    s65 = p2.enter_context(nc.sbuf_tensor("s65", [65, 1024], F32))
    sbcA = p2.enter_context(nc.sbuf_tensor("sbcA", [64, 512], F32))
    sbcB = p2.enter_context(nc.sbuf_tensor("sbcB", [64, 512], F32))
    astA = p2.enter_context(nc.sbuf_tensor("astA", [64, 512], BF16))
    astB = p2.enter_context(nc.sbuf_tensor("astB", [64, 512], BF16))
    psum_s = [p2.enter_context(nc.psum_tensor(f"psum_s{i}", [128, 1024], F32)) for i in range(3)]
    pavA = [p2.enter_context(nc.psum_tensor("pavA0", [128, 512], F32))] * 2
    pavB = [p2.enter_context(nc.psum_tensor("pavB0", [128, 512], F32))] * 2

    kvk_r = kvkout.rearrange("(r h j2) c -> h j2 r c", h=H, j2=65)
    kvv_r = kvvout.rearrange("(r bb i) d -> bb r i d", bb=2, i=MLOC)

    for s_ in ("d_kst", "d_vst", "d_qst", "d_qs1"):
        W("vector", s_, v.n[s_])

    def vones(e):
        for buf in (vbufA[0], vbufA[1], vbufB[0], vbufB[1]):
            e.memset(buf[:].rearrange("p (k c) -> p k c", c=65)[:, :, 64:65], 1.0).then_inc(sems["c_dv"], 1)
    v.do("vector", vones)
    CD_VONES = v.inc("c_dv", 4)

    NUNIT = 32
    W("sync", "cc", 3)
    for s_ in ("d_kst", "d_vst", "d_qst", "d_qs1", "d_kmx"):
        W("sync", s_, v.n[s_])
    for j in range(NUNIT):
        u, hp = divmod(j, 8)
        b, qbi = divmod(u, 2)
        colbase = b * MLOC + qbi * 512
        par = j % 2
        kldsem = "d_kld" if par == 0 else "d_kld"
        ksem = "d_kld" if par == 0 else "d_kld1"
        vsem = "d_vld" if par == 0 else "d_vld1"
        qsem = "d_qld" if par == 0 else "d_qld1"
        if j >= 2:
            W("sync", "c_av", 64 * (j - 1))
        dma("sync", ksem, kbufA[par][:].rearrange("p (r c) -> p r c", r=8),
            kvk_r[2 * hp][:, :, b * MLOC:(b + 1) * MLOC])
        dma("sync", ksem, kbufB[par][:].rearrange("p (r c) -> p r c", r=8),
            kvk_r[2 * hp + 1][:, :, b * MLOC:(b + 1) * MLOC])
        for c3 in range(8):
            src = kvv_r[b][c3].rearrange("(kt p) d -> p kt d", p=128)
            srcA = src[:, :, 128 * hp:128 * hp + 64]
            srcB = src[:, :, 128 * hp + 64:128 * hp + 128]
            dstA = vbufA[par][:].rearrange("p (k c) -> p k c", c=65)[:, 8 * c3:8 * c3 + 8, 0:64]
            dstB = vbufB[par][:].rearrange("p (k c) -> p k c", c=65)[:, 8 * c3:8 * c3 + 8, 0:64]
            dma("sync", vsem, dstA, srcA)
            dma("sync", vsem, dstB, srcB)
        dma("sync", qsem, qbufA[par][0:64, :], q_d[128 * hp:128 * hp + 64, colbase:colbase + 512])
        dma("sync", qsem, qbufA[par][64:65, :], m8_d[2 * hp:2 * hp + 1, colbase:colbase + 512])
        dma("sync", qsem, qbufB[par][0:64, :], q_d[128 * hp + 64:128 * hp + 128, colbase:colbase + 512])
        dma("sync", qsem, qbufB[par][64:65, :], m8_d[2 * hp + 1:2 * hp + 2, colbase:colbase + 512])
        kld_tgt = (ksem, v.n[ksem])
        vld_tgt = (vsem, v.n[vsem])
        qld_tgt = (qsem, v.n[qsem])

        W("tensor", kld_tgt[0], kld_tgt[1])
        W("tensor", qld_tgt[0], qld_tgt[1])

        def emit_scores(kt):
            I = 64 * j + kt
            if I >= 3:
                W("tensor", "c_ex", I - 2)

            def sc(e, par=par, kt=kt, I=I):
                e.matmul(out=psum_s[I % 3][:, 0:512],
                         lhsT=kbufA[par][:, 128 * kt:128 * kt + 128],
                         rhs=qbufA[par][:, :], start=True, stop=True)
                e.matmul(out=psum_s[I % 3][:, 512:1024],
                         lhsT=kbufB[par][:, 128 * kt:128 * kt + 128],
                         rhs=qbufB[par][:, :], start=True,
                         stop=True).then_inc(sems["c_sc"], 1)
            v.do("tensor", sc)
            v.inc("c_sc")
            W("scalar", "c_sc", I + 1)
            if I >= 4:
                W("scalar", "c_av", I - 3)

            def ex(e, kt=kt, I=I):
                e.activation(out=pt[I % 4][:, :], in_=psum_s[I % 3][:, :], func=AF.Exp,
                             bias=zero_b[:, 0:1], scale=0.125).then_inc(sems["c_ex"], 1)
            v.do("scalar", ex)
            v.inc("c_ex")

        def emit_attnv(kt):
            I = 64 * j + kt
            W("tensor", "c_ex", I + 1)
            if kt == 0:
                W("tensor", vld_tgt[0], vld_tgt[1])
                if j == 0:
                    W("tensor", "c_dv", CD_VONES)
                if j >= 1:
                    W("tensor", "c_nr", j)

            def av(e, par=par, kt=kt, I=I, j=j):
                e.matmul(out=pavA[j % 2][0:65, :], lhsT=vbufA[par][:, 65 * kt:65 * kt + 65],
                         rhs=pt[I % 4][:, 0:512], start=(kt == 0), stop=(kt == 63))
                e.matmul(out=pavB[j % 2][0:65, :], lhsT=vbufB[par][:, 65 * kt:65 * kt + 65],
                         rhs=pt[I % 4][:, 512:1024], start=(kt == 0),
                         stop=(kt == 63)).then_inc(sems["c_av"], 1)
            v.do("tensor", av)
            v.inc("c_av")

        emit_scores(0)
        emit_scores(1)
        for kt in range(2, NKT):
            emit_scores(kt)
            emit_attnv(kt - 2)
        emit_attnv(NKT - 2)
        emit_attnv(NKT - 1)

        W("vector", "c_av", 64 * (j + 1))
        if j >= 1:
            W("vector", "d_at", 32 * j)

        def nrm1(e, j=j):
            e.tensor_scalar(out=s65[64:65, 0:512], in0=pavA[j % 2][64:65, :], scalar1=1e-30,
                            scalar2=None, op0=ALU.add)
            e.tensor_scalar(out=s65[64:65, 512:1024], in0=pavB[j % 2][64:65, :], scalar1=1e-30,
                            scalar2=None, op0=ALU.add).then_inc(sems["c_rc"], 1)
            e.tensor_copy(out=uA[:], in_=pavA[j % 2][0:64, :])
            e.tensor_copy(out=uB[:], in_=pavB[j % 2][0:64, :]).then_inc(sems["c_nr"], 1)
        v.do("vector", nrm1)
        v.inc("c_rc")
        v.inc("c_nr")
        W("gpsimd", "c_rc", v.n["c_rc"])

        def bc(e, n0=v.n["d_bc"]):
            e.dma_start(out=bc_d[:, :], in_=s65[64:65, :]).then_inc(sems["d_bc"], 16)
            e.wait_ge(sems["d_bc"], n0 + 16)
            e.dma_start(out=sbcA[:], in_=bc_d[0:1, 0:512].broadcast_to([64, 512])).then_inc(sems["d_bc"], 16)
            e.dma_start(out=sbcB[:], in_=bc_d[0:1, 512:1024].broadcast_to([64, 512])).then_inc(sems["d_bc"], 16)
        v.do("gpsimd", bc)
        v.inc("d_bc", 48)
        W("vector", "d_bc", v.n["d_bc"])

        def nrm2(e):
            e.drain()
            e.reciprocal(out=sbcA[:], in_=sbcA[:])
            e.reciprocal(out=sbcB[:], in_=sbcB[:])
            e.drain()
            e.tensor_tensor(out=astA[:], in0=uA[:], in1=sbcA[:], op=ALU.mult).then_inc(sems["c_mul"], 1)
            e.tensor_tensor(out=astB[:], in0=uB[:], in1=sbcB[:], op=ALU.mult).then_inc(sems["c_mul"], 1)
        v.do("vector", nrm2)
        v.inc("c_mul", 2)
        W("gpsimd", "c_mul", v.n["c_mul"])
        dma("gpsimd", "d_at", attn_d[b, 128 * hp:128 * hp + 64, qbi * 512:qbi * 512 + 512], astA[:])
        dma("gpsimd", "d_at", attn_d[b, 128 * hp + 64:128 * hp + 128, qbi * 512:qbi * 512 + 512], astB[:])

    if debug and phases == 2:
        v.do("gpsimd", lambda e, x=v.n["d_at"]: e.wait_ge(sems["d_at"], x))
        dma("gpsimd", "d_dbg", dbg["dbg_attn"][:, :], attn_d[0, 0:128, :])
        W("gpsimd", "d_dbg", v.n["d_dbg"])

    barrier_all(2)
    v.flush(nc)
    p2.close()

    if phases < 3:
        es.close()
        return nc, dbg

    p36 = ExitStack()
    xln3T = p36.enter_context(nc.sbuf_tensor("xln3T", [128, 8, TLOC], BF16))
    phi_sb = p36.enter_context(nc.sbuf_tensor("phi_sb", [128, 8, NP], BF16))
    p34 = ExitStack()
    xln3 = p34.enter_context(nc.sbuf_tensor("xln3", [128, NT, D], BF16))
    p3 = ExitStack()
    attn_t = p3.enter_context(nc.sbuf_tensor("attn_t", [128, 2, 8, 128], BF16))
    pwT = p3.enter_context(nc.sbuf_tensor("pwT", [128, 8, D], BF16))
    prow = p3.enter_context(nc.sbuf_tensor("prow", [1, D], BF16))
    o_bf = p3.enter_context(nc.sbuf_tensor("o_bf", [128, 8, D], BF16))
    scrA = p3.enter_context(nc.sbuf_tensor("scrA", [128, D], F32))
    scrB_ap = sq_scr[:].bitcast(F32)
    ppo = [p3.enter_context(nc.psum_tensor(f"ppo{i}", [128, D], F32)) for i in range(2)]
    ptr2 = [p3.enter_context(nc.psum_tensor(f"ptr2{i}", [128, 512], BF16)) for i in range(2)]

    def lnstats_sbuf(src_ap_fn, ntiles, base_col=0):
        for t in range(ntiles):
            v.do("vector", lambda e, t=t: e.tensor_reduce(
                out=stat[:, base_col + t:base_col + t + 1], in_=src_ap_fn(t), axis=AX, op=ALU.add))
            W("scalar", "c_ac", v.n["c_ac"])

            def sqf(e, t=t):
                e.activation(out=sq_scr[:, 0:D], in_=src_ap_fn(t), func=AF.Square,
                             bias=zero_b[:, 0:1], scale=1.0,
                             accum_out=stat2[:, base_col + t:base_col + t + 1]).then_inc(sems["c_ac"], 1)
            v.do("scalar", sqf)
            v.inc("c_ac")

    def ln_rstd(n, c0=0):
        def f(e):
            sl = slice(c0, c0 + n)
            e.drain()
            e.tensor_scalar(out=mean_t[:, sl], in0=stat[:, sl], scalar1=1.0 / D,
                            scalar2=None, op0=ALU.mult)
            e.tensor_scalar(out=nsc1[:, sl], in0=stat2[:, sl], scalar1=1.0 / D,
                            scalar2=EPS, op0=ALU.mult, op1=ALU.add)
            e.drain()
            e.tensor_tensor(out=var_t[:, sl], in0=mean_t[:, sl], in1=mean_t[:, sl], op=ALU.mult)
            e.drain()
            e.tensor_tensor(out=var_t[:, sl], in0=nsc1[:, sl], in1=var_t[:, sl], op=ALU.subtract)
            e.drain()
            e.tensor_scalar(out=nsi1[:, sl], in0=var_t[:, sl].bitcast(I32), scalar1=1,
                            scalar2=None, op0=ALU.arith_shift_right)
            e.drain()
            e.tensor_scalar(out=nsi2[:, sl], in0=nsi1[:, sl], scalar1=-1, scalar2=int(MAGIC),
                            op0=ALU.mult, op1=ALU.add)
            e.drain()
            e.tensor_copy(out=rstd_t[:, sl].bitcast(I32), in_=nsi2[:, sl])
            e.drain()
            for _ in range(4):
                e.tensor_tensor(out=nsc1[:, sl], in0=rstd_t[:, sl], in1=rstd_t[:, sl], op=ALU.mult)
                e.drain()
                e.tensor_tensor(out=nsc1[:, sl], in0=nsc1[:, sl], in1=var_t[:, sl], op=ALU.mult)
                e.drain()
                e.tensor_scalar(out=nsc1[:, sl], in0=nsc1[:, sl], scalar1=-0.5, scalar2=1.5,
                                op0=ALU.mult, op1=ALU.add)
                e.drain()
                e.tensor_tensor(out=rstd_t[:, sl], in0=rstd_t[:, sl], in1=nsc1[:, sl], op=ALU.mult)
                e.drain()
        v.do("vector", f)

    dma("gpsimd", "d_pw", pwT[:], proj_wT_d.rearrange("(dt p) e -> p dt e", p=128))
    PWT_TGT = v.n["d_pw"]
    for i in range(2):
        dma("gpsimd", "d_ph2", phi_sb[:, 4 * i:4 * i + 4, :],
            phi_d.rearrange("(dt p) n -> p dt n", p=128)[:, 4 * i:4 * i + 4, :])
    PHI_TGT = v.n["d_ph2"]
    dma("gpsimd", "d_g", prow[:], projb_d[:, :])
    PROW_TGT = v.n["d_g"]
    W("sync", "d_at", v.n["d_at"])

    pmm_hist = []
    evac3_hist = []
    for b in range(B):
        if b == 0:
            W("tensor", "d_pw", PWT_TGT)
            W("tensor", "d_g", PROW_TGT)
        for ti8 in range(8):
            t_glob = 8 * b + ti8
            n3 = 8 * b + ti8
            psem = "d_p0" if n3 % 2 == 0 else "d_p1"
            if n3 >= 2:
                W("sync", "c_pe", pmm_hist[n3 - 2])
            dma("sync", psem, attn_t[:, ti8 % 2],
                attn_d[b].rearrange("(dt p) q -> p dt q", p=128)[:, :, 128 * ti8:128 * ti8 + 128])
            W("tensor", psem, v.n[psem])
            if n3 >= 2:
                W("tensor", "c_dv", evac3_hist[n3 - 2])

            def pmm(e, ti8=ti8):
                for c in range(2):
                    for dt in range(8):
                        e.matmul(out=ppo[ti8 % 2][:, 512 * c:512 * c + 512],
                                 lhsT=attn_t[:, ti8 % 2, dt, :],
                                 rhs=pwT[:, dt, 512 * c:512 * c + 512],
                                 start=(dt == 0), stop=False)
                    mm = e.matmul(out=ppo[ti8 % 2][:, 512 * c:512 * c + 512],
                                  lhsT=ones_row[0:1, 0:128],
                                  rhs=prow[0:1, 512 * c:512 * c + 512],
                                  start=False, stop=True)
                    if c == 1:
                        mm.then_inc(sems["c_pe"], 1)
            v.do("tensor", pmm)
            pe_p = v.inc("c_pe")
            pmm_hist.append(pe_p)
            W("vector", "c_pe", pe_p)
            v.do("vector", lambda e, ti8=ti8: e.tensor_copy(
                out=o_bf[:, ti8, :], in_=ppo[ti8 % 2][:, :]).then_inc(sems["c_dv"], 1))
            evac3_hist.append(v.inc("c_dv"))
        CD_EV = v.n["c_dv"]
        W("vector", "c_dv", CD_EV)
        W("scalar", "c_dv", CD_EV)
        lnstats_sbuf(lambda t: o_bf[:, t, :], 8)
        AC_LN2 = v.n["c_ac"]
        W("vector", "c_ac", AC_LN2)
        emit_rstd_sqrt(8)
        for pair in range(4):
            def appl(e, pair=pair, b=b):
                for k2 in range(2):
                    t = 2 * pair + k2
                    e.tensor_scalar(out=(scrA[:] if k2 == 0 else scrB_ap),
                                    in0=o_bf[:, t, :], scalar1=mean_t[:, t:t + 1],
                                    scalar2=rstd_t[:, t:t + 1], op0=ALU.subtract, op1=ALU.mult)
                e.drain()
                for k2 in range(2):
                    t = 2 * pair + k2
                    mm = e.tensor_tensor(out=Rt[:, 8 * b + t, :], in0=Rt[:, 8 * b + t, :],
                                         in1=(scrA[:] if k2 == 0 else scrB_ap), op=ALU.add)
                    if k2 == 1:
                        mm.then_inc(sems["c_dv"], 1)
                e.drain()
            v.do("vector", appl)
            v.inc("c_dv")
        W("scalar", "c_dv", v.n["c_dv"])
        lnstats_sbuf(lambda t, b=b: Rt[:, 8 * b + t, :], 8)
        W("vector", "c_ac", v.n["c_ac"])
        emit_rstd_sqrt(8)
        for ti8 in range(8):
            t_glob = 8 * b + ti8
            if ti8 >= 1:
                W("vector", "c_pe", v.n["c_pe"])
            v.do("vector", lambda e, ti8=ti8, t_glob=t_glob: e.tensor_scalar(
                out=xln3[:, t_glob, :], in0=Rt[:, t_glob, :], scalar1=mean_t[:, ti8:ti8 + 1],
                scalar2=rstd_t[:, ti8:ti8 + 1], op0=ALU.subtract,
                op1=ALU.mult).then_inc(sems["c_dv"], 1))
            cd_x3 = v.inc("c_dv")
            W("tensor", "c_dv", cd_x3)
            for dh in range(2):
                W("tensor", "c_dv", v.n["c_dv"])

                def tr3b(e, t_glob=t_glob, dh=dh, p=ptr2[dh]):
                    for jj in range(4):
                        dt = 4 * dh + jj
                        mm = e.transpose(out=p[:, 128 * jj:128 * jj + 128],
                                         in_=xln3[:, t_glob, 128 * dt:128 * dt + 128],
                                         identity=ident[:])
                        if jj == 3:
                            mm.then_inc(sems["c_pe"], 1)
                v.do("tensor", tr3b)
                pe_t3 = v.inc("c_pe")
                W("vector", "c_pe", pe_t3)

                def ev3(e, t_glob=t_glob, dh=dh, p=ptr2[dh]):
                    e.tensor_copy(
                        out=xln3T[:, 4 * dh:4 * dh + 4, 128 * t_glob:128 * t_glob + 128],
                        in_=p[:].rearrange("p (dt c) -> p dt c", c=128),
                    ).then_inc(sems["c_dv"], 1)
                v.do("vector", ev3)
                v.inc("c_dv")

    if debug and phases == 3:
        v.do("gpsimd", lambda e, x=v.n["c_dv"]: e.wait_ge(sems["c_dv"], x))
        dma("gpsimd", "d_dbg", dbg["dbg_xln3"][:, :], xln3[:, 0, :])
        W("gpsimd", "d_dbg", v.n["d_dbg"])

    barrier_all(3)
    v.flush(nc)
    p3.close()

    if phases < 4:
        p34.close()
        p36.close()
        es.close()
        return nc, dbg

    p4 = ExitStack()
    logits_q = p4.enter_context(nc.sbuf_tensor("logits_q", [128, 16, 512], BF16))
    disp_q = p4.enter_context(nc.sbuf_tensor("disp_q", [128, 16, 512], BF16))
    dsc = p4.enter_context(nc.sbuf_tensor("dsc", [128, 512], BF16))
    th = p4.enter_context(nc.sbuf_tensor("th", [128, 512], BF16))
    slscr = p4.enter_context(nc.sbuf_tensor("slscr", [128, D], F32))
    ppl = [p4.enter_context(nc.psum_tensor(f"ppl{i}", [128, 512], F32)) for i in range(2)]
    psl = [p4.enter_context(nc.psum_tensor(f"psl{i}", [128, D], F32)) for i in range(2)]

    W("tensor", "d_ph2", PHI_TGT)

    lev_hist = []
    sev_hist = []
    for qtr in range(4):
        for ti in range(NT):
            nl = 16 * qtr + ti
            if nl >= 2:
                W("tensor", "c_dv", lev_hist[nl - 2])

            def lmm(e, ti=ti, qtr=qtr):
                for dt in range(8):
                    mm = e.matmul(out=ppl[ti % 2][:, :],
                                  lhsT=xln3T[:, dt, 128 * ti:128 * ti + 128],
                                  rhs=phi_sb[:, dt, 512 * qtr:512 * qtr + 512],
                                  start=(dt == 0), stop=(dt == 7))
                    if dt == 7:
                        mm.then_inc(sems["c_pe"], 1)
            v.do("tensor", lmm)
            pe_l = v.inc("c_pe")
            W("vector", "c_pe", pe_l)
            v.do("vector", lambda e, ti=ti: e.tensor_copy(
                out=logits_q[:, ti, :], in_=ppl[ti % 2][:, :]).then_inc(sems["c_dv"], 1))
            lev_hist.append(v.inc("c_dv"))
        CD_LQ = v.n["c_dv"]
        for tp in range(8):
            def dsp1(e, tp=tp):
                e.drain()
                e.tensor_tensor(out=dsc[:], in0=logits_q[:, tp, :],
                                in1=logits_q[:, tp + 8, :],
                                op=ALU.subtract).then_inc(sems["c_dv"], 1)
            v.do("vector", dsp1)
            cd_d = v.inc("c_dv")
            W("scalar", "c_dv", cd_d)
            W("scalar", "c_ac", v.n["c_ac"])

            def dsp2(e):
                e.activation(out=th[:], in_=dsc[:], func=AF.Tanh,
                             bias=zero_b[:, 0:1], scale=0.5).then_inc(sems["c_ac"], 1)
            v.do("scalar", dsp2)
            ac_t = v.inc("c_ac")
            W("vector", "c_ac", ac_t)

            def dsp3(e, tp=tp):
                e.tensor_scalar(out=disp_q[:, tp, :], in0=th[:], scalar1=1.0,
                                scalar2=None, op0=ALU.add)
                e.tensor_scalar(out=disp_q[:, tp + 8, :], in0=th[:], scalar1=-1.0,
                                scalar2=1.0, op0=ALU.mult,
                                op1=ALU.add).then_inc(sems["c_dv"], 1)
            v.do("vector", dsp3)
            v.inc("c_dv")
        CD_DISP = v.n["c_dv"]
        W("tensor", "c_dv", CD_DISP)
        for b in range(B):
            for nlt in range(4):
                ns4 = 8 * qtr + 4 * b + nlt
                if ns4 >= 2:
                    W("tensor", "c_dv", sev_hist[ns4 - 2])

                def smm(e, b=b, nlt=nlt):
                    for c in range(2):
                        for mt in range(8):
                            mm = e.matmul(out=psl[nlt % 2][:, 512 * c:512 * c + 512],
                                          lhsT=disp_q[:, 8 * b + mt, 128 * nlt:128 * nlt + 128],
                                          rhs=xln3[:, 8 * b + mt, 512 * c:512 * c + 512],
                                          start=(mt == 0), stop=(mt == 7))
                            if c == 1 and mt == 7:
                                mm.then_inc(sems["c_pe"], 1)
                v.do("tensor", smm)
                pe_s = v.inc("c_pe")
                W("vector", "c_pe", pe_s)
                W("vector", "d_sl", v.n["d_sl"])
                v.do("vector", lambda e, nlt=nlt: e.tensor_copy(
                    out=slscr[:], in_=psl[nlt % 2][:, :]).then_inc(sems["c_dv"], 1))
                cd_s = v.inc("c_dv")
                sev_hist.append(cd_s)
                W("sync", "c_dv", cd_s)
                npg = 4 * qtr + nlt
                for eg in range(4):
                    row0 = 64 * (4 * npg + eg) + 32 * b
                    dma("sync", "d_sl", slots_in[row0:row0 + 32, :],
                        slscr[32 * eg:32 * eg + 32, :])
        if qtr in (1, 3):
            W("gpsimd", "d_sl", v.n["d_sl"])
            half = 0 if qtr == 1 else 1

            def rs(e, half=half):
                e.collective_compute("ReduceScatter", ALU.add, replica_groups=rg,
                                     ins=[slots_in[half * NP:(half + 1) * NP, :].opt()],
                                     outs=[slots_out[half * 256:(half + 1) * 256, :].opt()]).then_inc(sems["cc"])
            v.do("gpsimd", rs)
            v.inc("cc")

    if debug and phases == 4:
        v.do("gpsimd", lambda e, x=v.n["cc"]: e.wait_ge(sems["cc"], x))
        dma("gpsimd", "d_dbg", dbg["dbg_disp"][:, :], slots_out[0:128, :])
        W("gpsimd", "d_dbg", v.n["d_dbg"])

    barrier_all(4)
    v.flush(nc)
    p4.close()
    p34.close()

    if phases < 5:
        p36.close()
        es.close()
        return nc, dbg

    p5 = ExitStack()
    slots_sb = p5.enter_context(nc.sbuf_tensor("slots_sb", [128, 4, D], BF16))
    slotsT = p5.enter_context(nc.sbuf_tensor("slotsT", [128, 8, 512], BF16))
    wbuf = [p5.enter_context(nc.sbuf_tensor(f"wbuf{i}", [128, 8, D], BF16)) for i in range(2)]
    b2row = p5.enter_context(nc.sbuf_tensor("b2row", [1, EPC * D], BF16))
    ldb = p5.enter_context(nc.sbuf_tensor("ldb", [1, 2048], F32))
    ytmp = p5.enter_context(nc.sbuf_tensor("ytmp", [64, D], BF16))
    ptr5 = [p5.enter_context(nc.psum_tensor(f"ptr5{i}", [128, 512], BF16)) for i in range(2)]
    py = [p5.enter_context(nc.psum_tensor(f"py{i}", [64, D], F32)) for i in range(2)]

    W("gpsimd", "cc", 4)
    dma("gpsimd", "d_g", slots_sb[:, 0:2, :], slots_out[0:256, :].rearrange("(r p) d -> p r d", p=128))
    SLO_TGT1 = v.n["d_g"]
    W("gpsimd", "cc", 5)
    dma("gpsimd", "d_g2", slots_sb[:, 2:4, :], slots_out[256:512, :].rearrange("(r p) d -> p r d", p=128))
    SLO_TGT = v.n["d_g2"]
    for ch in range(4):
        if ch > 0:
            W("sync", "c_dv", v.n["c_dv"])
        dma("sync", "d_m", ldb[:], bsl[:, 2048 * ch:2048 * ch + 2048])
        W("vector", "d_m", v.n["d_m"])
        v.do("vector", lambda e, ch=ch: e.tensor_scalar(
            out=b2row[0:1, 2048 * ch:2048 * ch + 2048], in0=ldb[:], scalar1=2.0,
            scalar2=None, op0=ALU.mult).then_inc(sems["c_dv"], 1))
        v.inc("c_dv")
    CD_B2 = v.n["c_dv"]
    W("tensor", "d_g", SLO_TGT1)
    for r4 in range(4):
        if r4 == 2:
            W("tensor", "d_g2", SLO_TGT)
        for dh in range(2):
            W("tensor", "c_dv", v.n["c_dv"])

            def tr5(e, r4=r4, dh=dh, p=ptr5[dh]):
                for jj in range(4):
                    dt = 4 * dh + jj
                    mm = e.transpose(out=p[:, 128 * jj:128 * jj + 128],
                                     in_=slots_sb[:, r4, 128 * dt:128 * dt + 128],
                                     identity=ident[:])
                    if jj == 3:
                        mm.then_inc(sems["c_pe"], 1)
            v.do("tensor", tr5)
            pe_t5 = v.inc("c_pe")
            W("vector", "c_pe", pe_t5)

            def ev5(e, r4=r4, dh=dh, p=ptr5[dh]):
                e.tensor_copy(
                    out=slotsT[:, 4 * dh:4 * dh + 4, 128 * r4:128 * r4 + 128],
                    in_=p[:].rearrange("p (dt c) -> p dt c", c=128),
                ).then_inc(sems["c_dv"], 1)
            v.do("vector", ev5)
            v.inc("c_dv")
    CD_SLT = v.n["c_dv"]
    W("gpsimd", "d_w", v.n["d_w"])
    W("tensor", "c_dv", CD_SLT)
    W("tensor", "c_dv", CD_B2)
    for eI in range(EPC):
        wsem = "d_w0" if eI % 2 == 0 else "d_w1"
        if eI >= 2:
            W("gpsimd", "c_pe", v.n["c_pe"])
        dma("gpsimd", wsem, wbuf[eI % 2][:], wbf[eI].rearrange("(dt p) d2 -> p dt d2", p=128))
        W("tensor", wsem, v.n[wsem])
        if eI >= 2:
            W("tensor", "c_dv", v.n["c_dv"])

        def ymm(e, eI=eI):
            for c in range(2):
                for dt in range(8):
                    e.matmul(out=py[eI % 2][:, 512 * c:512 * c + 512],
                             lhsT=slotsT[:, dt, 64 * eI:64 * eI + 64],
                             rhs=wbuf[eI % 2][:, dt, 512 * c:512 * c + 512],
                             start=(dt == 0), stop=False)
                mm = e.matmul(out=py[eI % 2][:, 512 * c:512 * c + 512],
                              lhsT=ones_row[0:1, 0:64],
                              rhs=b2row[0:1, 1024 * eI + 512 * c:1024 * eI + 512 * c + 512],
                              start=False, stop=True)
                if c == 1:
                    mm.then_inc(sems["c_pe"], 1)
        v.do("tensor", ymm)
        pe_y = v.inc("c_pe")
        W("vector", "c_pe", pe_y)
        W("vector", "d_y", v.n["d_y"])

        def yev(e, eI=eI):
            e.tensor_copy(out=ytmp[:], in_=py[eI % 2][:, :]).then_inc(sems["c_dv"], 1)
        v.do("vector", yev)
        cd_y = v.inc("c_dv")
        W("sync", "c_dv", cd_y)
        dma("sync", "d_y", y_in[64 * eI:64 * eI + 64, :], ytmp[:])
        if eI == 3 or eI == 7:
            W("gpsimd", "d_y", v.n["d_y"])
            lo = 0 if eI == 3 else 256

            def agy(e, lo=lo):
                e.collective_compute("AllGather", ALU.bypass, replica_groups=rg,
                                     ins=[y_in[lo:lo + 256, :].opt()],
                                     outs=[y_ag[8 * lo:8 * lo + 2048, :].opt()]).then_inc(sems["cc"])
            v.do("gpsimd", agy)
            v.inc("cc")

    if debug and phases == 5:
        v.do("gpsimd", lambda e, x=v.n["cc"]: e.wait_ge(sems["cc"], x))
        dma("gpsimd", "d_dbg", dbg["dbg_y"][:, :], y_ag[0:128, :])
        W("gpsimd", "d_dbg", v.n["d_dbg"])

    barrier_all(5)
    v.flush(nc)
    p5.close()

    if phases < 6:
        p36.close()
        es.close()
        return nc, dbg

    p6 = ExitStack()
    y_b = p6.enter_context(nc.sbuf_tensor("y_b", [128, 16, D], BF16))
    ct_sb = p6.enter_context(nc.sbuf_tensor("ct_sb", [128, 16, D], BF16))
    co_f32 = p6.enter_context(nc.sbuf_tensor("co_f32", [128, 2, D], F32))
    scr4 = sq_scr[:].bitcast(F32)
    pct = [p6.enter_context(nc.psum_tensor(f"pct{i}", [128, D], F32)) for i in range(2)]
    pco = [p6.enter_context(nc.psum_tensor(f"pco{i}", [128, D], F32)) for i in range(2)]

    cte_hist = []
    cev_hist = []
    for b in range(B):
        W("gpsimd", "cc", 7)
        if b == 1:
            W("gpsimd", "c_pe", v.n["c_pe"])
        for g in range(16):
            for eg in range(4):
                e_glob = 4 * g + eg
                if e_glob < 32:
                    r_, l_ = e_glob // 4, e_glob % 4
                    row0 = 256 * r_ + 64 * l_ + 32 * b
                else:
                    r_, l_ = (e_glob - 32) // 4, (e_glob - 32) % 4
                    row0 = 2048 + 256 * r_ + 64 * l_ + 32 * b
                dma("gpsimd", "d_yb", y_b[32 * eg:32 * eg + 32, g, :],
                    y_ag[row0:row0 + 32, :])
        YB_TGT = v.n["d_yb"]
        for npt in range(NT):
            nc6 = 16 * b + npt
            if nc6 >= 2:
                W("tensor", "c_ac", cte_hist[nc6 - 2])

            def ctm(e, npt=npt, b=b):
                for c in range(2):
                    for dt in range(8):
                        mm = e.matmul(out=pct[npt % 2][:, 512 * c:512 * c + 512],
                                      lhsT=phi_sb[:, dt, 128 * npt:128 * npt + 128],
                                      rhs=xln3T[:, dt, MLOC * b + 512 * c:MLOC * b + 512 * c + 512],
                                      start=(dt == 0), stop=(dt == 7))
                        if c == 1 and dt == 7:
                            mm.then_inc(sems["c_pe"], 1)
            v.do("tensor", ctm)
            pe_c = v.inc("c_pe")
            W("scalar", "c_pe", pe_c)
            W("scalar", "c_ac", v.n["c_ac"])

            def cte(e, npt=npt):
                e.activation(out=ct_sb[:, npt, :], in_=pct[npt % 2][:, :], func=AF.Exp,
                             bias=negc[:, 0:1], scale=1.0).then_inc(sems["c_ac"], 1)
            v.do("scalar", cte)
            cte_hist.append(v.inc("c_ac"))
        AC_CT = v.n["c_ac"]
        W("tensor", "c_ac", AC_CT)
        W("tensor", "d_yb", YB_TGT)
        for ti8 in range(8):
            t_glob = 8 * b + ti8
            n6 = 8 * b + ti8
            if n6 >= 2:
                W("tensor", "c_dv", cev_hist[n6 - 2])

            def com(e, ti8=ti8):
                for c in range(2):
                    for npt in range(NT):
                        mm = e.matmul(out=pco[ti8 % 2][:, 512 * c:512 * c + 512],
                                      lhsT=ct_sb[:, npt, 128 * ti8:128 * ti8 + 128],
                                      rhs=y_b[:, npt, 512 * c:512 * c + 512],
                                      start=(npt == 0), stop=(npt == 15))
                        if c == 1 and npt == 15:
                            mm.then_inc(sems["c_pe"], 1)
            v.do("tensor", com)
            pe_co = v.inc("c_pe")
            W("vector", "c_pe", pe_co)
            W("vector", "c_ac", v.n["c_ac"])

            def cev(e, ti8=ti8):
                e.tensor_copy(out=co_f32[:, ti8 % 2, :],
                              in_=pco[ti8 % 2][:, :]).then_inc(sems["c_dv"], 1)
            v.do("vector", cev)
            cd_co = v.inc("c_dv")
            cev_hist.append(cd_co)
            W("vector", "c_dv", cd_co)
            v.do("vector", lambda e, ti8=ti8: e.tensor_reduce(
                out=stat[:, ti8:ti8 + 1], in_=co_f32[:, ti8 % 2, :], axis=AX, op=ALU.add))
            W("scalar", "c_dv", cd_co)
            W("scalar", "c_ac", v.n["c_ac"])

            W("scalar", "c_dv", v.n["c_dv"])

            def csq(e, ti8=ti8):
                e.activation(out=sq_scr[:, 0:D], in_=co_f32[:, ti8 % 2, :], func=AF.Square,
                             bias=zero_b[:, 0:1], scale=1.0,
                             accum_out=stat2[:, ti8:ti8 + 1]).then_inc(sems["c_ac"], 1)
            v.do("scalar", csq)
            ac_c = v.inc("c_ac")
            if ti8 % 2 == 1:
                W("vector", "c_ac", ac_c)
                ln_rstd(2, c0=ti8 - 1)

                def capply(e, ti8=ti8, b=b):
                    for k2 in range(2):
                        tt_ = ti8 - 1 + k2
                        tg = 8 * b + tt_
                        e.drain()
                        e.tensor_scalar(out=scr4, in0=co_f32[:, tt_ % 2, :],
                                        scalar1=mean_t[:, tt_:tt_ + 1],
                                        scalar2=rstd_t[:, tt_:tt_ + 1],
                                        op0=ALU.subtract, op1=ALU.mult)
                        e.drain()
                        e.tensor_tensor(out=Rt[:, tg, :], in0=Rt[:, tg, :],
                                        in1=scr4, op=ALU.add).then_inc(sems["c_dv"], 1)
                    e.drain()
                v.do("vector", capply)
                v.inc("c_dv", 2)
                cd_fin = v.n["c_dv"]
                W("sync", "c_dv", cd_fin)
                tg0 = 8 * b + ti8 - 1
                dma("sync", "d_out", yout[128 * tg0:128 * tg0 + 128, :], Rt[:, tg0, :])
                dma("sync", "d_out", yout[128 * tg0 + 128:128 * tg0 + 256, :], Rt[:, tg0 + 1, :])

    W("sync", "d_out", v.n["d_out"])

    barrier_all(6)
    v.flush(nc)
    p6.close()
    p36.close()
    es.close()
    return nc, dbg


def host_prep(L0_attn_x, L0_attn_qkv_w, L0_attn_qkv_b, L0_attn_proj_w,
              L0_attn_proj_b, L0_ffn_phi, L0_ffn_W, L0_ffn_b):
    x = np.asarray(L0_attn_x, np.float32)
    qkv_w = np.asarray(L0_attn_qkv_w, np.float32)
    qkv_b = np.asarray(L0_attn_qkv_b, np.float32)
    proj_w = np.asarray(L0_attn_proj_w, np.float32)
    proj_b = np.asarray(L0_attn_proj_b, np.float32)
    phi = np.asarray(L0_ffn_phi, np.float32)
    Wx = np.asarray(L0_ffn_W, np.float32)
    bx = np.asarray(L0_ffn_b, np.float32)

    qkv_wT = np.ascontiguousarray(qkv_w.T)
    proj_wT = np.ascontiguousarray(proj_w.T)
    phi2 = np.ascontiguousarray(phi.reshape(D, NP))
    ident = np.eye(128, dtype=ml_dtypes.bfloat16)
    in_maps = []
    for c in range(NCORE):
        xsl = np.ascontiguousarray(x[:, MLOC * c:MLOC * (c + 1), :].reshape(TLOC, D))
        in_maps.append({
            "xs": xsl,
            "qkv_wT": qkv_wT,
            "proj_wT": proj_wT,
            "phi2": phi2,
            "wsl": np.ascontiguousarray(np.concatenate(
                [Wx[4 * c:4 * c + 4], Wx[32 + 4 * c:32 + 4 * c + 4]], axis=0)),
            "bsl": np.ascontiguousarray(np.concatenate(
                [bx[4 * c:4 * c + 4], bx[32 + 4 * c:32 + 4 * c + 4]], axis=0).reshape(1, EPC * D)),
            "qkvb": qkv_b.reshape(1, 3 * D),
            "projb": proj_b.reshape(1, D),
            "ident": ident,
        })
    return in_maps


def kernel(**inputs):
    in_maps = host_prep(**inputs)
    nc, _ = build_nc(debug=False)
    res = run_bass_kernel_spmd(nc, in_maps, core_ids=list(range(NCORE)))
    out = np.empty((B, NCORE * MLOC, D), np.float32)
    for c in range(NCORE):
        o = res.results[c]["yout"].reshape(B, MLOC, D)
        out[:, MLOC * c:MLOC * (c + 1), :] = o
    return out
